# version 03
# baseline (speedup 1.0000x reference)
import numpy as np
from contextlib import ExitStack

import concourse.bass as bass
import concourse.mybir as mybir
from concourse.bass_utils import run_bass_kernel_spmd

F32 = mybir.dt.float32
BF16 = mybir.dt.bfloat16
AF = mybir.ActivationFunctionType
ALU = mybir.AluOpType
AX = mybir.AxisListType.X

T = 2048
D = 1024
NTT = 16
NDC = 8

ENGINES = ("sync", "tensor", "vector", "scalar", "gpsimd")
OWN_SEM = {"tensor": "T", "vector": "V", "scalar": "SC", "gpsimd": "P"}


class Sched:
    def __init__(self):
        self.ops = []
        from collections import defaultdict
        self.cnt = defaultdict(int)

    def op(self, eng, fn, wait=(), sig=None, n=1, note=""):
        tok = None
        w = tuple((sm, v) for (sm, v) in wait if v > 0)
        if sig is not None:
            self.cnt[sig] += n
            tok = (sig, self.cnt[sig])
        self.ops.append((eng, w, fn, (sig, n) if sig else None, note))
        return tok

    def tok(self, sem):
        return (sem, self.cnt[sem])

    def toks(self, *sems):
        return [self.tok(sm) for sm in sems]

    def fence(self):
        self.ops.append(("FENCE", dict(self.cnt), None, None, ""))


def replay(nc, s, es_p):
    names = []
    for item in s.ops:
        if item[0] == "FENCE":
            continue
        _, waits, _, sig, _ = item
        for sm, v in waits:
            if sm not in names:
                names.append(sm)
        if sig is not None and sig[0] not in names:
            names.append(sig[0])
    sems = {sm: es_p.enter_context(nc.semaphore(sm)) for sm in names}
    per_eng = {e: [] for e in ENGINES}
    pending_fence = {e: None for e in ENGINES}
    for item in s.ops:
        if item[0] == "FENCE":
            for e in ENGINES:
                pending_fence[e] = dict(item[1])
            continue
        eng = item[0]
        per_eng[eng].append((pending_fence[eng], item))
        pending_fence[eng] = None
    with nc.Block() as block:
        for ename in ENGINES:
            items = per_eng[ename]

            def body(e, ename=ename, items=items):
                waited = {}
                own = OWN_SEM.get(ename)
                for extra, (eng, waits, fn, sig, note) in items:
                    allw = []
                    if extra is not None:
                        allw.extend((sm, v) for sm, v in extra.items() if sm != own)
                    allw.extend(waits)
                    for sm, v in allw:
                        if v > 0 and waited.get(sm, 0) < v:
                            e.wait_ge(sems[sm], v)
                            waited[sm] = v
                    inst = fn(e)
                    if sig is not None:
                        inst.then_inc(sems[sig[0]], sig[1])
            getattr(block, ename)(body)


class NS:
    def __init__(self, **kw):
        self.__dict__.update(kw)


class NCProxy:
    def __init__(self, nc):
        self._nc = nc
        self._n = {}

    def _uniq(self, name):
        k = self._n.get(name, 0)
        self._n[name] = k + 1
        return name if k == 0 else f"{name}_u{k}"

    def sbuf_tensor(self, name, shape, dtype, **kw):
        return self._nc.sbuf_tensor(self._uniq(name), shape, dtype, **kw)

    def psum_tensor(self, name, shape, dtype, **kw):
        return self._nc.psum_tensor(self._uniq(name), shape, dtype, **kw)

    def dram_tensor(self, name, shape, dtype, **kw):
        return self._nc.dram_tensor(self._uniq(name), shape, dtype, **kw)

    def __getattr__(self, a):
        return getattr(self._nc, a)


def x_slice(P, tt, c0=0, n=D):
    return P.x_sb[:, D * tt + c0: D * tt + c0 + n]


def xnT_slice(P, c, t0, n):
    return P.xnT[:, T * c + t0: T * c + t0 + n]


def emit_setup(s, P, xin_ap):
    P.xload = s.op("sync", lambda e: e.dma_start(
        out=P.x_sb.ap().rearrange("p (tt d) -> p tt d", tt=NTT),
        in_=xin_ap.rearrange("(tt p) d -> p tt d", p=128)), sig="d_x", n=16)
    t_ms = s.op("gpsimd", lambda e: e.memset(P.ident[:], 1.0), sig="P")
    s.op("gpsimd", lambda e: e.affine_select(
        out=P.ident[:], in_=P.ident[:], pattern=[[-1, 128]],
        compare_op=ALU.is_equal, fill=0.0, base=0, channel_multiplier=1),
        wait=[t_ms])
    s.op("gpsimd", lambda e: e.memset(P.ones_row[:], 1.0))
    P.setup_tok = s.op("gpsimd", lambda e: e.memset(P.eps_t[:], 1e-5), sig="P")


def emit_prologue(s, P, A):
    xload = P.xload
    setup = P.setup_tok
    st = P.stat
    xnT3 = P.xnT.ap().rearrange("p (c t) -> p c t", c=NDC)
    ev_tok = {}
    last_bank_ev = [None, None]
    stats = {}
    for tt in range(NTT):
        xt = x_slice(P, tt)
        c = 8 * tt
        slot = tt % 2
        t_sum = s.op("vector", lambda e, xt=xt, c=c: e.reduce_sum(
            st[:, c:c + 1], xt, axis=AX), wait=[xload], sig="V")
        t_ssq = s.op("scalar", lambda e, xt=xt, c=c: e.activation(
            out=A.scr[:, 0:D], in_=xt, func=AF.Square, bias=0.0, scale=1.0,
            accum_out=st[:, c + 1:c + 2]), wait=[xload], sig="SC")
        t_mean = s.op("vector", lambda e, c=c: e.tensor_scalar_mul(
            st[:, c + 2:c + 3], st[:, c:c + 1], 1.0 / D), wait=[t_sum], sig="V")
        t_msq = s.op("vector", lambda e, c=c: e.tensor_mul(
            st[:, c + 6:c + 7], st[:, c + 2:c + 3], st[:, c + 2:c + 3]),
            wait=[t_mean], sig="V")
        t_var = s.op("vector", lambda e, c=c: e.scalar_tensor_tensor(
            st[:, c + 3:c + 4], st[:, c + 1:c + 2], 1.0 / D, st[:, c + 6:c + 7],
            ALU.mult, ALU.subtract), wait=[t_msq, t_ssq], sig="V")
        t_ln = s.op("scalar", lambda e, c=c: e.activation(
            out=st[:, c + 4:c + 5], in_=st[:, c + 3:c + 4], func=AF.Ln,
            bias=P.eps_t[:, 0:1], scale=1.0), wait=[t_var, setup], sig="SC")
        t_rstd = s.op("scalar", lambda e, c=c: e.activation(
            out=st[:, c + 5:c + 6], in_=st[:, c + 4:c + 5], func=AF.Exp,
            bias=0.0, scale=-0.5), wait=[t_ln], sig="SC")
        wapp = [t_rstd]
        if tt >= 2:
            wapp.append(ev_tok[tt - 2][0])
        t_xn = s.op("vector", lambda e, xt=xt, slot=slot, c=c: e.tensor_scalar(
            A.xn[:, D * slot:D * slot + D], xt, st[:, c + 2:c + 3],
            st[:, c + 5:c + 6], ALU.subtract, ALU.mult), wait=wapp, sig="V")
        bank = A.psumb[slot]
        tok_T = None
        for cc in range(NDC):
            w = [t_xn, setup]
            if last_bank_ev[slot] is not None:
                w.append(last_bank_ev[slot])
            tok_T = s.op("tensor", lambda e, slot=slot, cc=cc, bank=bank: e.transpose(
                bank[:, 128 * cc:128 * cc + 128],
                A.xn[:, D * slot + 128 * cc: D * slot + 128 * cc + 128],
                P.ident[:]), wait=w, sig="T")
        t_ev = s.op("vector", lambda e, tt=tt, bank=bank: e.tensor_copy(
            xnT3[:, :, 128 * tt:128 * tt + 128],
            bank.ap().rearrange("p (c t) -> p c t", c=NDC)), wait=[tok_T], sig="V")
        last_bank_ev[slot] = t_ev
        ev_tok[tt] = (tok_T, t_ev)
        stats[tt] = (t_xn, t_ssq)
    return NS(done_V=s.tok("V"), done_SC=s.tok("SC"), done_T=s.tok("T"),
              stats_done=stats)


def emit_moe_gating(s, P, A, prol, gw_name, topk):
    t_load = s.op("gpsimd", lambda e: e.dma_start(
        out=A.gws[:], in_=P.W[gw_name]), sig="d_gw", n=16)
    st = A.gst
    banktok = [None, None]
    for tt in range(NTT):
        par = tt % 2
        bank = A.psumf[par]
        w = [t_load, prol.done_V]
        if banktok[par] is not None:
            w.append(banktok[par])
        tok = None
        for c in range(NDC):
            tok = s.op("tensor", lambda e, c=c, tt=tt, bank=bank: e.matmul(
                bank[:, 0:8], xnT_slice(P, c, 128 * tt, 128), A.gws[:, 8 * c:8 * c + 8],
                start=(c == 0), stop=(c == NDC - 1)), wait=w, sig="T")
            w = []
        c0 = 8 * tt
        t_lg = s.op("vector", lambda e, tt=tt, bank=bank: e.tensor_copy(
            A.glog[:, 8 * tt:8 * tt + 8], bank[:, 0:8]), wait=[tok], sig="V")
        banktok[par] = t_lg
        t_max = s.op("vector", lambda e, tt=tt: e.max(
            A.gtop[:, 8 * tt:8 * tt + 8], A.glog[:, 8 * tt:8 * tt + 8]),
            wait=[t_lg], sig="V")
        t_neg = s.op("vector", lambda e, tt=tt, c0=c0: e.tensor_scalar_mul(
            st[:, c0 + 2:c0 + 3], A.gtop[:, 8 * tt:8 * tt + 1], -1.0),
            wait=[t_max], sig="V")
        t_exp = s.op("scalar", lambda e, tt=tt, c0=c0: e.activation(
            out=A.gexp[:, 8 * tt:8 * tt + 8], in_=A.glog[:, 8 * tt:8 * tt + 8],
            func=AF.Exp, bias=st[:, c0 + 2:c0 + 3], scale=1.0,
            accum_out=st[:, c0 + 3:c0 + 4]), wait=[t_neg], sig="SC")
        if topk == 1:
            t_r = s.op("vector", lambda e, c0=c0: e.reciprocal(
                st[:, c0 + 4:c0 + 5], st[:, c0 + 3:c0 + 4]), wait=[t_exp], sig="V")
            t_oh = s.op("vector", lambda e, tt=tt: e.tensor_scalar(
                A.gw[:, 8 * tt:8 * tt + 8], A.glog[:, 8 * tt:8 * tt + 8],
                A.gtop[:, 8 * tt:8 * tt + 1], None, ALU.is_equal),
                wait=[t_max], sig="V")
            s.op("vector", lambda e, tt=tt, c0=c0: e.tensor_scalar_mul(
                A.gw[:, 8 * tt:8 * tt + 8], A.gw[:, 8 * tt:8 * tt + 8],
                st[:, c0 + 4:c0 + 5]), wait=[t_r, t_oh], sig="V")
        else:
            t_r = s.op("vector", lambda e, c0=c0: e.reciprocal(
                st[:, c0 + 4:c0 + 5], st[:, c0 + 3:c0 + 4]), wait=[t_exp], sig="V")
            t_p = s.op("vector", lambda e, tt=tt, c0=c0: e.tensor_scalar_mul(
                A.gexp[:, 8 * tt:8 * tt + 8], A.gexp[:, 8 * tt:8 * tt + 8],
                st[:, c0 + 4:c0 + 5]), wait=[t_r], sig="V")
            t_m8 = s.op("vector", lambda e, tt=tt: e.max(
                A.gtop[:, 8 * tt:8 * tt + 8], A.gexp[:, 8 * tt:8 * tt + 8]),
                wait=[t_p], sig="V")
            t_d0 = s.op("vector", lambda e, tt=tt, c0=c0: e.tensor_add(
                st[:, c0 + 7:c0 + 8], A.gtop[:, 8 * tt:8 * tt + 1],
                A.gtop[:, 8 * tt + 1:8 * tt + 2]), wait=[t_m8], sig="V")
            t_d1 = s.op("vector", lambda e, c0=c0: e.tensor_scalar_add(
                st[:, c0 + 7:c0 + 8], st[:, c0 + 7:c0 + 8], 1e-8),
                wait=[t_d0], sig="V")
            t_rr = s.op("vector", lambda e, c0=c0: e.reciprocal(
                st[:, c0 + 4:c0 + 5], st[:, c0 + 7:c0 + 8]), wait=[t_d1], sig="V")
            t_a1 = s.op("vector", lambda e, tt=tt, c0=c0: e.tensor_mul(
                st[:, c0 + 5:c0 + 6], A.gtop[:, 8 * tt:8 * tt + 1],
                st[:, c0 + 4:c0 + 5]), wait=[t_rr], sig="V")
            t_a2 = s.op("vector", lambda e, tt=tt, c0=c0: e.tensor_mul(
                st[:, c0 + 6:c0 + 7], A.gtop[:, 8 * tt + 1:8 * tt + 2],
                st[:, c0 + 4:c0 + 5]), wait=[t_rr], sig="V")
            t_o1 = s.op("vector", lambda e, tt=tt, c0=c0: e.tensor_scalar(
                A.gw[:, 8 * tt:8 * tt + 8], A.gexp[:, 8 * tt:8 * tt + 8],
                A.gtop[:, 8 * tt:8 * tt + 1], st[:, c0 + 5:c0 + 6],
                ALU.is_equal, ALU.mult), wait=[t_a1], sig="V")
            t_o2 = s.op("vector", lambda e, tt=tt, c0=c0: e.tensor_scalar(
                A.goh2[:, 0:8], A.gexp[:, 8 * tt:8 * tt + 8],
                A.gtop[:, 8 * tt + 1:8 * tt + 2], st[:, c0 + 6:c0 + 7],
                ALU.is_equal, ALU.mult), wait=[t_a2, t_o1], sig="V")
            s.op("vector", lambda e, tt=tt: e.tensor_add(
                A.gw[:, 8 * tt:8 * tt + 8], A.gw[:, 8 * tt:8 * tt + 8],
                A.goh2[:, 0:8]), wait=[t_o2], sig="V")
    return s.tok("V"), banktok


def emit_ffn0(s, P, A, prol, gate_done, gbank, gsafe):
    t_b1 = s.op("sync", lambda e: e.dma_start(out=A.b1[:], in_=P.W["f0_b1"]),
                wait=gsafe, sig="d_f0b1", n=16)
    NFC, NQ = 32, 4
    NTB, TB = 4, 512
    NKS = 16
    w1v = P.W["f0_w1"]
    w2v = P.W["f0_w2"]
    w1sb = A.w1sb.ap().rearrange("p (c f) -> p c f", c=NDC)
    bank_free = {i: None for i in range(8)}
    bank_free[0] = gbank[0]
    bank_free[1] = gbank[1]
    last_phaseA_T = {q: None for q in range(NQ)}
    w2_last_T = [None, None]
    b2_last_T = None
    hT_last_T = None
    for e_i in range(_DEBUG_NEXPERTS):
        w1tok = []
        for q in range(NQ):
            w = list(gsafe) if e_i == 0 else []
            if last_phaseA_T[q] is not None:
                w.append(last_phaseA_T[q])
            w1tok.append(s.op("gpsimd", lambda e, e_i=e_i, q=q: e.dma_start(
                out=w1sb[:, :, 1024 * q:1024 * (q + 1)],
                in_=w1v[e_i].rearrange("(c p) f -> p c f", p=128)[:, :, 1024 * q:1024 * (q + 1)]),
                wait=w, sig=f"d_f0w1_{q}", n=16))
        wb2 = [b2_last_T] if b2_last_T is not None else list(gsafe)
        t_b2 = s.op("gpsimd", lambda e, e_i=e_i: e.dma_start(
            out=A.b2row[:], in_=P.W["f0_b2"][:, D * e_i:D * (e_i + 1)]),
            wait=wb2, sig="d_f0b2", n=16)
        for tb in range(min(NTB, _DEBUG_NTB)):
            for fc in range(NFC):
                bank_i = fc % 8
                bank = A.psumf[bank_i]
                w = [w1tok[fc // 8], prol.done_V]
                if bank_free[bank_i] is not None:
                    w.append(bank_free[bank_i])
                tok = None
                for c in range(NDC):
                    tok = s.op("tensor", lambda e, fc=fc, c=c, tb=tb, bank=bank: e.matmul(
                        bank[:], w1sb[:, c, 128 * fc:128 * fc + 128],
                        xnT_slice(P, c, TB * tb, TB),
                        start=(c == 0), stop=(c == NDC - 1)), wait=w, sig="T")
                    w = []
                wh = [tok, t_b1]
                if hT_last_T is not None:
                    wh.append(hT_last_T)
                if _DEBUG_NOGELU:
                    t_ev = s.op("scalar", lambda e, fc=fc, bank=bank: e.activation(
                        out=A.hT[:, 512 * fc:512 * fc + 512], in_=bank[:],
                        func=AF.Copy, bias=0.0, scale=1.0), wait=wh, sig="SC")
                else:
                    t_ev = s.op("scalar", lambda e, e_i=e_i, fc=fc, bank=bank: e.activation(
                        out=A.hT[:, 512 * fc:512 * fc + 512], in_=bank[:],
                        func=AF.Gelu, bias=A.b1[:, 32 * e_i + fc:32 * e_i + fc + 1],
                        scale=1.0), wait=wh, sig="SC")
                bank_free[bank_i] = t_ev
            phaseA_done = s.tok("SC")
            first_mm_waits = [phaseA_done, gate_done]
            for i in range(8):
                if bank_free[i] is not None:
                    first_mm_waits.append(bank_free[i])
            mm_tok = None
            for ks in range(NKS):
                par = ks % 2
                w = [] if w2_last_T[par] is not None or e_i > 0 else list(gsafe)
                if w2_last_T[par] is not None:
                    w.append(w2_last_T[par])
                w2tok = s.op("gpsimd", lambda e, e_i=e_i, ks=ks, par=par: e.dma_start(
                    out=A.w2sb[par].ap().rearrange("p (kc d) -> p kc d", kc=2),
                    in_=w2v[e_i].rearrange("(ks kc p) d -> ks p kc d", kc=2, p=128)[ks]),
                    wait=w, sig=f"d_f0w2_{par}", n=16)
                for kc in range(2):
                    fc = 2 * ks + kc
                    for tt in range(4):
                        for db in range(2):
                            bi = tt * 2 + db
                            w = [w2tok] + (first_mm_waits if fc == 0 else [])
                            mm_tok = s.op("tensor", lambda e, fc=fc, kc=kc, tt=tt, db=db, par=par, bi=bi: e.matmul(
                                A.psumf[bi][:],
                                A.hT[:, 512 * fc + 128 * tt: 512 * fc + 128 * tt + 128],
                                A.w2sb[par][:, 1024 * kc + 512 * db: 1024 * kc + 512 * db + 512],
                                start=(fc == 0), stop=False), wait=w, sig="T")
                w2_last_T[par] = mm_tok
            hT_last_T = mm_tok
            for tt in range(4):
                for db in range(2):
                    bi = tt * 2 + db
                    t_bias = s.op("tensor", lambda e, db=db, bi=bi: e.matmul(
                        A.psumf[bi][:], P.ones_row[0:1, 0:128],
                        A.b2row[0:1, 512 * db: 512 * db + 512],
                        start=False, stop=True),
                        wait=[P.setup_tok, t_b2], sig="T")
                    gtt = 4 * tb + tt
                    pv, psc = prol.stats_done[gtt]
                    t_ev = s.op("vector", lambda e, e_i=e_i, gtt=gtt, db=db, bi=bi: e.scalar_tensor_tensor(
                        x_slice(P, gtt, 512 * db, 512), A.psumf[bi][:],
                        A.gw[:, 8 * gtt + e_i: 8 * gtt + e_i + 1],
                        x_slice(P, gtt, 512 * db, 512),
                        ALU.mult, ALU.add), wait=[t_bias, pv, psc, gate_done], sig="V")
                    bank_free[bi] = t_ev
            b2_last_T = s.tok("T")
        for q in range(NQ):
            last_phaseA_T[q] = s.tok("T")


def emit_ffn1(s, P, A, prol, gate_done, gbank, gsafe):
    t_bg = s.op("sync", lambda e: e.dma_start(out=A.bg[:], in_=P.W["f1_bg"]),
                wait=gsafe, sig="d_f1bg", n=16)
    t_bu = s.op("sync", lambda e: e.dma_start(out=A.bu[:], in_=P.W["f1_bu"]),
                wait=gsafe, sig="d_f1bu", n=16)
    NMC = 16
    NTB, TB = 4, 512
    NKS = 8
    wgv, wuv, wov = P.W["f1_wg"], P.W["f1_wu"], P.W["f1_wo"]
    wgsb = A.wgsb.ap().rearrange("p (c f) -> p c f", c=NDC)
    wusb = A.wusb.ap().rearrange("p (c f) -> p c f", c=NDC)
    bank_free = {i: None for i in range(8)}
    bank_free[0] = gbank[0]
    bank_free[1] = gbank[1]
    last_phaseA_T = {q: None for q in range(2)}
    wo_last_T = [None, None]
    bo_last_T = None
    hT_last_T = None
    gscr_tok = [None, None]
    for e_i in range(8):
        wgtok, wutok = [], []
        for q in range(2):
            w = list(gsafe) if e_i == 0 else []
            if last_phaseA_T[q] is not None:
                w.append(last_phaseA_T[q])
            wgtok.append(s.op("gpsimd", lambda e, e_i=e_i, q=q: e.dma_start(
                out=wgsb[:, :, 1024 * q:1024 * (q + 1)],
                in_=wgv[e_i].rearrange("(c p) f -> p c f", p=128)[:, :, 1024 * q:1024 * (q + 1)]),
                wait=w, sig=f"d_f1wg_{q}", n=16))
            wutok.append(s.op("gpsimd", lambda e, e_i=e_i, q=q: e.dma_start(
                out=wusb[:, :, 1024 * q:1024 * (q + 1)],
                in_=wuv[e_i].rearrange("(c p) f -> p c f", p=128)[:, :, 1024 * q:1024 * (q + 1)]),
                wait=w, sig=f"d_f1wu_{q}", n=16))
        wbo = [bo_last_T] if bo_last_T is not None else list(gsafe)
        t_bo = s.op("gpsimd", lambda e, e_i=e_i: e.dma_start(
            out=A.borow[:], in_=P.W["f1_bo"][:, D * e_i:D * (e_i + 1)]),
            wait=wbo, sig="d_f1bo", n=16)
        for tb in range(NTB):
            for mc in range(NMC):
                bg_i = 2 * (mc % 4)
                bu_i = bg_i + 1
                bank_g, bank_u = A.psumf[bg_i], A.psumf[bu_i]
                wq = mc // 8
                wg_ = [wgtok[wq], prol.done_V]
                if bank_free[bg_i] is not None:
                    wg_.append(bank_free[bg_i])
                tokg = None
                for c in range(NDC):
                    tokg = s.op("tensor", lambda e, mc=mc, c=c, tb=tb, bank=bank_g: e.matmul(
                        bank[:], wgsb[:, c, 128 * mc:128 * mc + 128],
                        xnT_slice(P, c, TB * tb, TB),
                        start=(c == 0), stop=(c == NDC - 1)), wait=wg_, sig="T")
                    wg_ = []
                wu_ = [wutok[wq]]
                if bank_free[bu_i] is not None:
                    wu_.append(bank_free[bu_i])
                toku = None
                for c in range(NDC):
                    toku = s.op("tensor", lambda e, mc=mc, c=c, tb=tb, bank=bank_u: e.matmul(
                        bank[:], wusb[:, c, 128 * mc:128 * mc + 128],
                        xnT_slice(P, c, TB * tb, TB),
                        start=(c == 0), stop=(c == NDC - 1)), wait=wu_, sig="T")
                    wu_ = []
                slot = mc % 2
                wsg = [tokg, t_bg]
                if gscr_tok[slot] is not None:
                    wsg.append(gscr_tok[slot])
                t_g = s.op("scalar", lambda e, e_i=e_i, mc=mc, slot=slot, bank=bank_g: e.activation(
                    out=A.gscr[:, 512 * slot:512 * slot + 512], in_=bank[:],
                    func=AF.Silu, bias=A.bg[:, 16 * e_i + mc:16 * e_i + mc + 1],
                    scale=1.0), wait=wsg, sig="SC")
                bank_free[bg_i] = t_g
                wh = [toku, t_g, t_bu]
                if hT_last_T is not None:
                    wh.append(hT_last_T)
                t_h = s.op("vector", lambda e, e_i=e_i, mc=mc, slot=slot, bank=bank_u: e.scalar_tensor_tensor(
                    A.hT[:, 512 * mc:512 * mc + 512], bank[:],
                    A.bu[:, 16 * e_i + mc:16 * e_i + mc + 1],
                    A.gscr[:, 512 * slot:512 * slot + 512],
                    ALU.add, ALU.mult), wait=wh, sig="V")
                bank_free[bu_i] = t_h
                gscr_tok[slot] = t_h
            phaseA_done = s.tok("V")
            first_mm_waits = [phaseA_done, gate_done]
            for i in range(8):
                if bank_free[i] is not None:
                    first_mm_waits.append(bank_free[i])
            mm_tok = None
            for ks in range(NKS):
                par = ks % 2
                w = [] if wo_last_T[par] is not None or e_i > 0 else list(gsafe)
                if wo_last_T[par] is not None:
                    w.append(wo_last_T[par])
                wotok = s.op("gpsimd", lambda e, e_i=e_i, ks=ks, par=par: e.dma_start(
                    out=A.wosb[par].ap().rearrange("p (kc d) -> p kc d", kc=2),
                    in_=wov[e_i].rearrange("(ks kc p) d -> ks p kc d", kc=2, p=128)[ks]),
                    wait=w, sig=f"d_f1wo_{par}", n=16)
                for kc in range(2):
                    fc = 2 * ks + kc
                    for tt in range(4):
                        for db in range(2):
                            bi = tt * 2 + db
                            w = [wotok] + (first_mm_waits if fc == 0 else [])
                            mm_tok = s.op("tensor", lambda e, fc=fc, kc=kc, tt=tt, db=db, par=par, bi=bi: e.matmul(
                                A.psumf[bi][:],
                                A.hT[:, 512 * fc + 128 * tt: 512 * fc + 128 * tt + 128],
                                A.wosb[par][:, 1024 * kc + 512 * db: 1024 * kc + 512 * db + 512],
                                start=(fc == 0), stop=False), wait=w, sig="T")
                wo_last_T[par] = mm_tok
            hT_last_T = mm_tok
            for tt in range(4):
                for db in range(2):
                    bi = tt * 2 + db
                    t_bias = s.op("tensor", lambda e, db=db, bi=bi: e.matmul(
                        A.psumf[bi][:], P.ones_row[0:1, 0:128],
                        A.borow[0:1, 512 * db: 512 * db + 512],
                        start=False, stop=True),
                        wait=[P.setup_tok, t_bo], sig="T")
                    gtt = 4 * tb + tt
                    pv, psc = prol.stats_done[gtt]
                    t_ev = s.op("vector", lambda e, e_i=e_i, gtt=gtt, db=db, bi=bi: e.scalar_tensor_tensor(
                        x_slice(P, gtt, 512 * db, 512), A.psumf[bi][:],
                        A.gw[:, 8 * gtt + e_i: 8 * gtt + e_i + 1],
                        x_slice(P, gtt, 512 * db, 512),
                        ALU.mult, ALU.add), wait=[t_bias, pv, psc, gate_done], sig="V")
                    bank_free[bi] = t_ev
            bo_last_T = s.tok("T")
        for q in range(2):
            last_phaseA_T[q] = s.tok("T")


ALIBI_SLOPES = [2.0 ** (-0.5 * (h + 1)) for h in range(16)]


def _attn_head(s, P, A, C, oc, r0, q_done, v_done, t_tab, causal, stt_scalar):
    sb, avb, tbk, bank_free = C.sb, C.avb, C.tbk, C.bank_free
    for qt in range(NTT):
        nkb = (qt // 4 + 1) if causal else 4
        W_ = nkb * 512
        nb = (qt + 1) if causal else NTT
        sl = qt % 4
        mm_toks = []
        for kb in range(nkb):
            w = [q_done, v_done]
            if bank_free.get(("s", kb)) is not None:
                w.append(bank_free[("s", kb)])
            mm_toks.append(s.op("tensor", lambda e, r0=r0, qt=qt, kb=kb: e.matmul(
                sb[kb][:], A.qT[r0:r0 + 64, 128 * qt:128 * qt + 128],
                A.kT[r0:r0 + 64, 512 * kb:512 * kb + 512],
                start=True, stop=True), wait=w, sig="T"))
        if stt_scalar is None:
            c0 = 16 * sl
            wf = []
            if C.rs_free[sl] is not None:
                wf.append(C.rs_free[sl])
            bm = []
            for kb in range(nkb):
                bm.append(s.op("vector", lambda e, kb=kb, c0=c0: e.reduce_max(
                    A.rs[:, c0 + 4 + kb:c0 + 5 + kb], sb[kb][:], axis=AX),
                    wait=[mm_toks[kb]] + wf, sig="V"))
            t_max = s.op("vector", lambda e, c0=c0, nkb=nkb: e.reduce_max(
                A.rs[:, c0:c0 + 1], A.rs[:, c0 + 4:c0 + 4 + nkb], axis=AX),
                wait=[bm[-1]], sig="V")
            t_neg = s.op("vector", lambda e, c0=c0: e.tensor_scalar_mul(
                A.rs[:, c0 + 1:c0 + 2], A.rs[:, c0:c0 + 1], -1.0),
                wait=[t_max], sig="V")
            exps = []
            for kb in range(nkb):
                wexp = [t_neg]
                if C.psb_war is not None and kb == 0:
                    wexp.append(C.psb_war)
                exps.append(s.op("scalar", lambda e, kb=kb, c0=c0: e.activation(
                    out=A.Psb[:, 512 * kb:512 * kb + 512], in_=sb[kb][:],
                    func=AF.Exp, bias=A.rs[:, c0 + 1:c0 + 2], scale=1.0,
                    accum_out=A.rs[:, c0 + 8 + kb:c0 + 9 + kb]), wait=wexp, sig="SC"))
                bank_free[("s", kb)] = exps[-1]
            C.prev_exp = exps[-1]
            t_sum = s.op("vector", lambda e, c0=c0, nkb=nkb: e.reduce_sum(
                A.rs[:, c0 + 2:c0 + 3], A.rs[:, c0 + 8:c0 + 8 + nkb], axis=AX),
                wait=[exps[-1]], sig="V")
            t_rc = s.op("vector", lambda e, c0=c0: e.reciprocal(
                A.rs[:, c0 + 3:c0 + 4], A.rs[:, c0 + 2:c0 + 3]),
                wait=[t_sum], sig="V")
            t_exp = exps[-1]
            rc_col = A.rs[:, c0 + 3:c0 + 4]
        else:
            stt_toks = []
            for kb in range(nkb):
                i0 = 1920 - 128 * qt + 512 * kb
                w = [mm_toks[kb]]
                if t_tab is not None:
                    w.append(t_tab)
                if C.prev_exp is not None:
                    w.append(C.prev_exp)
                stt_toks.append(s.op("vector", lambda e, kb=kb, i0=i0: e.scalar_tensor_tensor(
                    A.Ssb[:, 512 * kb:512 * kb + 512],
                    A.dtab[:, i0:i0 + 512], stt_scalar, sb[kb][:],
                    ALU.mult, ALU.add), wait=w, sig="V"))
                bank_free[("s", kb)] = stt_toks[-1]
            w = [stt_toks[-1]]
            if C.rs_free[sl] is not None:
                w.append(C.rs_free[sl])
            t_max = s.op("vector", lambda e, sl=sl, W_=W_: e.reduce_max(
                A.rs[:, 4 * sl:4 * sl + 1], A.Ssb[:, 0:W_], axis=AX), wait=w, sig="V")
            t_neg = s.op("vector", lambda e, sl=sl: e.tensor_scalar_mul(
                A.rs[:, 4 * sl + 1:4 * sl + 2], A.rs[:, 4 * sl:4 * sl + 1], -1.0),
                wait=[t_max], sig="V")
            wexp = [t_neg]
            if C.psb_war is not None:
                wexp.append(C.psb_war)
            t_exp = s.op("scalar", lambda e, sl=sl, W_=W_: e.activation(
                out=A.Psb[:, 0:W_], in_=A.Ssb[:, 0:W_], func=AF.Exp,
                bias=A.rs[:, 4 * sl + 1:4 * sl + 2], scale=1.0,
                accum_out=A.rs[:, 4 * sl + 2:4 * sl + 3]), wait=wexp, sig="SC")
            C.prev_exp = t_exp
            t_rc = s.op("vector", lambda e, sl=sl: e.reciprocal(
                A.rs[:, 4 * sl + 3:4 * sl + 4], A.rs[:, 4 * sl + 2:4 * sl + 3]),
                wait=[t_exp], sig="V")
            rc_col = A.rs[:, 4 * sl + 3:4 * sl + 4]
        pev = {}
        for g in range((nb + 7) // 8):
            cnt = min(8, nb - 8 * g)
            gb = g % 2
            w = [t_exp, P.setup_tok]
            if bank_free.get(("tb", gb)) is not None:
                w.append(bank_free[("tb", gb)])
            tok = None
            for j in range(cnt):
                jj = 8 * g + j
                tok = s.op("tensor", lambda e, gb=gb, j=j, jj=jj: e.transpose(
                    tbk[gb][:, 128 * j:128 * j + 128],
                    A.Psb[:, 128 * jj:128 * jj + 128], P.ident[:]), wait=w, sig="T")
                w = []
            t_pev = s.op("vector", lambda e, g=g, gb=gb, cnt=cnt: e.tensor_copy(
                A.PT[:, 1024 * g:1024 * g + 128 * cnt],
                tbk[gb][:, 0:128 * cnt]), wait=[tok], sig="V")
            bank_free[("tb", gb)] = t_pev
            pev[g] = t_pev
        C.psb_war = s.tok("T")
        t_av = None
        for j in range(nb):
            w = [pev[j // 8]]
            if j == 0 and bank_free.get(("av",)) is not None:
                w.append(bank_free[("av",)])
            t_av = s.op("tensor", lambda e, j=j, r0=r0, nb=nb: e.matmul(
                avb[:, 0:64], A.PT[:, 128 * j:128 * j + 128],
                A.vc[:, 128 * j + r0:128 * j + r0 + 64],
                start=(j == 0), stop=(j == nb - 1)), wait=w, sig="T")
        t_oev = s.op("vector", lambda e, rc_col=rc_col: e.tensor_scalar_mul(
            A.osb[:, 0:64], avb[:, 0:64], rc_col),
            wait=[t_av, t_rc], sig="V")
        bank_free[("av",)] = t_oev
        C.rs_free[sl] = t_oev
        gb = qt % 2
        w = [t_oev, P.setup_tok]
        if bank_free.get(("tb", gb)) is not None:
            w.append(bank_free[("tb", gb)])
        t_otr = s.op("tensor", lambda e, gb=gb: e.transpose(
            tbk[gb][0:64, 0:128], A.osb[:, 0:64], P.ident[:]), wait=w, sig="T")
        t_ofin = s.op("vector", lambda e, gb=gb, r0=r0, oc=oc, qt=qt: e.tensor_copy(
            A.oT[r0:r0 + 64, T * oc + 128 * qt: T * oc + 128 * qt + 128],
            tbk[gb][0:64, 0:128]), wait=[t_otr], sig="V")
        bank_free[("tb", gb)] = t_ofin


def _emit_out_proj(s, nc, P, A, prol, wosb, t_wo, attn_done,
                   bias_row=None, t_bias_load=None):
    with ExitStack() as esc2:
        yb = [esc2.enter_context(nc.psum_tensor(f"t_yb{i}", [128, 512], F32))
              for i in range(8)]
        ybfree = {}
        for tb in range(4):
            for fc in range(NDC):
                for tt in range(4):
                    for db in range(2):
                        bi = 2 * tt + db
                        gtt = 4 * tb + tt
                        w = [t_wo] + (attn_done if fc == 0 else [])
                        if fc == 0 and ybfree.get(bi) is not None:
                            w.append(ybfree[bi])
                        tok = s.op("tensor", lambda e, fc=fc, gtt=gtt, db=db, bi=bi: e.matmul(
                            yb[bi][:], A.oT[:, T * fc + 128 * gtt: T * fc + 128 * gtt + 128],
                            wosb[:, fc, 512 * db:512 * db + 512],
                            start=(fc == 0),
                            stop=(fc == NDC - 1 and bias_row is None)),
                            wait=w, sig="T")
                        if fc == NDC - 1:
                            if bias_row is not None:
                                tok = s.op("tensor", lambda e, db=db, bi=bi: e.matmul(
                                    yb[bi][:], P.ones_row[0:1, 0:128],
                                    bias_row[0:1, 512 * db: 512 * db + 512],
                                    start=False, stop=True),
                                    wait=[P.setup_tok, t_bias_load], sig="T")
                            pv, psc = prol.stats_done[gtt]
                            t_ev = s.op("vector", lambda e, gtt=gtt, db=db, bi=bi: e.tensor_add(
                                x_slice(P, gtt, 512 * db, 512), yb[bi][:],
                                x_slice(P, gtt, 512 * db, 512)),
                                wait=[tok, pv, psc], sig="V")
                            ybfree[bi] = t_ev


def emit_attn_a0(s, nc, es, P):
    A = NS()
    with ExitStack() as es1:
        A.scr = es1.enter_context(nc.sbuf_tensor("t_scr", [128, D], F32))
        A.xn = es1.enter_context(nc.sbuf_tensor("t_xn", [128, 2 * D], BF16))
        with ExitStack() as esp:
            A.psumb = [esp.enter_context(nc.psum_tensor(f"t_psb{i}", [128, 1024], BF16))
                       for i in range(2)]
            prol = emit_prologue(s, P, A)
    gsafe0 = s.toks("T", "V", "SC")
    A.dtab = es.enter_context(nc.sbuf_tensor("t_dtab", [128, 2432], F32))
    A.wq = [es.enter_context(nc.sbuf_tensor(f"t_wq{i}", [128, NDC * 128], BF16))
            for i in range(2)]
    A.wk = [es.enter_context(nc.sbuf_tensor(f"t_wk{i}", [128, NDC * 128], BF16))
            for i in range(2)]
    A.wv = [es.enter_context(nc.sbuf_tensor(f"t_wv{i}", [128, NDC * 128], BF16))
            for i in range(2)]
    A.wo = es.enter_context(nc.sbuf_tensor("t_wo", [128, NDC * 1024], BF16))
    A.qT = es.enter_context(nc.sbuf_tensor("t_qT", [128, T], BF16))
    A.kT = es.enter_context(nc.sbuf_tensor("t_kT", [128, T], BF16))
    A.vTs = es.enter_context(nc.sbuf_tensor("t_vTs", [128, T], BF16))
    A.vc = es.enter_context(nc.sbuf_tensor("t_vc", [128, NTT * 128], BF16))
    A.oT = es.enter_context(nc.sbuf_tensor("t_oT", [128, NDC * T], BF16))
    A.Ssb = es.enter_context(nc.sbuf_tensor("t_Ssb", [128, T], F32))
    A.Psb = es.enter_context(nc.sbuf_tensor("t_Psb", [128, T], BF16))
    A.PT = es.enter_context(nc.sbuf_tensor("t_PT", [128, NTT * 128], BF16))
    A.rs = es.enter_context(nc.sbuf_tensor("t_rs", [128, 16], F32))
    A.osb = es.enter_context(nc.sbuf_tensor("t_osb", [128, 64], BF16))
    t_io = s.op("gpsimd", lambda e: e.iota(
        A.dtab[:], pattern=[[-1, 2432]], base=1920, channel_multiplier=1,
        allow_small_or_imprecise_dtypes=True), wait=gsafe0, sig="P")
    t_dt = s.op("gpsimd", lambda e, _w=None: e.affine_select(
        out=A.dtab[:], in_=A.dtab[:], pattern=[[-1, 2432]],
        compare_op=ALU.is_ge, fill=1e33, base=1920, channel_multiplier=1),
        wait=[t_io], sig="P")
    t_wo = s.op("gpsimd", lambda e: e.dma_start(
        out=A.wo.ap().rearrange("p (c d) -> p c d", c=NDC),
        in_=P.W["a0_wo"].rearrange("(c p) d -> p c d", p=128)),
        wait=gsafe0, sig="d_a0wo", n=16)

    wosb = A.wo.ap().rearrange("p (c d) -> p c d", c=NDC)
    oT_tok = {}
    attn_T_done = {}
    proj_T_done = {}
    with ExitStack() as esc:
        C = NS(
            sb=[esc.enter_context(nc.psum_tensor(f"t_sp{i}", [128, 512], F32))
                for i in range(4)],
            avb=esc.enter_context(nc.psum_tensor("t_av", [128, 512], F32)),
            tbk=[esc.enter_context(nc.psum_tensor(f"t_tb{i}", [128, 1024], BF16))
                 for i in range(2)],
            bank_free={}, rs_free={sl: None for sl in range(4)},
            psb_war=None, prev_exp=None)
        sb, tbk, bank_free = C.sb, C.tbk, C.bank_free
        for oc in range(8):
            par = oc % 2
            wl = []
            for nm, buf in (("a0_wq", A.wq[par]), ("a0_wk", A.wk[par]), ("a0_wv", A.wv[par])):
                w = list(gsafe0) if oc < 2 else [proj_T_done[oc - 2]]
                wl.append(s.op("gpsimd", lambda e, nm=nm, buf=buf, oc=oc: e.dma_start(
                    out=buf.ap().rearrange("p (c o) -> p c o", c=NDC),
                    in_=P.W[nm][:, 128 * oc:128 * oc + 128].rearrange(
                        "(c p) o -> p c o", p=128)), wait=w,
                    sig=f"d_{nm[3:]}_{par}", n=16))
            war = [attn_T_done[oc - 1]] if oc >= 1 else []
            wqs = A.wq[par].ap().rearrange("p (c o) -> p c o", c=NDC)
            wks = A.wk[par].ap().rearrange("p (c o) -> p c o", c=NDC)
            wvs = A.wv[par].ap().rearrange("p (c o) -> p c o", c=NDC)
            for src_i, (wvw, wtok, dst, kind) in enumerate((
                    (wqs, wl[0], A.qT, "q"), (wks, wl[1], A.kT, "k"),
                    (wvs, wl[2], A.vTs, "v"))):
                for tblk in range(4):
                    bk = (tblk + src_i) % 2
                    bank = sb[bk]
                    w = [wtok, prol.done_V]
                    if bank_free.get(("s", bk)) is not None:
                        w.append(bank_free[("s", bk)])
                    tok = None
                    for c in range(NDC):
                        tok = s.op("tensor", lambda e, wvw=wvw, c=c, tblk=tblk, bank=bank: e.matmul(
                            bank[:], wvw[:, c, :], xnT_slice(P, c, 512 * tblk, 512),
                            start=(c == 0), stop=(c == NDC - 1)), wait=w, sig="T")
                        w = []
                    ww = [tok] + war
                    if kind == "q":
                        t_ev = s.op("scalar", lambda e, dst=dst, tblk=tblk, bank=bank: e.mul(
                            dst[:, 512 * tblk:512 * tblk + 512], bank[:], 0.125),
                            wait=ww, sig="SC")
                    elif kind == "k":
                        t_ev = s.op("scalar", lambda e, dst=dst, tblk=tblk, bank=bank: e.copy(
                            dst[:, 512 * tblk:512 * tblk + 512], bank[:]),
                            wait=ww, sig="SC")
                    else:
                        t_ev = s.op("vector", lambda e, dst=dst, tblk=tblk, bank=bank: e.tensor_copy(
                            dst[:, 512 * tblk:512 * tblk + 512], bank[:]),
                            wait=ww, sig="V")
                    bank_free[("s", bk)] = t_ev
            vT_done = s.tok("V")
            q_done = s.tok("SC")
            for g in range(2):
                w = [vT_done, P.setup_tok] + war
                if bank_free.get(("tb", g)) is not None:
                    w.append(bank_free[("tb", g)])
                tok = None
                for j in range(8):
                    tt = 8 * g + j
                    tok = s.op("tensor", lambda e, g=g, j=j, tt=tt: e.transpose(
                        tbk[g][:, 128 * j:128 * j + 128],
                        A.vTs[:, 128 * tt:128 * tt + 128], P.ident[:]),
                        wait=w, sig="T")
                    w = []
                t_ev = s.op("vector", lambda e, g=g: e.tensor_copy(
                    A.vc[:, 1024 * g:1024 * g + 1024], tbk[g][:]), wait=[tok], sig="V")
                bank_free[("tb", g)] = t_ev
            v_done = s.tok("V")
            proj_T_done[oc] = s.tok("T")
            for h2 in range(2):
                h = 2 * oc + h2
                _attn_head(s, P, A, C, oc, 64 * h2, q_done, v_done, t_dt,
                           causal=True, stt_scalar=-ALIBI_SLOPES[h])
            attn_T_done[oc] = s.tok("T")
            oT_tok[oc] = s.tok("V")
    attn_done = s.toks("T", "V", "SC")
    _emit_out_proj(s, nc, P, A, prol, wosb, t_wo, attn_done)


def emit_attn_a1(s, nc, es, P):
    A = NS()
    with ExitStack() as es1:
        A.scr = es1.enter_context(nc.sbuf_tensor("t_scr", [128, D], F32))
        A.xn = es1.enter_context(nc.sbuf_tensor("t_xn", [128, 2 * D], BF16))
        with ExitStack() as esp:
            A.psumb = [esp.enter_context(nc.psum_tensor(f"t_psb{i}", [128, 1024], BF16))
                       for i in range(2)]
            prol = emit_prologue(s, P, A)
    gsafe0 = s.toks("T", "V", "SC")
    A.dtab = es.enter_context(nc.sbuf_tensor("t_dtab", [128, 2432], BF16))
    A.wq = [es.enter_context(nc.sbuf_tensor(f"t_wq{i}", [128, NDC * 128], BF16))
            for i in range(2)]
    A.wk = [es.enter_context(nc.sbuf_tensor(f"t_wk{i}", [128, NDC * 128], BF16))
            for i in range(2)]
    A.wv = [es.enter_context(nc.sbuf_tensor(f"t_wv{i}", [128, NDC * 128], BF16))
            for i in range(2)]
    A.wo = es.enter_context(nc.sbuf_tensor("t_wo", [128, NDC * 1024], BF16))
    A.qT = es.enter_context(nc.sbuf_tensor("t_qT", [128, T], BF16))
    A.kT = es.enter_context(nc.sbuf_tensor("t_kT", [128, T], BF16))
    A.vTs = es.enter_context(nc.sbuf_tensor("t_vTs", [128, T], BF16))
    A.vc = es.enter_context(nc.sbuf_tensor("t_vc", [128, NTT * 128], BF16))
    A.oT = es.enter_context(nc.sbuf_tensor("t_oT", [128, NDC * T], BF16))
    A.Ssb = es.enter_context(nc.sbuf_tensor("t_Ssb", [128, T], F32))
    A.Psb = es.enter_context(nc.sbuf_tensor("t_Psb", [128, T], BF16))
    A.PT = es.enter_context(nc.sbuf_tensor("t_PT", [128, NTT * 128], BF16))
    A.rs = es.enter_context(nc.sbuf_tensor("t_rs", [128, 16], F32))
    A.osb = es.enter_context(nc.sbuf_tensor("t_osb", [128, 64], BF16))
    A.cosb = es.enter_context(nc.sbuf_tensor("t_cosb", [128, T], BF16))
    A.sinb = es.enter_context(nc.sbuf_tensor("t_sinb", [128, T], BF16))
    A.rot = es.enter_context(nc.sbuf_tensor("t_rot", [128, 512], BF16))
    A.qb = es.enter_context(nc.sbuf_tensor("t_qb", [128, 8], F32))
    A.kb = es.enter_context(nc.sbuf_tensor("t_kb", [128, 8], F32))
    A.vb = es.enter_context(nc.sbuf_tensor("t_vb", [128, 8], F32))
    A.pbrow = es.enter_context(nc.sbuf_tensor("t_pbrow", [1, D], BF16))
    t_mz = s.op("gpsimd", lambda e: e.memset(A.dtab[:], 0.0), wait=gsafe0, sig="P")
    t_tab = s.op("gpsimd", lambda e: e.affine_select(
        out=A.dtab[:], in_=A.dtab[:], pattern=[[-1, 2432]],
        compare_op=ALU.is_ge, fill=-1e38, base=1920, channel_multiplier=1),
        wait=[t_mz], sig="P")
    t_wo = s.op("gpsimd", lambda e: e.dma_start(
        out=A.wo.ap().rearrange("p (c d) -> p c d", c=NDC),
        in_=P.W["a1_wo"].rearrange("(c p) d -> p c d", p=128)),
        wait=gsafe0, sig="d_a1wo", n=16)
    t_cos = s.op("gpsimd", lambda e: e.dma_start(out=A.cosb[:], in_=P.W["a1_cos"]),
                 wait=gsafe0, sig="d_a1cos", n=16)
    t_sin = s.op("gpsimd", lambda e: e.dma_start(out=A.sinb[:], in_=P.W["a1_sin"]),
                 wait=gsafe0, sig="d_a1sin", n=16)
    t_qb = s.op("sync", lambda e: e.dma_start(out=A.qb[:], in_=P.W["a1_qb"]),
                wait=gsafe0, sig="d_a1qb", n=16)
    t_kb = s.op("sync", lambda e: e.dma_start(out=A.kb[:], in_=P.W["a1_kb"]),
                wait=gsafe0, sig="d_a1kb", n=16)
    t_vb = s.op("sync", lambda e: e.dma_start(out=A.vb[:], in_=P.W["a1_vb"]),
                wait=gsafe0, sig="d_a1vb", n=16)
    t_pb = s.op("gpsimd", lambda e: e.dma_start(out=A.pbrow[:], in_=P.W["a1_pb"]),
                wait=gsafe0, sig="d_a1pb", n=16)
    bload = [t_qb, t_kb, t_vb]

    wosb = A.wo.ap().rearrange("p (c d) -> p c d", c=NDC)
    attn_T_done = {}
    proj_T_done = {}
    with ExitStack() as esc:
        C = NS(
            sb=[esc.enter_context(nc.psum_tensor(f"t_sp{i}", [128, 512], F32))
                for i in range(4)],
            avb=esc.enter_context(nc.psum_tensor("t_av", [128, 512], F32)),
            tbk=[esc.enter_context(nc.psum_tensor(f"t_tb{i}", [128, 1024], BF16))
                 for i in range(2)],
            bank_free={}, rs_free={sl: None for sl in range(4)},
            psb_war=None, prev_exp=None)
        sb, tbk, bank_free = C.sb, C.tbk, C.bank_free
        for oc in range(8):
            par = oc % 2
            wl = []
            for nm, buf in (("a1_wq", A.wq[par]), ("a1_wk", A.wk[par]), ("a1_wv", A.wv[par])):
                w = list(gsafe0) if oc < 2 else [proj_T_done[oc - 2]]
                wl.append(s.op("gpsimd", lambda e, nm=nm, buf=buf, oc=oc: e.dma_start(
                    out=buf.ap().rearrange("p (c o) -> p c o", c=NDC),
                    in_=P.W[nm][:, 128 * oc:128 * oc + 128].rearrange(
                        "(c p) o -> p c o", p=128)), wait=w,
                    sig=f"d_{nm[3:]}_{par}", n=16))
            war = [attn_T_done[oc - 1]] if oc >= 1 else []
            wqs = A.wq[par].ap().rearrange("p (c o) -> p c o", c=NDC)
            wks = A.wk[par].ap().rearrange("p (c o) -> p c o", c=NDC)
            wvs = A.wv[par].ap().rearrange("p (c o) -> p c o", c=NDC)
            etoks = {}
            for src_i, (wvw, wtok, dst, kind) in enumerate((
                    (wqs, wl[0], A.qT, "q"), (wks, wl[1], A.kT, "k"),
                    (wvs, wl[2], A.vTs, "v"))):
                for tblk in range(4):
                    bk = (tblk + src_i) % 2
                    bank = sb[bk]
                    w = [wtok, prol.done_V]
                    if bank_free.get(("s", bk)) is not None:
                        w.append(bank_free[("s", bk)])
                    tok = None
                    for c in range(NDC):
                        tok = s.op("tensor", lambda e, wvw=wvw, c=c, tblk=tblk, bank=bank: e.matmul(
                            bank[:], wvw[:, c, :], xnT_slice(P, c, 512 * tblk, 512),
                            start=(c == 0), stop=(c == NDC - 1)), wait=w, sig="T")
                        w = []
                    ww = [tok] + bload + war
                    if kind == "q":
                        t_ev = s.op("vector", lambda e, dst=dst, tblk=tblk, bank=bank, oc=oc: e.tensor_scalar(
                            dst[:, 512 * tblk:512 * tblk + 512], bank[:],
                            A.qb[:, oc:oc + 1], 0.125, ALU.add, ALU.mult),
                            wait=ww, sig="V")
                    elif kind == "k":
                        t_ev = s.op("vector", lambda e, dst=dst, tblk=tblk, bank=bank, oc=oc: e.tensor_scalar(
                            dst[:, 512 * tblk:512 * tblk + 512], bank[:],
                            A.kb[:, oc:oc + 1], None, ALU.add),
                            wait=ww, sig="V")
                    else:
                        t_ev = s.op("vector", lambda e, dst=dst, tblk=tblk, bank=bank, oc=oc: e.tensor_scalar(
                            dst[:, 512 * tblk:512 * tblk + 512], bank[:],
                            A.vb[:, oc:oc + 1], None, ALU.add),
                            wait=ww, sig="V")
                    bank_free[("s", bk)] = t_ev
                    etoks[(kind, tblk)] = t_ev
            rot_free = None
            for kind, dst in (("q", A.qT), ("k", A.kT)):
                for tblk in range(4):
                    sl_ = slice(512 * tblk, 512 * tblk + 512)
                    last_ev = etoks[(kind, tblk)]
                    cps = []
                    for (d0, s0) in ((0, 32), (32, 0), (64, 96), (96, 64)):
                        w = [last_ev]
                        if rot_free is not None:
                            w.append(rot_free)
                        cps.append(s.op("vector", lambda e, dst=dst, d0=d0, s0=s0, sl_=sl_: e.tensor_copy(
                            A.rot[d0:d0 + 32, :], dst[s0:s0 + 32, sl_]),
                            wait=w, sig="V"))
                    t_m2 = s.op("vector", lambda e, sl_=sl_: e.tensor_mul(
                        A.rot[:], A.rot[:], A.sinb[:, sl_]),
                        wait=[cps[-1], t_sin], sig="V")
                    t_m1 = s.op("vector", lambda e, dst=dst, sl_=sl_: e.tensor_mul(
                        dst[:, sl_], dst[:, sl_], A.cosb[:, sl_]),
                        wait=[cps[-1], t_cos], sig="V")
                    rot_free = s.op("vector", lambda e, dst=dst, sl_=sl_: e.tensor_add(
                        dst[:, sl_], dst[:, sl_], A.rot[:]),
                        wait=[t_m1, t_m2], sig="V")
            vT_done = s.tok("V")
            for g in range(2):
                w = [vT_done, P.setup_tok] + war
                if bank_free.get(("tb", g)) is not None:
                    w.append(bank_free[("tb", g)])
                tok = None
                for j in range(8):
                    tt = 8 * g + j
                    tok = s.op("tensor", lambda e, g=g, j=j, tt=tt: e.transpose(
                        tbk[g][:, 128 * j:128 * j + 128],
                        A.vTs[:, 128 * tt:128 * tt + 128], P.ident[:]),
                        wait=w, sig="T")
                    w = []
                t_ev = s.op("vector", lambda e, g=g: e.tensor_copy(
                    A.vc[:, 1024 * g:1024 * g + 1024], tbk[g][:]), wait=[tok], sig="V")
                bank_free[("tb", g)] = t_ev
            v_done = s.tok("V")
            q_done = s.tok("V")
            proj_T_done[oc] = s.tok("T")
            for h2 in range(2):
                _attn_head(s, P, A, C, oc, 64 * h2, q_done, v_done, t_tab,
                           causal=True, stt_scalar=1.0)
            attn_T_done[oc] = s.tok("T")
    attn_done = s.toks("T", "V", "SC")
    _emit_out_proj(s, nc, P, A, prol, wosb, t_wo, attn_done,
                   bias_row=A.pbrow, t_bias_load=t_pb)


def _emit_out_proj_streamed(s, nc, es, P, A, prol, wov, attn_done, bias_row,
                            t_bias_load, pfx):
    wosb = [es.enter_context(nc.sbuf_tensor(f"t_wos{i}_{pfx}", [128, D], BF16))
            for i in range(2)]
    with ExitStack() as esc2:
        yb = [esc2.enter_context(nc.psum_tensor(f"t_yb{i}", [128, 512], F32))
              for i in range(8)]
        ybfree = {}
        wo_last_T = [None, None]
        for tb in range(4):
            mm_tok = None
            for ks in range(8):
                par = ks % 2
                fc = ks
                w = list(attn_done) if wo_last_T[par] is None else [wo_last_T[par]]
                wotok = s.op("gpsimd", lambda e, ks=ks, par=par: e.dma_start(
                    out=wosb[par][:],
                    in_=wov.rearrange("(ks p) d -> ks p d", p=128)[ks]),
                    wait=w, sig=f"d_{pfx}wos_{par}", n=16)
                for tt in range(4):
                    for db in range(2):
                        bi = 2 * tt + db
                        gtt = 4 * tb + tt
                        w = [wotok] + (attn_done if fc == 0 else [])
                        if fc == 0 and ybfree.get(bi) is not None:
                            w.append(ybfree[bi])
                        mm_tok = s.op("tensor", lambda e, fc=fc, gtt=gtt, db=db, par=par, bi=bi: e.matmul(
                            yb[bi][:],
                            A.oT[:, T * fc + 128 * gtt: T * fc + 128 * gtt + 128],
                            wosb[par][:, 512 * db: 512 * db + 512],
                            start=(fc == 0), stop=False), wait=w, sig="T")
                wo_last_T[par] = mm_tok
            for tt in range(4):
                for db in range(2):
                    bi = 2 * tt + db
                    gtt = 4 * tb + tt
                    t_bias = s.op("tensor", lambda e, db=db, bi=bi: e.matmul(
                        yb[bi][:], P.ones_row[0:1, 0:128],
                        bias_row[0:1, 512 * db: 512 * db + 512],
                        start=False, stop=True),
                        wait=[P.setup_tok, t_bias_load], sig="T")
                    pv, psc = prol.stats_done[gtt]
                    t_ev = s.op("vector", lambda e, gtt=gtt, db=db, bi=bi: e.tensor_add(
                        x_slice(P, gtt, 512 * db, 512), yb[bi][:],
                        x_slice(P, gtt, 512 * db, 512)),
                        wait=[t_bias, pv, psc], sig="V")
                    ybfree[bi] = t_ev


def emit_attn_b1(s, nc, es, P):
    A = NS()
    with ExitStack() as es1:
        A.scr = es1.enter_context(nc.sbuf_tensor("t_scr", [128, D], F32))
        A.xn = es1.enter_context(nc.sbuf_tensor("t_xn", [128, 2 * D], BF16))
        with ExitStack() as esp:
            A.psumb = [esp.enter_context(nc.psum_tensor(f"t_psb{i}", [128, 1024], BF16))
                       for i in range(2)]
            prol = emit_prologue(s, P, A)
    gsafe0 = s.toks("T", "V", "SC")
    QSCALE = 1.0 / float(np.sqrt(np.float32(1040.0)))
    A.ctq = es.enter_context(nc.sbuf_tensor("t_ctq", [128, 4 * T], BF16))
    A.ctkv = es.enter_context(nc.sbuf_tensor("t_ctkv", [128, 4 * T], BF16))
    A.dqb = es.enter_context(nc.sbuf_tensor("t_dqb", [128, 4], F32))
    A.dkvb = es.enter_context(nc.sbuf_tensor("t_dkvb", [128, 4], F32))
    ct_done = None
    with ExitStack() as esw:
        wdq = esw.enter_context(nc.sbuf_tensor("t_wdq", [128, NDC * 512], BF16))
        wdkv = esw.enter_context(nc.sbuf_tensor("t_wdkv", [128, NDC * 512], BF16))
        with ExitStack() as escp:
            pb = [escp.enter_context(nc.psum_tensor(f"t_pp{i}", [128, 512], F32))
                  for i in range(2)]
            t_wdq = s.op("gpsimd", lambda e: e.dma_start(
                out=wdq.ap().rearrange("p (c o) -> p c o", c=NDC),
                in_=P.W["b1_wdq"].rearrange("(c p) o -> p c o", p=128)),
                wait=gsafe0, sig="d_b1wdq", n=16)
            t_wdkv = s.op("gpsimd", lambda e: e.dma_start(
                out=wdkv.ap().rearrange("p (c o) -> p c o", c=NDC),
                in_=P.W["b1_wdkv"].rearrange("(c p) o -> p c o", p=128)),
                wait=gsafe0, sig="d_b1wdkv", n=16)
            t_dqb = s.op("sync", lambda e: e.dma_start(out=A.dqb[:], in_=P.W["b1_dqb"]),
                         wait=gsafe0, sig="d_b1dqb", n=16)
            t_dkvb = s.op("sync", lambda e: e.dma_start(out=A.dkvb[:], in_=P.W["b1_dkvb"]),
                          wait=gsafe0, sig="d_b1dkvb", n=16)
            wdqv = wdq.ap().rearrange("p (c o) -> p c o", c=NDC)
            wdkvv = wdkv.ap().rearrange("p (c o) -> p c o", c=NDC)
            bfree = [None, None]
            for wview, wtok, btok, bias, dst in (
                    (wdqv, t_wdq, t_dqb, A.dqb, A.ctq),
                    (wdkvv, t_wdkv, t_dkvb, A.dkvb, A.ctkv)):
                for qc in range(4):
                    for tblk in range(4):
                        bi = tblk % 2
                        w = [wtok, prol.done_V]
                        if bfree[bi] is not None:
                            w.append(bfree[bi])
                        tok = None
                        for c in range(NDC):
                            tok = s.op("tensor", lambda e, wview=wview, c=c, qc=qc, tblk=tblk, bi=bi: e.matmul(
                                pb[bi][:], wview[:, c, 128 * qc:128 * qc + 128],
                                xnT_slice(P, c, 512 * tblk, 512),
                                start=(c == 0), stop=(c == NDC - 1)), wait=w, sig="T")
                            w = []
                        import os as _os
                        if _os.environ.get("DBG_NOBIAS") == "1":
                            t_ev = s.op("vector", lambda e, dst=dst, qc=qc, tblk=tblk, bi=bi: e.tensor_copy(
                                dst[:, T * qc + 512 * tblk: T * qc + 512 * tblk + 512],
                                pb[bi][:]), wait=[tok, btok], sig="V")
                        else:
                            t_ev = s.op("vector", lambda e, dst=dst, bias=bias, qc=qc, tblk=tblk, bi=bi: e.tensor_scalar(
                                dst[:, T * qc + 512 * tblk: T * qc + 512 * tblk + 512],
                                pb[bi][:], bias[:, qc:qc + 1], None, ALU.add),
                                wait=[tok, btok], sig="V")
                        bfree[bi] = t_ev
            ct_done = s.toks("T", "V")
            import os as _os
            if _os.environ.get("DBG_CTONLY") == "1":
                P.dbg = {"d_ctq": (A.ctq.ap(), BF16), "d_ctkv": (A.ctkv.ap(), BF16),
                         "d_wdq": (wdq.ap(), BF16), "d_dqb": (A.dqb.ap(), F32)}
                return
    A.dtab = None
    A.wuq = [es.enter_context(nc.sbuf_tensor("t_wuq0", [128, 4 * 128], BF16))]
    A.wqr = [es.enter_context(nc.sbuf_tensor("t_wqr0", [128, 4 * 128], BF16))]
    A.wuk = [es.enter_context(nc.sbuf_tensor("t_wuk0", [128, 4 * 128], BF16))]
    A.wkr = [es.enter_context(nc.sbuf_tensor("t_wkr0", [128, NDC * 128], BF16))]
    A.wuv = [es.enter_context(nc.sbuf_tensor("t_wuv0", [128, 4 * 128], BF16))]
    A.ub = es.enter_context(nc.sbuf_tensor("t_ub", [128, 4 * 8], F32))
    A.krb = es.enter_context(nc.sbuf_tensor("t_krb", [128, 8], F32))
    A.qT = es.enter_context(nc.sbuf_tensor("t_qT", [128, T], BF16))
    A.kT = es.enter_context(nc.sbuf_tensor("t_kT", [128, T], BF16))
    A.tmpT = es.enter_context(nc.sbuf_tensor("t_tmpT", [128, T], BF16))
    A.vc = es.enter_context(nc.sbuf_tensor("t_vc", [128, NTT * 128], BF16))
    A.oT = es.enter_context(nc.sbuf_tensor("t_oT", [128, NDC * T], BF16))
    A.Psb = es.enter_context(nc.sbuf_tensor("t_Psb", [128, T], BF16))
    A.PT = es.enter_context(nc.sbuf_tensor("t_PT", [128, NTT * 128], BF16))
    A.rs = es.enter_context(nc.sbuf_tensor("t_rs", [128, 64], F32))
    A.osb = es.enter_context(nc.sbuf_tensor("t_osb", [128, 64], BF16))
    A.cosb = es.enter_context(nc.sbuf_tensor("t_cosb", [128, T], BF16))
    A.sinb = es.enter_context(nc.sbuf_tensor("t_sinb", [128, T], BF16))
    A.rot = es.enter_context(nc.sbuf_tensor("t_rot", [128, 512], BF16))
    A.borow = es.enter_context(nc.sbuf_tensor("t_borow", [1, D], BF16))
    t_ub = s.op("sync", lambda e: e.dma_start(out=A.ub[:], in_=P.W["b1_ub"]),
                wait=list(gsafe0) + list(ct_done), sig="d_b1ub", n=16)
    t_krb = s.op("sync", lambda e: e.dma_start(out=A.krb[:], in_=P.W["b1_krb"]),
                 wait=list(gsafe0) + list(ct_done), sig="d_b1krb", n=16)
    t_bo = s.op("gpsimd", lambda e: e.dma_start(out=A.borow[:], in_=P.W["b1_bo"]),
                wait=gsafe0, sig="d_b1bo", n=16)
    bload = [t_ub, t_krb]
    attn_T_done = {}
    proj_T_done = {}
    tab_tok = {}
    with ExitStack() as esc:
        C = NS(
            sb=[esc.enter_context(nc.psum_tensor(f"t_sp{i}", [128, 512], F32))
                for i in range(4)],
            avb=esc.enter_context(nc.psum_tensor("t_av", [128, 512], F32)),
            tbk=[esc.enter_context(nc.psum_tensor(f"t_tb{i}", [128, 1024], BF16))
                 for i in range(2)],
            bank_free={}, rs_free={sl: None for sl in range(4)},
            psb_war=None, prev_exp=None)
        sb, tbk, bank_free = C.sb, C.tbk, C.bank_free
        ctqv = A.ctq.ap().rearrange("p (c t) -> p c t", c=4)
        ctkvv = A.ctkv.ap().rearrange("p (c t) -> p c t", c=4)
        for oc in range(8):
            par = 0
            wl = {}
            for nm, buf, nch in (("b1_wuq", A.wuq[par], 4), ("b1_wqr", A.wqr[par], 4),
                                 ("b1_wuk", A.wuk[par], 4), ("b1_wkr", A.wkr[par], 8),
                                 ("b1_wuv", A.wuv[par], 4)):
                w = (list(gsafe0) + list(ct_done)) if oc == 0 else [proj_T_done[oc - 1]]
                wl[nm] = s.op("gpsimd", lambda e, nm=nm, buf=buf, nch=nch, oc=oc: e.dma_start(
                    out=buf.ap().rearrange("p (c o) -> p c o", c=nch),
                    in_=P.W[nm][:, 128 * oc:128 * oc + 128].rearrange(
                        "(c p) o -> p c o", p=128)), wait=w, sig=f"d_{nm}", n=16)
            wtb = [tab_tok[oc - 1]] if oc >= 1 else list(gsafe0)
            t_cos = s.op("gpsimd", lambda e, oc=oc: e.dma_start(
                out=A.cosb[:], in_=P.W["b1_cos"][oc]), wait=wtb, sig="d_b1cos", n=16)
            t_sin = s.op("gpsimd", lambda e, oc=oc: e.dma_start(
                out=A.sinb[:], in_=P.W["b1_sin"][oc]), wait=wtb, sig="d_b1sin", n=16)
            war = [attn_T_done[oc - 1]] if oc >= 1 else []
            wuqv = A.wuq[par].ap().rearrange("p (c o) -> p c o", c=4)
            wqrv = A.wqr[par].ap().rearrange("p (c o) -> p c o", c=4)
            wukv = A.wuk[par].ap().rearrange("p (c o) -> p c o", c=4)
            wkrv = A.wkr[par].ap().rearrange("p (c o) -> p c o", c=NDC)
            wuvv = A.wuv[par].ap().rearrange("p (c o) -> p c o", c=NDC // 2)
            jobs = (
                ("qc", A.qT, wuqv, 4, ctqv, A.ub, 0, QSCALE, False),
                ("qr", A.tmpT, wqrv, 4, ctqv, A.ub, 1, QSCALE, True),
                ("kc", A.kT, wukv, 4, ctkvv, A.ub, 2, 1.0, False),
                ("kr", A.tmpT, wkrv, 8, None, A.krb, 0, 1.0, True),
                ("v", A.tmpT, wuvv, 4, ctkvv, A.ub, 3, 1.0, False),
            )
            rot_free = None
            ev_toks = {}
            for (jn, dst, wview, nch, srcv, btile, bcol, scale, do_rope) in jobs:
                wtok = wl[{"qc": "b1_wuq", "qr": "b1_wqr", "kc": "b1_wuk",
                           "kr": "b1_wkr", "v": "b1_wuv"}[jn]]
                for tblk in range(4):
                    bi = tblk % 2
                    bank = sb[bi]
                    w = [wtok, prol.done_V] + bload
                    if jn in ("qc", "qr"):
                        w += list(ct_done) if oc == 0 else []
                    if bank_free.get(("s", bi)) is not None:
                        w.append(bank_free[("s", bi)])
                    tok = None
                    for c in range(nch):
                        if srcv is None:
                            rhs = xnT_slice(P, c, 512 * tblk, 512)
                        else:
                            rhs = None
                        tok = s.op("tensor", lambda e, wview=wview, c=c, tblk=tblk, bank=bank, srcv=srcv, nch=nch: e.matmul(
                            bank[:], wview[:, c, :],
                            (xnT_slice(P, c, 512 * tblk, 512) if srcv is None
                             else srcv[:, c, 512 * tblk:512 * tblk + 512]),
                            start=(c == 0), stop=(c == nch - 1)), wait=w, sig="T")
                        w = []
                    ww = [tok] + war
                    bcol_ap = btile[:, 8 * bcol + oc: 8 * bcol + oc + 1] if btile is A.ub \
                        else btile[:, oc:oc + 1]
                    if jn == "qc":
                        t_ev = s.op("vector", lambda e, dst=dst, tblk=tblk, bank=bank, bcol_ap=bcol_ap, scale=scale: e.tensor_scalar(
                            dst[:, 512 * tblk:512 * tblk + 512], bank[:],
                            bcol_ap, scale, ALU.add, ALU.mult), wait=ww, sig="V")
                    elif scale != 1.0:
                        t_ev = s.op("vector", lambda e, dst=dst, tblk=tblk, bank=bank, bcol_ap=bcol_ap, scale=scale: e.tensor_scalar(
                            dst[:, 512 * tblk:512 * tblk + 512], bank[:],
                            bcol_ap, scale, ALU.add, ALU.mult), wait=ww, sig="V")
                    else:
                        t_ev = s.op("vector", lambda e, dst=dst, tblk=tblk, bank=bank, bcol_ap=bcol_ap: e.tensor_scalar(
                            dst[:, 512 * tblk:512 * tblk + 512], bank[:],
                            bcol_ap, None, ALU.add), wait=ww, sig="V")
                    bank_free[("s", bi)] = t_ev
                    ev_toks[(jn, tblk)] = t_ev
                    if do_rope:
                        sl_ = slice(512 * tblk, 512 * tblk + 512)
                        wz = [t_ev, t_cos, t_sin]
                        if rot_free is not None:
                            wz.append(rot_free)
                        cps = []
                        for (d0, s0) in ((0, 32), (32, 0), (64, 96), (96, 64)):
                            cps.append(s.op("vector", lambda e, dst=dst, d0=d0, s0=s0, sl_=sl_: e.tensor_copy(
                                A.rot[d0:d0 + 32, :], dst[s0:s0 + 32, sl_]),
                                wait=wz if not cps else [], sig="V"))
                        t_m2 = s.op("vector", lambda e, sl_=sl_: e.tensor_mul(
                            A.rot[:], A.rot[:], A.sinb[:, sl_]), wait=[cps[-1]], sig="V")
                        t_m1 = s.op("vector", lambda e, dst=dst, sl_=sl_: e.tensor_mul(
                            dst[:, sl_], dst[:, sl_], A.cosb[:, sl_]),
                            wait=[cps[-1]], sig="V")
                        acc = A.qT if jn == "qr" else A.kT
                        base_tok = ev_toks[("qc" if jn == "qr" else "kc", tblk)]
                        rot_free = s.op("vector", lambda e, dst=dst, acc=acc, sl_=sl_: e.scalar_tensor_tensor(
                            dst[:, sl_], dst[:, sl_], 1.0, A.rot[:],
                            ALU.mult, ALU.add), wait=[t_m1, t_m2], sig="V")
                        s.op("vector", lambda e, dst=dst, acc=acc, sl_=sl_: e.tensor_add(
                            acc[:, sl_], acc[:, sl_], dst[:, sl_]),
                            wait=[rot_free, base_tok], sig="V")
                if jn == "v":
                    vT_done = s.tok("V")
                    for g in range(2):
                        w = [vT_done, P.setup_tok] + war
                        if bank_free.get(("tb", g)) is not None:
                            w.append(bank_free[("tb", g)])
                        tok = None
                        for j in range(8):
                            tt = 8 * g + j
                            tok = s.op("tensor", lambda e, g=g, j=j, tt=tt: e.transpose(
                                tbk[g][:, 128 * j:128 * j + 128],
                                A.tmpT[:, 128 * tt:128 * tt + 128], P.ident[:]),
                                wait=w, sig="T")
                            w = []
                        t_ev2 = s.op("vector", lambda e, g=g: e.tensor_copy(
                            A.vc[:, 1024 * g:1024 * g + 1024], tbk[g][:]),
                            wait=[tok], sig="V")
                        bank_free[("tb", g)] = t_ev2
            tab_tok[oc] = s.tok("V")
            v_done = s.tok("V")
            q_done = s.tok("V")
            proj_T_done[oc] = s.tok("T")
            for h2 in range(2):
                _attn_head(s, P, A, C, oc, 64 * h2, q_done, v_done, None,
                           causal=False, stt_scalar=None)
            attn_T_done[oc] = s.tok("T")
    attn_done = s.toks("T", "V", "SC")
    P.dbg = {"d_ctq": (A.ctq.ap(), BF16), "d_ctkv": (A.ctkv.ap(), BF16),
             "d_qT": (A.qT.ap(), BF16), "d_kT": (A.kT.ap(), BF16),
             "d_vc": (A.vc.ap(), BF16), "d_oT": (A.oT.ap(), BF16),
             "d_Psb": (A.Psb.ap(), BF16), "d_rs": (A.rs.ap(), F32)}
    _emit_out_proj_streamed(s, nc, es, P, A, prol, P.W["b1_wo"], attn_done,
                            A.borow, t_bo, "b1")


def emit_attn_b0(s, nc, es, P):
    DN = float(64.0 ** -0.25)
    RATIO = 0.0625
    EPS = 1e-4
    A = NS()
    with ExitStack() as es1:
        A.scr = es1.enter_context(nc.sbuf_tensor("t_scr", [128, D], F32))
        A.xn = es1.enter_context(nc.sbuf_tensor("t_xn", [128, 2 * D], BF16))
        with ExitStack() as esp:
            A.psumb = [esp.enter_context(nc.psum_tensor(f"t_psb{i}", [128, 1024], BF16))
                       for i in range(2)]
            prol = emit_prologue(s, P, A)
    gsafe0 = s.toks("T", "V", "SC")
    A.kT = es.enter_context(nc.sbuf_tensor("t_kT", [128, NDC * T], BF16))
    A.oT = es.enter_context(nc.sbuf_tensor("t_oT", [128, NDC * T], BF16))
    A.wk = [es.enter_context(nc.sbuf_tensor(f"t_wk{i}", [128, NDC * 128], BF16))
            for i in range(2)]
    A.wq = [es.enter_context(nc.sbuf_tensor(f"t_wq{i}", [128, NDC * 128], BF16))
            for i in range(2)]
    A.wv = [es.enter_context(nc.sbuf_tensor(f"t_wv{i}", [128, NDC * 128], BF16))
            for i in range(2)]
    A.ptsb = es.enter_context(nc.sbuf_tensor("t_ptsb", [128, 256], BF16))
    A.onescol = es.enter_context(nc.sbuf_tensor("t_onescol", [128, 1], BF16))
    A.qT = es.enter_context(nc.sbuf_tensor("t_qT", [128, T], BF16))
    A.sqc = es.enter_context(nc.sbuf_tensor("t_sqc", [128, T], BF16))
    A.vca = es.enter_context(nc.sbuf_tensor("t_vca", [128, NTT * 130], BF16))
    A.vTs = es.enter_context(nc.sbuf_tensor("t_vTs", [128, T], BF16))
    A.diagk = es.enter_context(nc.sbuf_tensor("t_diagk", [128, 256], F32))
    A.negdm = es.enter_context(nc.sbuf_tensor("t_negdm", [128, 256], F32))
    A.gmax = es.enter_context(nc.sbuf_tensor("t_gmax", [128, 8], F32))
    A.kp = es.enter_context(nc.sbuf_tensor("t_kp", [128, 256], BF16))
    A.qp = es.enter_context(nc.sbuf_tensor("t_qp", [128, 256], BF16))
    A.expsb = es.enter_context(nc.sbuf_tensor("t_expsb", [128, 256], F32))
    A.qpT = es.enter_context(nc.sbuf_tensor("t_qpT", [128, 256], BF16))
    A.ctx = es.enter_context(nc.sbuf_tensor("t_ctx", [128, 130], BF16))
    A.osb = es.enter_context(nc.sbuf_tensor("t_osb", [128, 64], BF16))
    A.ccs = es.enter_context(nc.sbuf_tensor("t_ccs", [128, 128], F32))
    A.borow = es.enter_context(nc.sbuf_tensor("t_borow", [1, D], BF16))
    cc_in = nc.dram_tensor("b0cc_in", [128, 128], F32)
    cc_out = nc.dram_tensor("b0cc_out", [128, 128], F32)
    t_pt = s.op("gpsimd", lambda e: e.dma_start(out=A.ptsb[:], in_=P.W["b0_pt"]),
                wait=gsafe0, sig="d_b0pt", n=16)
    t_bo = s.op("gpsimd", lambda e: e.dma_start(out=A.borow[:], in_=P.W["b0_bo"]),
                wait=gsafe0, sig="d_b0bo", n=16)
    t_vca1 = s.op("gpsimd", lambda e: e.memset(A.vca[:], 1.0), wait=gsafe0, sig="P")
    t_oc1 = s.op("gpsimd", lambda e: e.memset(A.onescol[:], 1.0), wait=gsafe0, sig="P")
    t_gm0 = s.op("vector", lambda e: e.memset(A.gmax[:, 0:1], -3e38),
                 wait=gsafe0, sig="V")
    proj_T_done = {}
    with ExitStack() as esc:
        sbk = [esc.enter_context(nc.psum_tensor(f"t_sp{i}", [128, 512], F32))
               for i in range(2)]
        dgb = esc.enter_context(nc.psum_tensor("t_dg", [128, 512], F32))
        ctxb = [esc.enter_context(nc.psum_tensor(f"t_cx{i}", [128, 512], F32))
                for i in range(2)]
        outb = esc.enter_context(nc.psum_tensor("t_ob", [128, 512], F32))
        tbk = [esc.enter_context(nc.psum_tensor(f"t_tb{i}", [128, 1024], BF16))
               for i in range(2)]
        bank_free = {}
        t_gmax = t_gm0
        t_diag_prev = None
        for oc in range(8):
            par = oc % 2
            w = list(gsafe0) if oc < 2 else [proj_T_done[oc - 2]]
            t_wk = s.op("gpsimd", lambda e, oc=oc, par=par: e.dma_start(
                out=A.wk[par].ap().rearrange("p (c o) -> p c o", c=NDC),
                in_=P.W["b0_wk"][:, 128 * oc:128 * oc + 128].rearrange(
                    "(c p) o -> p c o", p=128)), wait=w, sig=f"d_b0wk_{par}", n=16)
            wks = A.wk[par].ap().rearrange("p (c o) -> p c o", c=NDC)
            for tblk in range(4):
                bi = tblk % 2
                bank = sbk[bi]
                w = [t_wk, prol.done_V]
                if bank_free.get(("s", bi)) is not None:
                    w.append(bank_free[("s", bi)])
                tok = None
                for c in range(NDC):
                    tok = s.op("tensor", lambda e, wks=wks, c=c, tblk=tblk, bank=bank: e.matmul(
                        bank[:], wks[:, c, :], xnT_slice(P, c, 512 * tblk, 512),
                        start=(c == 0), stop=(c == NDC - 1)), wait=w, sig="T")
                    w = []
                t_ev = s.op("scalar", lambda e, oc=oc, tblk=tblk, bank=bank: e.mul(
                    A.kT[:, T * oc + 512 * tblk: T * oc + 512 * tblk + 512],
                    bank[:], DN), wait=[tok], sig="SC")
                bank_free[("s", bi)] = t_ev
            k_done = s.tok("SC")
            t_sq = s.op("scalar", lambda e, oc=oc: e.activation(
                out=A.sqc[:], in_=A.kT[:, T * oc:T * oc + T], func=AF.Square,
                bias=0.0, scale=1.0), wait=[k_done], sig="SC")
            for h2 in range(2):
                h = 2 * oc + h2
                r0 = 64 * h2
                for tt in range(NTT):
                    bi = tt % 2
                    bank = sbk[bi]
                    w = [k_done, t_pt]
                    if bank_free.get(("s", bi)) is not None:
                        w.append(bank_free[("s", bi)])
                    t_dd = s.op("tensor", lambda e, oc=oc, r0=r0, tt=tt, bank=bank: e.matmul(
                        bank[:, 0:256],
                        A.kT[r0:r0 + 64, T * oc + 128 * tt: T * oc + 128 * tt + 128],
                        A.ptsb[r0:r0 + 64, :], start=True, stop=True), wait=w, sig="T")
                    t_mx = s.op("vector", lambda e, bank=bank: e.reduce_max(
                        A.gmax[:, 1:2], bank[:, 0:256], axis=AX),
                        wait=[t_dd, t_gmax], sig="V")
                    bank_free[("s", bi)] = t_mx
                    t_gmax = s.op("vector", lambda e: e.tensor_max(
                        A.gmax[:, 0:1], A.gmax[:, 0:1], A.gmax[:, 1:2]),
                        wait=[t_mx], sig="V")
                    w = [t_sq]
                    if t_diag_prev is not None:
                        w.append(t_diag_prev)
                    t_dg = s.op("tensor", lambda e, oc=oc, r0=r0, tt=tt: e.matmul(
                        dgb[:, 0:1],
                        A.sqc[r0:r0 + 64, 128 * tt:128 * tt + 128],
                        A.onescol[r0:r0 + 64, 0:1],
                        start=True, stop=True), wait=w + [t_oc1], sig="T")
                    t_diag_prev = s.op("scalar", lambda e, h=h, tt=tt: e.mul(
                        A.diagk[:, 16 * h + tt:16 * h + tt + 1], dgb[:, 0:1], 0.5),
                        wait=[t_dg], sig="SC")
            proj_T_done[oc] = s.tok("T")
        t_c1 = s.op("vector", lambda e: e.tensor_copy(A.kp[:, 0:1], A.gmax[:, 0:1]),
                    wait=[t_gmax], sig="V")
        t_c2 = s.op("tensor", lambda e: e.transpose(
            tbk[0][0:1, 0:128], A.kp[:, 0:1], P.ident[:]),
            wait=[t_c1, P.setup_tok], sig="T")
        t_mz = s.op("vector", lambda e: e.memset(A.ccs[:], -3e38), wait=[t_gmax], sig="V")
        t_c8 = s.op("vector", lambda e: e.reduce_max(
            A.ccs[0:1, 0:1], tbk[0][0:1, 0:128], axis=AX),
            wait=[t_c2, t_mz], sig="V")
        t_d1 = s.op("gpsimd", lambda e: e.dma_start(out=cc_in.ap(), in_=A.ccs[:, 0:128]),
                    wait=[t_c8], sig="d_cc1", n=16)
        t_cc = s.op("gpsimd", lambda e: e.collective_compute(
            "AllReduce", ALU.max, replica_groups=[list(range(8))],
            ins=[cc_in.ap().opt()], outs=[cc_out.ap().opt()]),
            wait=[t_d1], sig="ccv", n=1)
        t_d2 = s.op("gpsimd", lambda e: e.dma_start(out=A.ccs[:, 0:128], in_=cc_out.ap()),
                    wait=[t_cc], sig="d_cc2", n=16)
        t_on = s.op("vector", lambda e: e.memset(A.expsb[0:1, 0:128], 1.0),
                    wait=[t_c8], sig="V")
        t_bc = s.op("tensor", lambda e: e.matmul(
            dgb[:, 1:2], A.expsb[0:1, 0:128], A.ccs[0:1, 0:1],
            start=True, stop=True), wait=[t_d2, t_on, t_diag_prev], sig="T")
        t_n1 = s.op("vector", lambda e: e.tensor_copy(A.gmax[:, 2:3], dgb[:, 1:2]),
                    wait=[t_bc], sig="V")
        t_n2 = s.op("vector", lambda e: e.tensor_scalar(
            A.negdm[:], A.diagk[:], A.gmax[:, 2:3], -1.0, ALU.add, ALU.mult),
            wait=[t_n1], sig="V")
        negdm_done = t_n2
        attn_T_done = {}
        for oc in range(8):
            par = oc % 2
            w = [negdm_done] if oc < 2 else [attn_T_done[oc - 2]]
            t_wq = s.op("gpsimd", lambda e, oc=oc, par=par: e.dma_start(
                out=A.wq[par].ap().rearrange("p (c o) -> p c o", c=NDC),
                in_=P.W["b0_wq"][:, 128 * oc:128 * oc + 128].rearrange(
                    "(c p) o -> p c o", p=128)), wait=w, sig=f"d_b0wq_{par}", n=16)
            t_wv = s.op("gpsimd", lambda e, oc=oc, par=par: e.dma_start(
                out=A.wv[par].ap().rearrange("p (c o) -> p c o", c=NDC),
                in_=P.W["b0_wv"][:, 128 * oc:128 * oc + 128].rearrange(
                    "(c p) o -> p c o", p=128)), wait=w, sig=f"d_b0wv_{par}", n=16)
            wqs = A.wq[par].ap().rearrange("p (c o) -> p c o", c=NDC)
            wvs = A.wv[par].ap().rearrange("p (c o) -> p c o", c=NDC)
            war = [attn_T_done[oc - 1]] if oc >= 1 else []
            for src_i, (wview, wtok, kind) in enumerate(((wqs, t_wq, "q"), (wvs, t_wv, "v"))):
                for tblk in range(4):
                    bi = tblk % 2
                    bank = sbk[bi]
                    w = [wtok, prol.done_V]
                    if bank_free.get(("s", bi)) is not None:
                        w.append(bank_free[("s", bi)])
                    tok = None
                    for c in range(NDC):
                        tok = s.op("tensor", lambda e, wview=wview, c=c, tblk=tblk, bank=bank: e.matmul(
                            bank[:], wview[:, c, :], xnT_slice(P, c, 512 * tblk, 512),
                            start=(c == 0), stop=(c == NDC - 1)), wait=w, sig="T")
                        w = []
                    if kind == "q":
                        t_ev = s.op("scalar", lambda e, tblk=tblk, bank=bank: e.mul(
                            A.qT[:, 512 * tblk:512 * tblk + 512], bank[:], DN),
                            wait=[tok] + war, sig="SC")
                    else:
                        t_ev = s.op("vector", lambda e, tblk=tblk, bank=bank: e.tensor_copy(
                            A.vTs[:, 512 * tblk:512 * tblk + 512], bank[:]),
                            wait=[tok] + war, sig="V")
                    bank_free[("s", bi)] = t_ev
            q_done = s.tok("SC")
            vT_done = s.tok("V")
            t_sq = s.op("scalar", lambda e: e.activation(
                out=A.sqc[:], in_=A.qT[:], func=AF.Square, bias=0.0, scale=1.0),
                wait=[q_done], sig="SC")
            for g in range(2):
                w = [vT_done, P.setup_tok, t_vca1] + war
                if bank_free.get(("tb", g)) is not None:
                    w.append(bank_free[("tb", g)])
                tok = None
                for j in range(8):
                    tt = 8 * g + j
                    tok = s.op("tensor", lambda e, g=g, j=j, tt=tt: e.transpose(
                        tbk[g][:, 128 * j:128 * j + 128],
                        A.vTs[:, 128 * tt:128 * tt + 128], P.ident[:]),
                        wait=w, sig="T")
                    w = []
                tks = []
                for j in range(8):
                    tt = 8 * g + j
                    tks.append(s.op("vector", lambda e, g=g, j=j, tt=tt: e.tensor_copy(
                        A.vca[:, 130 * tt:130 * tt + 64],
                        tbk[g][:, 128 * j:128 * j + 64]), wait=[tok], sig="V"))
                    tks.append(s.op("vector", lambda e, g=g, j=j, tt=tt: e.tensor_copy(
                        A.vca[:, 130 * tt + 65:130 * tt + 129],
                        tbk[g][:, 128 * j + 64:128 * j + 128]), wait=[tok], sig="V"))
                bank_free[("tb", g)] = tks[-1]
            v_done = s.tok("V")
            for h2 in range(2):
                h = 2 * oc + h2
                r0 = 64 * h2
                ctx_mm = None
                prev_kp_ev = None
                for tt in range(NTT):
                    bi = tt % 2
                    bank = sbk[bi]
                    w = [negdm_done, v_done]
                    if bank_free.get(("s", bi)) is not None:
                        w.append(bank_free[("s", bi)])
                    t_dd = s.op("tensor", lambda e, oc=oc, r0=r0, tt=tt, bank=bank: e.matmul(
                        bank[:, 0:256],
                        A.kT[r0:r0 + 64, T * oc + 128 * tt: T * oc + 128 * tt + 128],
                        A.ptsb[r0:r0 + 64, :], start=True, stop=True), wait=w, sig="T")
                    wv_ = [t_dd]
                    if prev_kp_ev is not None:
                        wv_.append(prev_kp_ev)
                    t_ex = s.op("scalar", lambda e, h=h, tt=tt, bank=bank: e.activation(
                        out=A.expsb[:, 0:256], in_=bank[:, 0:256], func=AF.Exp,
                        bias=A.negdm[:, 16 * h + tt:16 * h + tt + 1], scale=1.0),
                        wait=wv_, sig="SC")
                    bank_free[("s", bi)] = t_ex
                    t_kp = s.op("vector", lambda e: e.tensor_scalar(
                        A.kp[:], A.expsb[:, 0:256], RATIO, RATIO * EPS,
                        ALU.mult, ALU.add), wait=[t_ex], sig="V")
                    for mc in range(2):
                        w = [t_kp]
                        if tt == 0 and bank_free.get(("cx", mc)) is not None:
                            w.append(bank_free[("cx", mc)])
                        ctx_mm = s.op("tensor", lambda e, mc=mc, tt=tt, h2=h2: e.matmul(
                            ctxb[mc][:, 0:65], A.kp[:, 128 * mc:128 * mc + 128],
                            A.vca[:, 130 * tt + 65 * h2: 130 * tt + 65 * h2 + 65],
                            start=(tt == 0), stop=(tt == NTT - 1)), wait=w, sig="T")
                    prev_kp_ev = ctx_mm
                tcx = []
                for mc in range(2):
                    tcx.append(s.op("vector", lambda e, mc=mc: e.tensor_copy(
                        A.ctx[:, 65 * mc:65 * mc + 65], ctxb[mc][:, 0:65]),
                        wait=[ctx_mm], sig="V"))
                    bank_free[("cx", mc)] = tcx[-1]
                ctx_done = tcx[-1]
                prev_q = None
                prev_qp = None
                for tt in range(NTT):
                    bi = tt % 2
                    bank = sbk[bi]
                    w = [t_sq, q_done]
                    if bank_free.get(("s", bi)) is not None:
                        w.append(bank_free[("s", bi)])
                    t_dd = s.op("tensor", lambda e, r0=r0, tt=tt, bank=bank: e.matmul(
                        bank[:, 0:256],
                        A.qT[r0:r0 + 64, 128 * tt:128 * tt + 128],
                        A.ptsb[r0:r0 + 64, :], start=True, stop=True), wait=w, sig="T")
                    wdg = [t_sq, negdm_done]
                    if prev_q is not None:
                        wdg.append(prev_q)
                    t_dg = s.op("tensor", lambda e, r0=r0, tt=tt: e.matmul(
                        dgb[:, 0:1], A.sqc[r0:r0 + 64, 128 * tt:128 * tt + 128],
                        A.onescol[r0:r0 + 64, 0:1],
                        start=True, stop=True), wait=wdg, sig="T")
                    t_m = s.op("vector", lambda e, bank=bank: e.reduce_max(
                        A.gmax[:, 3:4], bank[:, 0:256], axis=AX), wait=[t_dd], sig="V")
                    t_d5 = s.op("scalar", lambda e: e.mul(
                        A.gmax[:, 4:5], dgb[:, 0:1], 0.5), wait=[t_dg], sig="SC")
                    t_b = s.op("vector", lambda e: e.tensor_add(
                        A.gmax[:, 5:6], A.gmax[:, 3:4], A.gmax[:, 4:5]),
                        wait=[t_m, t_d5], sig="V")
                    t_bn = s.op("vector", lambda e: e.tensor_scalar_mul(
                        A.gmax[:, 6:7], A.gmax[:, 5:6], -1.0), wait=[t_b], sig="V")
                    wex = [t_bn]
                    if prev_qp is not None:
                        wex.append(prev_qp)
                    t_ex = s.op("scalar", lambda e, bank=bank: e.activation(
                        out=A.expsb[:, 0:256], in_=bank[:, 0:256], func=AF.Exp,
                        bias=A.gmax[:, 6:7], scale=1.0), wait=wex, sig="SC")
                    bank_free[("s", bi)] = t_ex
                    t_qp = s.op("vector", lambda e: e.tensor_scalar(
                        A.qp[:], A.expsb[:, 0:256], RATIO, RATIO * EPS,
                        ALU.mult, ALU.add), wait=[t_ex], sig="V")
                    prev_qp = t_qp
                    w = [t_qp, P.setup_tok]
                    if bank_free.get(("tb", 0)) is not None:
                        w.append(bank_free[("tb", 0)])
                    tok = None
                    for mc in range(2):
                        tok = s.op("tensor", lambda e, mc=mc: e.transpose(
                            tbk[0][:, 128 * mc:128 * mc + 128],
                            A.qp[:, 128 * mc:128 * mc + 128], P.ident[:]),
                            wait=w, sig="T")
                        w = []
                    t_qpT = s.op("vector", lambda e: e.tensor_copy(
                        A.qpT[:], tbk[0][:, 0:256]), wait=[tok], sig="V")
                    bank_free[("tb", 0)] = t_qpT
                    t_om = None
                    for mc in range(2):
                        w = [t_qpT, ctx_done]
                        if mc == 0 and bank_free.get(("ob",)) is not None:
                            w.append(bank_free[("ob",)])
                        t_om = s.op("tensor", lambda e, mc=mc: e.matmul(
                            outb[:, 0:65], A.qpT[:, 128 * mc:128 * mc + 128],
                            A.ctx[:, 65 * mc:65 * mc + 65],
                            start=(mc == 0), stop=(mc == 1)), wait=w, sig="T")
                    t_rc = s.op("vector", lambda e: e.reciprocal(
                        A.gmax[:, 7:8], outb[:, 64:65]), wait=[t_om], sig="V")
                    t_oev = s.op("vector", lambda e: e.tensor_scalar_mul(
                        A.osb[:, 0:64], outb[:, 0:64], A.gmax[:, 7:8]),
                        wait=[t_rc], sig="V")
                    bank_free[("ob",)] = t_oev
                    prev_q = t_oev
                    w = [t_oev, P.setup_tok]
                    if bank_free.get(("tb", 1)) is not None:
                        w.append(bank_free[("tb", 1)])
                    t_otr = s.op("tensor", lambda e: e.transpose(
                        tbk[1][0:64, 0:128], A.osb[:, 0:64], P.ident[:]),
                        wait=w, sig="T")
                    t_ofin = s.op("vector", lambda e, r0=r0, oc=oc, tt=tt: e.tensor_copy(
                        A.oT[r0:r0 + 64, T * oc + 128 * tt: T * oc + 128 * tt + 128],
                        tbk[1][0:64, 0:128]), wait=[t_otr], sig="V")
                    bank_free[("tb", 1)] = t_ofin
            attn_T_done[oc] = s.tok("T")
    attn_done = s.toks("T", "V", "SC")
    _emit_out_proj_streamed(s, nc, es, P, A, prol, P.W["b0_wo"], attn_done,
                            A.borow, t_bo, "b0")


def emit_ffn_sub(s, nc, es, P, kind):
    A = NS()
    A.gw = es.enter_context(nc.sbuf_tensor("a_gw", [128, NTT * 8], F32))
    with ExitStack() as es1:
        A.scr = es1.enter_context(nc.sbuf_tensor("a_scr", [128, D], F32))
        A.xn = es1.enter_context(nc.sbuf_tensor("a_xn", [128, 2 * D], BF16))
        with ExitStack() as esp:
            A.psumb = [esp.enter_context(nc.psum_tensor(f"a_psb{i}", [128, 1024], BF16))
                       for i in range(2)]
            prol = emit_prologue(s, P, A)
        A.psumf = [es.enter_context(nc.psum_tensor(f"a_psf{i}", [128, 512], F32))
                   for i in range(8)]
        A.gws = es1.enter_context(nc.sbuf_tensor("a_gws", [128, 64], BF16))
        A.gst = es1.enter_context(nc.sbuf_tensor("a_gst", [128, NTT * 8], F32))
        A.glog = es1.enter_context(nc.sbuf_tensor("a_glog", [128, NTT * 8], F32))
        A.gtop = es1.enter_context(nc.sbuf_tensor("a_gtop", [128, NTT * 8], F32))
        A.gexp = es1.enter_context(nc.sbuf_tensor("a_gexp", [128, NTT * 8], F32))
        A.goh2 = es1.enter_context(nc.sbuf_tensor("a_goh2", [128, 8], F32))
        gw_name = "f0_gw" if kind == "ffn0" else "f1_gw"
        if _DEBUG_STAGE == "prologue":
            P.dbgA = A
            return
        gate_done, gbank = emit_moe_gating(s, P, A, prol, gw_name,
                                           topk=1 if kind == "ffn0" else 2)
    gsafe = s.toks("T", "V", "SC")
    P.dbgA = A
    if _DEBUG_STAGE in ("prologue", "gating"):
        return
    if kind == "ffn0":
        A.b1 = es.enter_context(nc.sbuf_tensor("a_b1", [128, 256], F32))
        A.b2row = es.enter_context(nc.sbuf_tensor("a_b2row", [1, D], BF16))
        A.w1sb = es.enter_context(nc.sbuf_tensor("a_w1sb", [128, NDC * 4096], BF16))
        A.w2sb = [es.enter_context(nc.sbuf_tensor(f"a_w2sb{i}", [128, 2 * D], BF16))
                  for i in range(2)]
        A.hT = es.enter_context(nc.sbuf_tensor("a_hT", [128, 32 * 512], BF16))
        emit_ffn0(s, P, A, prol, gate_done, gbank, gsafe)
    else:
        A.bg = es.enter_context(nc.sbuf_tensor("a_bg", [128, 128], F32))
        A.bu = es.enter_context(nc.sbuf_tensor("a_bu", [128, 128], F32))
        A.borow = es.enter_context(nc.sbuf_tensor("a_borow", [1, D], BF16))
        A.wgsb = es.enter_context(nc.sbuf_tensor("a_wgsb", [128, NDC * 2048], BF16))
        A.wusb = es.enter_context(nc.sbuf_tensor("a_wusb", [128, NDC * 2048], BF16))
        A.wosb = [es.enter_context(nc.sbuf_tensor(f"a_wosb{i}", [128, 2 * D], BF16))
                  for i in range(2)]
        A.gscr = es.enter_context(nc.sbuf_tensor("a_gscr", [128, 1024], BF16))
        A.hT = es.enter_context(nc.sbuf_tensor("a_hT1", [128, 16 * 512], BF16))
        emit_ffn1(s, P, A, prol, gate_done, gbank, gsafe)


WEIGHT_SPECS = {
    "f0_gw": ([128, 64], ("ffn0a", "ffn0b")),
    "f0_b1": ([128, 256], ("ffn0a", "ffn0b")),
    "f0_b2": ([1, 8 * D], ("ffn0a", "ffn0b")),
    "f0_w1": ([8, 1024, 4096], ("ffn0a", "ffn0b")),
    "f0_w2": ([8, 4096, 1024], ("ffn0a", "ffn0b")),
    "f1_gw": ([128, 64], ("ffn1a", "ffn1b")),
    "f1_bg": ([128, 128], ("ffn1a", "ffn1b")),
    "f1_bu": ([128, 128], ("ffn1a", "ffn1b")),
    "f1_bo": ([1, 8 * D], ("ffn1a", "ffn1b")),
    "f1_wg": ([8, 1024, 2048], ("ffn1a", "ffn1b")),
    "f1_wu": ([8, 1024, 2048], ("ffn1a", "ffn1b")),
    "f1_wo": ([8, 2048, 1024], ("ffn1a", "ffn1b")),
    "a0_wq": ([1024, 1024], ("attn_a0",)),
    "a0_wk": ([1024, 1024], ("attn_a0",)),
    "a0_wv": ([1024, 1024], ("attn_a0",)),
    "a0_wo": ([1024, 1024], ("attn_a0",)),
    "a1_wq": ([1024, 1024], ("attn_a1",)),
    "a1_wk": ([1024, 1024], ("attn_a1",)),
    "a1_wv": ([1024, 1024], ("attn_a1",)),
    "a1_wo": ([1024, 1024], ("attn_a1",)),
    "a1_qb": ([128, 8], ("attn_a1",)),
    "a1_kb": ([128, 8], ("attn_a1",)),
    "a1_vb": ([128, 8], ("attn_a1",)),
    "a1_pb": ([1, D], ("attn_a1",)),
    "a1_cos": ([128, T], ("attn_a1",)),
    "a1_sin": ([128, T], ("attn_a1",)),
    "b1_wdq": ([1024, 512], ("attn_b1",)),
    "b1_wdkv": ([1024, 512], ("attn_b1",)),
    "b1_wuq": ([512, 1024], ("attn_b1",)),
    "b1_wqr": ([512, 1024], ("attn_b1",)),
    "b1_wuk": ([512, 1024], ("attn_b1",)),
    "b1_wkr": ([1024, 1024], ("attn_b1",)),
    "b1_wuv": ([512, 1024], ("attn_b1",)),
    "b1_wo": ([1024, 1024], ("attn_b1",)),
    "b1_dqb": ([128, 4], ("attn_b1",)),
    "b1_dkvb": ([128, 4], ("attn_b1",)),
    "b1_ub": ([128, 32], ("attn_b1",)),
    "b1_krb": ([128, 8], ("attn_b1",)),
    "b1_bo": ([1, D], ("attn_b1",)),
    "b1_cos": ([8, 128, T], ("attn_b1",)),
    "b1_sin": ([8, 128, T], ("attn_b1",)),
    "b0_wq": ([1024, 1024], ("attn_b0",)),
    "b0_wk": ([1024, 1024], ("attn_b0",)),
    "b0_wv": ([1024, 1024], ("attn_b0",)),
    "b0_wo": ([1024, 1024], ("attn_b0",)),
    "b0_bo": ([1, D], ("attn_b0",)),
    "b0_pt": ([128, 256], ("attn_b0",)),
}


def build_program(sub_ids):
    nc = bass.Bass("TRN2", target_bir_lowering=False)
    xin = nc.dram_tensor("x_in", [T, D], F32, kind="ExternalInput").ap()
    xout = nc.dram_tensor("x_out", [T, D], F32, kind="ExternalOutput").ap()
    needed = [n for n, (shp, subs) in WEIGHT_SPECS.items()
              if any(sid in subs for sid in sub_ids)]
    P = NS()
    P.W = {n: nc.dram_tensor(n, WEIGHT_SPECS[n][0], F32, kind="ExternalInput").ap()
           for n in needed}
    s = Sched()
    es_p = ExitStack()
    P.x_sb = es_p.enter_context(nc.sbuf_tensor("x_sb", [128, NTT * D], F32))
    P.xnT = es_p.enter_context(nc.sbuf_tensor("xnT", [128, NDC * T], BF16))
    P.ident = es_p.enter_context(nc.sbuf_tensor("ident", [128, 128], BF16))
    P.ones_row = es_p.enter_context(nc.sbuf_tensor("ones_row", [1, 128], BF16))
    P.eps_t = es_p.enter_context(nc.sbuf_tensor("eps_t", [128, 1], F32))
    P.stat = es_p.enter_context(nc.sbuf_tensor("stat", [128, NTT * 8], F32))
    emit_setup(s, P, xin)
    ncp = NCProxy(nc)
    for sid in sub_ids:
        with ExitStack() as es:
            if sid in ("ffn0a", "ffn0b"):
                emit_ffn_sub(s, ncp, es, P, "ffn0")
            elif sid in ("ffn1a", "ffn1b"):
                emit_ffn_sub(s, ncp, es, P, "ffn1")
            elif sid == "attn_a0":
                emit_attn_a0(s, ncp, es, P)
            elif sid == "attn_a1":
                emit_attn_a1(s, ncp, es, P)
            elif sid == "attn_b1":
                emit_attn_b1(s, ncp, es, P)
            elif sid == "attn_b0":
                emit_attn_b0(s, ncp, es, P)
            else:
                raise KeyError(sid)
            s.fence()
    s.op("sync", lambda e: e.dma_start(
        out=xout.rearrange("(tt p) d -> p tt d", p=128),
        in_=P.x_sb.ap().rearrange("p (tt d) -> p tt d", tt=NTT)), sig="d_out", n=16)
    if _DEBUG_DUMP:
        dumps = {"d_xnT": (P.xnT.ap(), BF16)}
        dumps.update(getattr(P, "dbg", {}))
        A = getattr(P, "dbgA", None)
        if A is not None and hasattr(A, "gw") and _DEBUG_STAGE != "prologue":
            dumps["d_gw"] = (A.gw.ap(), F32)
        if A is not None and hasattr(A, "hT") and _DEBUG_STAGE == "full":
            dumps["d_hT"] = (A.hT.ap(), BF16)
            dumps["d_w1"] = (A.w1sb.ap(), BF16)
        for nm, (ap, dt) in dumps.items():
            od = nc.dram_tensor(nm, list(ap.shape), dt, kind="ExternalOutput").ap()
            s.op("sync", lambda e, od=od, ap=ap: e.dma_start(out=od, in_=ap),
                 sig="d_out", n=16)
    final = s.tok("d_out")
    s.op("gpsimd", lambda e: e.engine_nop(), wait=[final])
    replay(nc, s, es_p)
    P._es = es_p
    return nc, needed


def _prep_weights(inp):
    W = {}
    W["f0_gw"] = np.ascontiguousarray(
        inp["L0_ffn_gate_w"].reshape(8, 128, 8).transpose(1, 0, 2).reshape(128, 64))
    W["f0_b1"] = np.ascontiguousarray(
        inp["L0_ffn_b1"].reshape(8, 32, 128).transpose(2, 0, 1).reshape(128, 256))
    W["f0_b2"] = np.ascontiguousarray(inp["L0_ffn_b2"].reshape(1, 8 * D))
    W["f0_w1"] = np.ascontiguousarray(inp["L0_ffn_w1"])
    W["f0_w2"] = np.ascontiguousarray(inp["L0_ffn_w2"])
    W["f1_gw"] = np.ascontiguousarray(
        inp["L1_ffn_W_router"].reshape(8, 128, 8).transpose(1, 0, 2).reshape(128, 64))
    W["f1_bg"] = np.ascontiguousarray(
        inp["L1_ffn_b_gate"].reshape(8, 16, 128).transpose(2, 0, 1).reshape(128, 128))
    W["f1_bu"] = np.ascontiguousarray(
        inp["L1_ffn_b_up"].reshape(8, 16, 128).transpose(2, 0, 1).reshape(128, 128))
    W["f1_bo"] = np.ascontiguousarray(inp["L1_ffn_b_out"].reshape(1, 8 * D))
    W["f1_wg"] = np.ascontiguousarray(inp["L1_ffn_W_gate"])
    W["f1_wu"] = np.ascontiguousarray(inp["L1_ffn_W_up"])
    W["f1_wo"] = np.ascontiguousarray(inp["L1_ffn_W_out"])
    W["a0_wq"] = np.ascontiguousarray(np.asarray(inp["L0_attn_a_Wq"]).T)
    W["a0_wk"] = np.ascontiguousarray(np.asarray(inp["L0_attn_a_Wk"]).T)
    W["a0_wv"] = np.ascontiguousarray(np.asarray(inp["L0_attn_a_Wv"]).T)
    W["a0_wo"] = np.ascontiguousarray(np.asarray(inp["L0_attn_a_Wo"]).T)
    qkv_w = np.asarray(inp["L1_attn_a_qkv_w"])
    qkv_b = np.asarray(inp["L1_attn_a_qkv_b"])
    W["a1_wq"] = np.ascontiguousarray(qkv_w[0:1024].T)
    W["a1_wk"] = np.ascontiguousarray(qkv_w[1024:2048].T)
    W["a1_wv"] = np.ascontiguousarray(qkv_w[2048:3072].T)
    W["a1_wo"] = np.ascontiguousarray(np.asarray(inp["L1_attn_a_proj_w"]).T)
    W["a1_qb"] = np.ascontiguousarray(qkv_b[0:1024].reshape(8, 128).T)
    W["a1_kb"] = np.ascontiguousarray(qkv_b[1024:2048].reshape(8, 128).T)
    W["a1_vb"] = np.ascontiguousarray(qkv_b[2048:3072].reshape(8, 128).T)
    W["a1_pb"] = np.ascontiguousarray(np.asarray(inp["L1_attn_a_proj_b"]).reshape(1, D))
    invf = (np.float32(1.0) /
            np.power(np.float32(10000.0),
                     np.arange(0, 64, 2, dtype=np.float32) / np.float32(64)))
    tpos = np.arange(T, dtype=np.float32)
    fr = tpos[:, None] * invf[None, :]
    cos32 = np.cos(fr).astype(np.float32)
    sin32 = np.sin(fr).astype(np.float32)
    ctab = np.zeros((128, T), np.float32)
    stab = np.zeros((128, T), np.float32)
    for r in range(128):
        j = r % 64
        ctab[r] = cos32[:, j % 32]
        stab[r] = (-sin32[:, j]) if j < 32 else sin32[:, j - 32]
    W["a1_cos"] = ctab
    W["a1_sin"] = stab
    g = lambda n: np.asarray(inp["L1_attn_b_" + n])
    perm = np.zeros(1024, np.int64)
    for h in range(16):
        for i in range(32):
            perm[64 * h + i] = 64 * h + 2 * i
            perm[64 * h + 32 + i] = 64 * h + 2 * i + 1
    W["b1_wdq"] = np.ascontiguousarray(g("Wd_q_w"))
    W["b1_wdkv"] = np.ascontiguousarray(g("Wd_kv_w"))
    W["b1_wuq"] = np.ascontiguousarray(g("Wu_q_w")[:, perm])
    W["b1_wqr"] = np.ascontiguousarray(g("Wq_r_w")[:, perm])
    W["b1_wuk"] = np.ascontiguousarray(g("Wu_k_w")[:, perm])
    W["b1_wkr"] = np.ascontiguousarray(g("Wk_r_w")[:, perm])
    W["b1_wuv"] = np.ascontiguousarray(g("Wu_v_w"))
    W["b1_wo"] = np.ascontiguousarray(g("Wo_w"))
    W["b1_dqb"] = np.ascontiguousarray(g("Wd_q_b").reshape(4, 128).T)
    W["b1_dkvb"] = np.ascontiguousarray(g("Wd_kv_b").reshape(4, 128).T)
    ub = np.zeros((128, 32), np.float32)
    ub[:, 0:8] = g("Wu_q_b")[perm].reshape(8, 128).T
    ub[:, 8:16] = g("Wq_r_b")[perm].reshape(8, 128).T
    ub[:, 16:24] = g("Wu_k_b")[perm].reshape(8, 128).T
    ub[:, 24:32] = g("Wu_v_b").reshape(8, 128).T
    W["b1_ub"] = ub
    W["b1_krb"] = np.ascontiguousarray(g("Wk_r_b")[perm].reshape(8, 128).T)
    W["b1_bo"] = np.ascontiguousarray(g("Wo_b").reshape(1, D))
    ids = np.arange(512, dtype=np.float32)
    theta = np.power(np.float32(10000.0), np.float32(-2.0) * ids) / np.float32(512.0)
    tpos = np.arange(T, dtype=np.float32)
    rarg = tpos[:, None] * theta[None, :]
    cosf = np.cos(rarg).astype(np.float32)
    sinf = np.sin(rarg).astype(np.float32)
    bc = np.zeros((8, 128, T), np.float32)
    bs = np.zeros((8, 128, T), np.float32)
    for oc in range(8):
        for r in range(128):
            h = 2 * oc + (1 if r >= 64 else 0)
            j = r % 64
            if j < 32:
                i = 32 * h + j
                bc[oc, r] = cosf[:, i]
                bs[oc, r] = -sinf[:, i]
            else:
                i = 32 * h + (j - 32)
                bc[oc, r] = cosf[:, i]
                bs[oc, r] = sinf[:, i]
    W["b1_cos"] = bc
    W["b1_sin"] = bs
    W["b0_wq"] = np.ascontiguousarray(inp["L0_attn_b_Wq"])
    W["b0_wk"] = np.ascontiguousarray(inp["L0_attn_b_Wk"])
    W["b0_wv"] = np.ascontiguousarray(inp["L0_attn_b_Wv"])
    W["b0_wo"] = np.ascontiguousarray(inp["L0_attn_b_Wo"])
    W["b0_bo"] = np.ascontiguousarray(np.asarray(inp["L0_attn_b_bo"]).reshape(1, D))
    pt = np.asarray(inp["L0_attn_b_P"]).T
    W["b0_pt"] = np.ascontiguousarray(np.vstack([pt, pt]))
    return {k: np.asarray(v, dtype=np.float32) for k, v in W.items()}


LAUNCH_PLAN = [["attn_a0", "ffn0a", "attn_b0", "ffn0b", "attn_a1", "ffn1a", "attn_b1", "ffn1b"]]
_DEBUG_STAGE = "full"
_DEBUG_DUMP = False
_DEBUG_NEXPERTS = 8
_DEBUG_NTB = 4
_DEBUG_NOGELU = False


def run_subs(plan, x_state, W):
    for group in plan:
        nc, needed = build_program(group)
        in_maps = []
        for c in range(8):
            m = {"x_in": np.ascontiguousarray(x_state[c])}
            for n in needed:
                m[n] = W[n]
            in_maps.append(m)
        res = run_bass_kernel_spmd(nc, in_maps, core_ids=list(range(8)))
        x_state = [np.asarray(res.results[c]["x_out"]) for c in range(8)]
    return x_state


def kernel(**inputs):
    W = _prep_weights(inputs)
    x = np.asarray(inputs["L0_attn_a_x"], dtype=np.float32)
    state = [np.ascontiguousarray(x[c]) for c in range(8)]
    state = run_subs(LAUNCH_PLAN, state, W)
    return np.stack(state, axis=0).astype(np.float32)


# version 04
# speedup vs baseline: 1.0067x; 1.0067x over previous
import numpy as np
from contextlib import ExitStack

import concourse.bass as bass
import concourse.mybir as mybir
from concourse.bass_utils import run_bass_kernel_spmd

F32 = mybir.dt.float32
BF16 = mybir.dt.bfloat16
AF = mybir.ActivationFunctionType
ALU = mybir.AluOpType
AX = mybir.AxisListType.X

T = 2048
D = 1024
NTT = 16
NDC = 8

ENGINES = ("sync", "tensor", "vector", "scalar", "gpsimd")
OWN_SEM = {"tensor": "T", "vector": "V", "scalar": "SC", "gpsimd": "P"}


class Sched:
    def __init__(self):
        self.ops = []
        from collections import defaultdict
        self.cnt = defaultdict(int)

    def op(self, eng, fn, wait=(), sig=None, n=1, note=""):
        tok = None
        w = tuple((sm, v) for (sm, v) in wait if v > 0)
        if sig is not None:
            self.cnt[sig] += n
            tok = (sig, self.cnt[sig])
        self.ops.append((eng, w, fn, (sig, n) if sig else None, note))
        return tok

    def tok(self, sem):
        return (sem, self.cnt[sem])

    def toks(self, *sems):
        return [self.tok(sm) for sm in sems]

    def fence(self):
        self.ops.append(("FENCE", dict(self.cnt), None, None, ""))


def replay(nc, s, es_p):
    names = []
    for item in s.ops:
        if item[0] == "FENCE":
            continue
        _, waits, _, sig, _ = item
        for sm, v in waits:
            if sm not in names:
                names.append(sm)
        if sig is not None and sig[0] not in names:
            names.append(sig[0])
    sems = {sm: es_p.enter_context(nc.semaphore(sm)) for sm in names}
    per_eng = {e: [] for e in ENGINES}
    pending_fence = {e: None for e in ENGINES}
    for item in s.ops:
        if item[0] == "FENCE":
            for e in ENGINES:
                pending_fence[e] = dict(item[1])
            continue
        eng = item[0]
        per_eng[eng].append((pending_fence[eng], item))
        pending_fence[eng] = None
    with nc.Block() as block:
        for ename in ENGINES:
            items = per_eng[ename]

            def body(e, ename=ename, items=items):
                waited = {}
                own = OWN_SEM.get(ename)
                for extra, (eng, waits, fn, sig, note) in items:
                    allw = []
                    if extra is not None:
                        allw.extend((sm, v) for sm, v in extra.items() if sm != own)
                    allw.extend(waits)
                    for sm, v in allw:
                        if v > 0 and waited.get(sm, 0) < v:
                            e.wait_ge(sems[sm], v)
                            waited[sm] = v
                    inst = fn(e)
                    if sig is not None:
                        inst.then_inc(sems[sig[0]], sig[1])
            getattr(block, ename)(body)


class NS:
    def __init__(self, **kw):
        self.__dict__.update(kw)


class NCProxy:
    def __init__(self, nc):
        self._nc = nc
        self._n = {}

    def _uniq(self, name):
        k = self._n.get(name, 0)
        self._n[name] = k + 1
        return name if k == 0 else f"{name}_u{k}"

    def sbuf_tensor(self, name, shape, dtype, **kw):
        return self._nc.sbuf_tensor(self._uniq(name), shape, dtype, **kw)

    def psum_tensor(self, name, shape, dtype, **kw):
        return self._nc.psum_tensor(self._uniq(name), shape, dtype, **kw)

    def dram_tensor(self, name, shape, dtype, **kw):
        return self._nc.dram_tensor(self._uniq(name), shape, dtype, **kw)

    def __getattr__(self, a):
        return getattr(self._nc, a)


def x_slice(P, tt, c0=0, n=D):
    return P.x_sb[:, D * tt + c0: D * tt + c0 + n]


def xnT_slice(P, c, t0, n):
    return P.xnT[:, T * c + t0: T * c + t0 + n]


def emit_setup(s, P, xin_ap):
    P.xload = s.op("sync", lambda e: e.dma_start(
        out=P.x_sb.ap().rearrange("p (tt d) -> p tt d", tt=NTT),
        in_=xin_ap.rearrange("(tt p) d -> p tt d", p=128)), sig="d_x", n=16)
    t_ms = s.op("gpsimd", lambda e: e.memset(P.ident[:], 1.0), sig="P")
    s.op("gpsimd", lambda e: e.affine_select(
        out=P.ident[:], in_=P.ident[:], pattern=[[-1, 128]],
        compare_op=ALU.is_equal, fill=0.0, base=0, channel_multiplier=1),
        wait=[t_ms])
    s.op("gpsimd", lambda e: e.memset(P.ones_row[:], 1.0))
    P.setup_tok = s.op("gpsimd", lambda e: e.memset(P.eps_t[:], 1e-5), sig="P")


def emit_prologue(s, P, A):
    xload = P.xload
    setup = P.setup_tok
    st = P.stat
    xnT3 = P.xnT.ap().rearrange("p (c t) -> p c t", c=NDC)
    ev_tok = {}
    last_bank_ev = [None, None]
    stats = {}
    for tt in range(NTT):
        xt = x_slice(P, tt)
        c = 8 * tt
        slot = tt % 2
        t_sum = s.op("vector", lambda e, xt=xt, c=c: e.reduce_sum(
            st[:, c:c + 1], xt, axis=AX), wait=[xload], sig="V")
        t_ssq = s.op("scalar", lambda e, xt=xt, c=c: e.activation(
            out=A.scr[:, 0:D], in_=xt, func=AF.Square, bias=0.0, scale=1.0,
            accum_out=st[:, c + 1:c + 2]), wait=[xload], sig="SC")
        t_mean = s.op("vector", lambda e, c=c: e.tensor_scalar_mul(
            st[:, c + 2:c + 3], st[:, c:c + 1], 1.0 / D), wait=[t_sum], sig="V")
        t_msq = s.op("vector", lambda e, c=c: e.tensor_mul(
            st[:, c + 6:c + 7], st[:, c + 2:c + 3], st[:, c + 2:c + 3]),
            wait=[t_mean], sig="V")
        t_var = s.op("vector", lambda e, c=c: e.scalar_tensor_tensor(
            st[:, c + 3:c + 4], st[:, c + 1:c + 2], 1.0 / D, st[:, c + 6:c + 7],
            ALU.mult, ALU.subtract), wait=[t_msq, t_ssq], sig="V")
        t_ln = s.op("scalar", lambda e, c=c: e.activation(
            out=st[:, c + 4:c + 5], in_=st[:, c + 3:c + 4], func=AF.Ln,
            bias=P.eps_t[:, 0:1], scale=1.0), wait=[t_var, setup], sig="SC")
        t_rstd = s.op("scalar", lambda e, c=c: e.activation(
            out=st[:, c + 5:c + 6], in_=st[:, c + 4:c + 5], func=AF.Exp,
            bias=0.0, scale=-0.5), wait=[t_ln], sig="SC")
        wapp = [t_rstd]
        if tt >= 2:
            wapp.append(ev_tok[tt - 2][0])
        t_xn = s.op("vector", lambda e, xt=xt, slot=slot, c=c: e.tensor_scalar(
            A.xn[:, D * slot:D * slot + D], xt, st[:, c + 2:c + 3],
            st[:, c + 5:c + 6], ALU.subtract, ALU.mult), wait=wapp, sig="V")
        bank = A.psumb[slot]
        tok_T = None
        for cc in range(NDC):
            w = [t_xn, setup]
            if last_bank_ev[slot] is not None:
                w.append(last_bank_ev[slot])
            tok_T = s.op("tensor", lambda e, slot=slot, cc=cc, bank=bank: e.transpose(
                bank[:, 128 * cc:128 * cc + 128],
                A.xn[:, D * slot + 128 * cc: D * slot + 128 * cc + 128],
                P.ident[:]), wait=w, sig="T")
        t_ev = s.op("vector", lambda e, tt=tt, bank=bank: e.tensor_copy(
            xnT3[:, :, 128 * tt:128 * tt + 128],
            bank.ap().rearrange("p (c t) -> p c t", c=NDC)), wait=[tok_T], sig="V")
        last_bank_ev[slot] = t_ev
        ev_tok[tt] = (tok_T, t_ev)
        stats[tt] = (t_xn, t_ssq)
    return NS(done_V=s.tok("V"), done_SC=s.tok("SC"), done_T=s.tok("T"),
              stats_done=stats)


CONVERT_BF16 = ("f0_w1", "f0_w2", "f1_wg", "f1_wu", "f1_wo")


def emit_bf16_convert(s, nc, P, names):
    P.WB = {}
    with ExitStack() as esc:
        stg = [esc.enter_context(nc.sbuf_tensor(f"cv_stg{i}", [128, 4096], BF16))
               for i in range(2)]
        st_tok = [None, None]
        for nm in names:
            shp = list(P.W[nm].shape)
            total = int(np.prod(shp))
            wb = nc.dram_tensor(nm + "_bf", shp, BF16)
            P.WB[nm] = wb.ap()
            n = total // (128 * 4096)
            flat_in = P.W[nm].rearrange(
                " ".join(f"s{i}" for i in range(len(shp))) + " -> (" +
                " ".join(f"s{i}" for i in range(len(shp))) + ")").rearrange(
                "(n p f) -> n p f", p=128, f=4096)
            flat_out = wb.ap().rearrange(
                " ".join(f"s{i}" for i in range(len(shp))) + " -> (" +
                " ".join(f"s{i}" for i in range(len(shp))) + ")").rearrange(
                "(n p f) -> n p f", p=128, f=4096)
            for i in range(n):
                par = i % 2
                w = [st_tok[par]] if st_tok[par] is not None else []
                t_in = s.op("gpsimd", lambda e, nm=nm, i=i, par=par, flat_in=flat_in: e.dma_start(
                    out=stg[par][:], in_=flat_in[i]), wait=w, sig=f"d_cvi{par}", n=16)
                st_tok[par] = s.op("sync", lambda e, nm=nm, i=i, par=par, flat_out=flat_out: e.dma_start(
                    out=flat_out[i], in_=stg[par][:]), wait=[t_in], sig=f"d_cvo{par}", n=16)


def emit_moe_gating(s, P, A, prol, gw_name, topk):
    t_load = s.op("gpsimd", lambda e: e.dma_start(
        out=A.gws[:], in_=P.W[gw_name]), sig="d_gw", n=16)
    st = A.gst
    banktok = [None, None]
    for tt in range(NTT):
        par = tt % 2
        bank = A.psumf[par]
        w = [t_load, prol.done_V]
        if banktok[par] is not None:
            w.append(banktok[par])
        tok = None
        for c in range(NDC):
            tok = s.op("tensor", lambda e, c=c, tt=tt, bank=bank: e.matmul(
                bank[:, 0:8], xnT_slice(P, c, 128 * tt, 128), A.gws[:, 8 * c:8 * c + 8],
                start=(c == 0), stop=(c == NDC - 1)), wait=w, sig="T")
            w = []
        c0 = 8 * tt
        t_lg = s.op("vector", lambda e, tt=tt, bank=bank: e.tensor_copy(
            A.glog[:, 8 * tt:8 * tt + 8], bank[:, 0:8]), wait=[tok], sig="V")
        banktok[par] = t_lg
        t_max = s.op("vector", lambda e, tt=tt: e.max(
            A.gtop[:, 8 * tt:8 * tt + 8], A.glog[:, 8 * tt:8 * tt + 8]),
            wait=[t_lg], sig="V")
        t_neg = s.op("vector", lambda e, tt=tt, c0=c0: e.tensor_scalar_mul(
            st[:, c0 + 2:c0 + 3], A.gtop[:, 8 * tt:8 * tt + 1], -1.0),
            wait=[t_max], sig="V")
        t_exp = s.op("scalar", lambda e, tt=tt, c0=c0: e.activation(
            out=A.gexp[:, 8 * tt:8 * tt + 8], in_=A.glog[:, 8 * tt:8 * tt + 8],
            func=AF.Exp, bias=st[:, c0 + 2:c0 + 3], scale=1.0,
            accum_out=st[:, c0 + 3:c0 + 4]), wait=[t_neg], sig="SC")
        if topk == 1:
            t_r = s.op("vector", lambda e, c0=c0: e.reciprocal(
                st[:, c0 + 4:c0 + 5], st[:, c0 + 3:c0 + 4]), wait=[t_exp], sig="V")
            t_oh = s.op("vector", lambda e, tt=tt: e.tensor_scalar(
                A.gw[:, 8 * tt:8 * tt + 8], A.glog[:, 8 * tt:8 * tt + 8],
                A.gtop[:, 8 * tt:8 * tt + 1], None, ALU.is_equal),
                wait=[t_max], sig="V")
            s.op("vector", lambda e, tt=tt, c0=c0: e.tensor_scalar_mul(
                A.gw[:, 8 * tt:8 * tt + 8], A.gw[:, 8 * tt:8 * tt + 8],
                st[:, c0 + 4:c0 + 5]), wait=[t_r, t_oh], sig="V")
        else:
            t_r = s.op("vector", lambda e, c0=c0: e.reciprocal(
                st[:, c0 + 4:c0 + 5], st[:, c0 + 3:c0 + 4]), wait=[t_exp], sig="V")
            t_p = s.op("vector", lambda e, tt=tt, c0=c0: e.tensor_scalar_mul(
                A.gexp[:, 8 * tt:8 * tt + 8], A.gexp[:, 8 * tt:8 * tt + 8],
                st[:, c0 + 4:c0 + 5]), wait=[t_r], sig="V")
            t_m8 = s.op("vector", lambda e, tt=tt: e.max(
                A.gtop[:, 8 * tt:8 * tt + 8], A.gexp[:, 8 * tt:8 * tt + 8]),
                wait=[t_p], sig="V")
            t_d0 = s.op("vector", lambda e, tt=tt, c0=c0: e.tensor_add(
                st[:, c0 + 7:c0 + 8], A.gtop[:, 8 * tt:8 * tt + 1],
                A.gtop[:, 8 * tt + 1:8 * tt + 2]), wait=[t_m8], sig="V")
            t_d1 = s.op("vector", lambda e, c0=c0: e.tensor_scalar_add(
                st[:, c0 + 7:c0 + 8], st[:, c0 + 7:c0 + 8], 1e-8),
                wait=[t_d0], sig="V")
            t_rr = s.op("vector", lambda e, c0=c0: e.reciprocal(
                st[:, c0 + 4:c0 + 5], st[:, c0 + 7:c0 + 8]), wait=[t_d1], sig="V")
            t_a1 = s.op("vector", lambda e, tt=tt, c0=c0: e.tensor_mul(
                st[:, c0 + 5:c0 + 6], A.gtop[:, 8 * tt:8 * tt + 1],
                st[:, c0 + 4:c0 + 5]), wait=[t_rr], sig="V")
            t_a2 = s.op("vector", lambda e, tt=tt, c0=c0: e.tensor_mul(
                st[:, c0 + 6:c0 + 7], A.gtop[:, 8 * tt + 1:8 * tt + 2],
                st[:, c0 + 4:c0 + 5]), wait=[t_rr], sig="V")
            t_o1 = s.op("vector", lambda e, tt=tt, c0=c0: e.tensor_scalar(
                A.gw[:, 8 * tt:8 * tt + 8], A.gexp[:, 8 * tt:8 * tt + 8],
                A.gtop[:, 8 * tt:8 * tt + 1], st[:, c0 + 5:c0 + 6],
                ALU.is_equal, ALU.mult), wait=[t_a1], sig="V")
            t_o2 = s.op("vector", lambda e, tt=tt, c0=c0: e.tensor_scalar(
                A.goh2[:, 0:8], A.gexp[:, 8 * tt:8 * tt + 8],
                A.gtop[:, 8 * tt + 1:8 * tt + 2], st[:, c0 + 6:c0 + 7],
                ALU.is_equal, ALU.mult), wait=[t_a2, t_o1], sig="V")
            s.op("vector", lambda e, tt=tt: e.tensor_add(
                A.gw[:, 8 * tt:8 * tt + 8], A.gw[:, 8 * tt:8 * tt + 8],
                A.goh2[:, 0:8]), wait=[t_o2], sig="V")
    return s.tok("V"), banktok


def emit_ffn0(s, P, A, prol, gate_done, gbank, gsafe):
    t_b1 = s.op("sync", lambda e: e.dma_start(out=A.b1[:], in_=P.W["f0_b1"]),
                wait=gsafe, sig="d_f0b1", n=16)
    NFC, NQ = 32, 4
    NTB, TB = 4, 512
    NKS = 16
    w1v = P.WB["f0_w1"]
    w2v = P.WB["f0_w2"]
    w1sb = A.w1sb.ap().rearrange("p (c f) -> p c f", c=NDC)
    bank_free = {i: None for i in range(8)}
    bank_free[0] = gbank[0]
    bank_free[1] = gbank[1]
    last_phaseA_T = {q: None for q in range(NQ)}
    w2_last_T = [None, None]
    b2_last_T = None
    hT_last_T = None
    for e_i in range(_DEBUG_NEXPERTS):
        w1tok = []
        for q in range(NQ):
            w = list(gsafe) if e_i == 0 else []
            if last_phaseA_T[q] is not None:
                w.append(last_phaseA_T[q])
            w1tok.append(s.op("sync", lambda e, e_i=e_i, q=q: e.dma_start(
                out=w1sb[:, :, 1024 * q:1024 * (q + 1)],
                in_=w1v[e_i].rearrange("(c p) f -> p c f", p=128)[:, :, 1024 * q:1024 * (q + 1)]),
                wait=w, sig=f"d_f0w1_{q}", n=16))
        wb2 = [b2_last_T] if b2_last_T is not None else list(gsafe)
        t_b2 = s.op("gpsimd", lambda e, e_i=e_i: e.dma_start(
            out=A.b2row[:], in_=P.W["f0_b2"][:, D * e_i:D * (e_i + 1)]),
            wait=wb2, sig="d_f0b2", n=16)
        for tb in range(min(NTB, _DEBUG_NTB)):
            for fc in range(NFC):
                bank_i = fc % 8
                bank = A.psumf[bank_i]
                w = [w1tok[fc // 8], prol.done_V]
                if bank_free[bank_i] is not None:
                    w.append(bank_free[bank_i])
                tok = None
                for c in range(NDC):
                    tok = s.op("tensor", lambda e, fc=fc, c=c, tb=tb, bank=bank: e.matmul(
                        bank[:], w1sb[:, c, 128 * fc:128 * fc + 128],
                        xnT_slice(P, c, TB * tb, TB),
                        start=(c == 0), stop=(c == NDC - 1)), wait=w, sig="T")
                    w = []
                wh = [tok, t_b1]
                if hT_last_T is not None:
                    wh.append(hT_last_T)
                if _DEBUG_NOGELU:
                    t_ev = s.op("scalar", lambda e, fc=fc, bank=bank: e.activation(
                        out=A.hT[:, 512 * fc:512 * fc + 512], in_=bank[:],
                        func=AF.Copy, bias=0.0, scale=1.0), wait=wh, sig="SC")
                else:
                    t_ev = s.op("scalar", lambda e, e_i=e_i, fc=fc, bank=bank: e.activation(
                        out=A.hT[:, 512 * fc:512 * fc + 512], in_=bank[:],
                        func=AF.Gelu, bias=A.b1[:, 32 * e_i + fc:32 * e_i + fc + 1],
                        scale=1.0), wait=wh, sig="SC")
                bank_free[bank_i] = t_ev
            phaseA_done = s.tok("SC")
            first_mm_waits = [phaseA_done, gate_done]
            for i in range(8):
                if bank_free[i] is not None:
                    first_mm_waits.append(bank_free[i])
            mm_tok = None
            for ks in range(NKS):
                par = ks % 2
                w = [] if w2_last_T[par] is not None or e_i > 0 else list(gsafe)
                if w2_last_T[par] is not None:
                    w.append(w2_last_T[par])
                w2tok = s.op("sync", lambda e, e_i=e_i, ks=ks, par=par: e.dma_start(
                    out=A.w2sb[par].ap().rearrange("p (kc d) -> p kc d", kc=2),
                    in_=w2v[e_i].rearrange("(ks kc p) d -> ks p kc d", kc=2, p=128)[ks]),
                    wait=w, sig=f"d_f0w2_{par}", n=16)
                for kc in range(2):
                    fc = 2 * ks + kc
                    for tt in range(4):
                        for db in range(2):
                            bi = tt * 2 + db
                            w = [w2tok] + (first_mm_waits if fc == 0 else [])
                            mm_tok = s.op("tensor", lambda e, fc=fc, kc=kc, tt=tt, db=db, par=par, bi=bi: e.matmul(
                                A.psumf[bi][:],
                                A.hT[:, 512 * fc + 128 * tt: 512 * fc + 128 * tt + 128],
                                A.w2sb[par][:, 1024 * kc + 512 * db: 1024 * kc + 512 * db + 512],
                                start=(fc == 0), stop=False), wait=w, sig="T")
                w2_last_T[par] = mm_tok
            hT_last_T = mm_tok
            for tt in range(4):
                for db in range(2):
                    bi = tt * 2 + db
                    t_bias = s.op("tensor", lambda e, db=db, bi=bi: e.matmul(
                        A.psumf[bi][:], P.ones_row[0:1, 0:128],
                        A.b2row[0:1, 512 * db: 512 * db + 512],
                        start=False, stop=True),
                        wait=[P.setup_tok, t_b2], sig="T")
                    gtt = 4 * tb + tt
                    pv, psc = prol.stats_done[gtt]
                    t_ev = s.op("vector", lambda e, e_i=e_i, gtt=gtt, db=db, bi=bi: e.scalar_tensor_tensor(
                        x_slice(P, gtt, 512 * db, 512), A.psumf[bi][:],
                        A.gw[:, 8 * gtt + e_i: 8 * gtt + e_i + 1],
                        x_slice(P, gtt, 512 * db, 512),
                        ALU.mult, ALU.add), wait=[t_bias, pv, psc, gate_done], sig="V")
                    bank_free[bi] = t_ev
            b2_last_T = s.tok("T")
        for q in range(NQ):
            last_phaseA_T[q] = s.tok("T")


def emit_ffn1(s, P, A, prol, gate_done, gbank, gsafe):
    t_bg = s.op("sync", lambda e: e.dma_start(out=A.bg[:], in_=P.W["f1_bg"]),
                wait=gsafe, sig="d_f1bg", n=16)
    t_bu = s.op("sync", lambda e: e.dma_start(out=A.bu[:], in_=P.W["f1_bu"]),
                wait=gsafe, sig="d_f1bu", n=16)
    NMC = 16
    NTB, TB = 4, 512
    NKS = 8
    wgv, wuv, wov = P.WB["f1_wg"], P.WB["f1_wu"], P.WB["f1_wo"]
    wgsb = A.wgsb.ap().rearrange("p (c f) -> p c f", c=NDC)
    wusb = A.wusb.ap().rearrange("p (c f) -> p c f", c=NDC)
    bank_free = {i: None for i in range(8)}
    bank_free[0] = gbank[0]
    bank_free[1] = gbank[1]
    last_phaseA_T = {q: None for q in range(2)}
    wo_last_T = [None, None]
    bo_last_T = None
    hT_last_T = None
    gscr_tok = [None, None]
    for e_i in range(8):
        wgtok, wutok = [], []
        for q in range(2):
            w = list(gsafe) if e_i == 0 else []
            if last_phaseA_T[q] is not None:
                w.append(last_phaseA_T[q])
            wgtok.append(s.op("sync", lambda e, e_i=e_i, q=q: e.dma_start(
                out=wgsb[:, :, 1024 * q:1024 * (q + 1)],
                in_=wgv[e_i].rearrange("(c p) f -> p c f", p=128)[:, :, 1024 * q:1024 * (q + 1)]),
                wait=w, sig=f"d_f1wg_{q}", n=16))
            wutok.append(s.op("sync", lambda e, e_i=e_i, q=q: e.dma_start(
                out=wusb[:, :, 1024 * q:1024 * (q + 1)],
                in_=wuv[e_i].rearrange("(c p) f -> p c f", p=128)[:, :, 1024 * q:1024 * (q + 1)]),
                wait=w, sig=f"d_f1wu_{q}", n=16))
        wbo = [bo_last_T] if bo_last_T is not None else list(gsafe)
        t_bo = s.op("gpsimd", lambda e, e_i=e_i: e.dma_start(
            out=A.borow[:], in_=P.W["f1_bo"][:, D * e_i:D * (e_i + 1)]),
            wait=wbo, sig="d_f1bo", n=16)
        for tb in range(NTB):
            for mc in range(NMC):
                bg_i = 2 * (mc % 4)
                bu_i = bg_i + 1
                bank_g, bank_u = A.psumf[bg_i], A.psumf[bu_i]
                wq = mc // 8
                wg_ = [wgtok[wq], prol.done_V]
                if bank_free[bg_i] is not None:
                    wg_.append(bank_free[bg_i])
                tokg = None
                for c in range(NDC):
                    tokg = s.op("tensor", lambda e, mc=mc, c=c, tb=tb, bank=bank_g: e.matmul(
                        bank[:], wgsb[:, c, 128 * mc:128 * mc + 128],
                        xnT_slice(P, c, TB * tb, TB),
                        start=(c == 0), stop=(c == NDC - 1)), wait=wg_, sig="T")
                    wg_ = []
                wu_ = [wutok[wq]]
                if bank_free[bu_i] is not None:
                    wu_.append(bank_free[bu_i])
                toku = None
                for c in range(NDC):
                    toku = s.op("tensor", lambda e, mc=mc, c=c, tb=tb, bank=bank_u: e.matmul(
                        bank[:], wusb[:, c, 128 * mc:128 * mc + 128],
                        xnT_slice(P, c, TB * tb, TB),
                        start=(c == 0), stop=(c == NDC - 1)), wait=wu_, sig="T")
                    wu_ = []
                slot = mc % 2
                wsg = [tokg, t_bg]
                if gscr_tok[slot] is not None:
                    wsg.append(gscr_tok[slot])
                t_g = s.op("scalar", lambda e, e_i=e_i, mc=mc, slot=slot, bank=bank_g: e.activation(
                    out=A.gscr[:, 512 * slot:512 * slot + 512], in_=bank[:],
                    func=AF.Silu, bias=A.bg[:, 16 * e_i + mc:16 * e_i + mc + 1],
                    scale=1.0), wait=wsg, sig="SC")
                bank_free[bg_i] = t_g
                wh = [toku, t_g, t_bu]
                if hT_last_T is not None:
                    wh.append(hT_last_T)
                t_h = s.op("vector", lambda e, e_i=e_i, mc=mc, slot=slot, bank=bank_u: e.scalar_tensor_tensor(
                    A.hT[:, 512 * mc:512 * mc + 512], bank[:],
                    A.bu[:, 16 * e_i + mc:16 * e_i + mc + 1],
                    A.gscr[:, 512 * slot:512 * slot + 512],
                    ALU.add, ALU.mult), wait=wh, sig="V")
                bank_free[bu_i] = t_h
                gscr_tok[slot] = t_h
            phaseA_done = s.tok("V")
            first_mm_waits = [phaseA_done, gate_done]
            for i in range(8):
                if bank_free[i] is not None:
                    first_mm_waits.append(bank_free[i])
            mm_tok = None
            for ks in range(NKS):
                par = ks % 2
                w = [] if wo_last_T[par] is not None or e_i > 0 else list(gsafe)
                if wo_last_T[par] is not None:
                    w.append(wo_last_T[par])
                wotok = s.op("sync", lambda e, e_i=e_i, ks=ks, par=par: e.dma_start(
                    out=A.wosb[par].ap().rearrange("p (kc d) -> p kc d", kc=2),
                    in_=wov[e_i].rearrange("(ks kc p) d -> ks p kc d", kc=2, p=128)[ks]),
                    wait=w, sig=f"d_f1wo_{par}", n=16)
                for kc in range(2):
                    fc = 2 * ks + kc
                    for tt in range(4):
                        for db in range(2):
                            bi = tt * 2 + db
                            w = [wotok] + (first_mm_waits if fc == 0 else [])
                            mm_tok = s.op("tensor", lambda e, fc=fc, kc=kc, tt=tt, db=db, par=par, bi=bi: e.matmul(
                                A.psumf[bi][:],
                                A.hT[:, 512 * fc + 128 * tt: 512 * fc + 128 * tt + 128],
                                A.wosb[par][:, 1024 * kc + 512 * db: 1024 * kc + 512 * db + 512],
                                start=(fc == 0), stop=False), wait=w, sig="T")
                wo_last_T[par] = mm_tok
            hT_last_T = mm_tok
            for tt in range(4):
                for db in range(2):
                    bi = tt * 2 + db
                    t_bias = s.op("tensor", lambda e, db=db, bi=bi: e.matmul(
                        A.psumf[bi][:], P.ones_row[0:1, 0:128],
                        A.borow[0:1, 512 * db: 512 * db + 512],
                        start=False, stop=True),
                        wait=[P.setup_tok, t_bo], sig="T")
                    gtt = 4 * tb + tt
                    pv, psc = prol.stats_done[gtt]
                    t_ev = s.op("vector", lambda e, e_i=e_i, gtt=gtt, db=db, bi=bi: e.scalar_tensor_tensor(
                        x_slice(P, gtt, 512 * db, 512), A.psumf[bi][:],
                        A.gw[:, 8 * gtt + e_i: 8 * gtt + e_i + 1],
                        x_slice(P, gtt, 512 * db, 512),
                        ALU.mult, ALU.add), wait=[t_bias, pv, psc, gate_done], sig="V")
                    bank_free[bi] = t_ev
            bo_last_T = s.tok("T")
        for q in range(2):
            last_phaseA_T[q] = s.tok("T")


ALIBI_SLOPES = [2.0 ** (-0.5 * (h + 1)) for h in range(16)]


def _attn_head(s, P, A, C, oc, r0, q_done, v_done, t_tab, causal, stt_scalar):
    sb, avb, tbk, bank_free = C.sb, C.avb, C.tbk, C.bank_free
    for qt in range(NTT):
        nkb = (qt // 4 + 1) if causal else 4
        W_ = nkb * 512
        nb = (qt + 1) if causal else NTT
        sl = qt % 4
        mm_toks = []
        for kb in range(nkb):
            w = [q_done, v_done]
            if bank_free.get(("s", kb)) is not None:
                w.append(bank_free[("s", kb)])
            mm_toks.append(s.op("tensor", lambda e, r0=r0, qt=qt, kb=kb: e.matmul(
                sb[kb][:], A.qT[r0:r0 + 64, 128 * qt:128 * qt + 128],
                A.kT[r0:r0 + 64, 512 * kb:512 * kb + 512],
                start=True, stop=True), wait=w, sig="T"))
        if stt_scalar is None:
            c0 = 16 * sl
            wf = []
            if C.rs_free[sl] is not None:
                wf.append(C.rs_free[sl])
            bm = []
            for kb in range(nkb):
                bm.append(s.op("vector", lambda e, kb=kb, c0=c0: e.reduce_max(
                    A.rs[:, c0 + 4 + kb:c0 + 5 + kb], sb[kb][:], axis=AX),
                    wait=[mm_toks[kb]] + wf, sig="V"))
            t_max = s.op("vector", lambda e, c0=c0, nkb=nkb: e.reduce_max(
                A.rs[:, c0:c0 + 1], A.rs[:, c0 + 4:c0 + 4 + nkb], axis=AX),
                wait=[bm[-1]], sig="V")
            t_neg = s.op("vector", lambda e, c0=c0: e.tensor_scalar_mul(
                A.rs[:, c0 + 1:c0 + 2], A.rs[:, c0:c0 + 1], -1.0),
                wait=[t_max], sig="V")
            exps = []
            for kb in range(nkb):
                wexp = [t_neg]
                if C.psb_war is not None and kb == 0:
                    wexp.append(C.psb_war)
                exps.append(s.op("scalar", lambda e, kb=kb, c0=c0: e.activation(
                    out=A.Psb[:, 512 * kb:512 * kb + 512], in_=sb[kb][:],
                    func=AF.Exp, bias=A.rs[:, c0 + 1:c0 + 2], scale=1.0,
                    accum_out=A.rs[:, c0 + 8 + kb:c0 + 9 + kb]), wait=wexp, sig="SC"))
                bank_free[("s", kb)] = exps[-1]
            C.prev_exp = exps[-1]
            t_sum = s.op("vector", lambda e, c0=c0, nkb=nkb: e.reduce_sum(
                A.rs[:, c0 + 2:c0 + 3], A.rs[:, c0 + 8:c0 + 8 + nkb], axis=AX),
                wait=[exps[-1]], sig="V")
            t_rc = s.op("vector", lambda e, c0=c0: e.reciprocal(
                A.rs[:, c0 + 3:c0 + 4], A.rs[:, c0 + 2:c0 + 3]),
                wait=[t_sum], sig="V")
            t_exp = exps[-1]
            rc_col = A.rs[:, c0 + 3:c0 + 4]
        else:
            stt_toks = []
            for kb in range(nkb):
                i0 = 1920 - 128 * qt + 512 * kb
                w = [mm_toks[kb]]
                if t_tab is not None:
                    w.append(t_tab)
                if C.prev_exp is not None:
                    w.append(C.prev_exp)
                stt_toks.append(s.op("vector", lambda e, kb=kb, i0=i0: e.scalar_tensor_tensor(
                    A.Ssb[:, 512 * kb:512 * kb + 512],
                    A.dtab[:, i0:i0 + 512], stt_scalar, sb[kb][:],
                    ALU.mult, ALU.add), wait=w, sig="V"))
                bank_free[("s", kb)] = stt_toks[-1]
            w = [stt_toks[-1]]
            if C.rs_free[sl] is not None:
                w.append(C.rs_free[sl])
            t_max = s.op("vector", lambda e, sl=sl, W_=W_: e.reduce_max(
                A.rs[:, 4 * sl:4 * sl + 1], A.Ssb[:, 0:W_], axis=AX), wait=w, sig="V")
            t_neg = s.op("vector", lambda e, sl=sl: e.tensor_scalar_mul(
                A.rs[:, 4 * sl + 1:4 * sl + 2], A.rs[:, 4 * sl:4 * sl + 1], -1.0),
                wait=[t_max], sig="V")
            wexp = [t_neg]
            if C.psb_war is not None:
                wexp.append(C.psb_war)
            t_exp = s.op("scalar", lambda e, sl=sl, W_=W_: e.activation(
                out=A.Psb[:, 0:W_], in_=A.Ssb[:, 0:W_], func=AF.Exp,
                bias=A.rs[:, 4 * sl + 1:4 * sl + 2], scale=1.0,
                accum_out=A.rs[:, 4 * sl + 2:4 * sl + 3]), wait=wexp, sig="SC")
            C.prev_exp = t_exp
            t_rc = s.op("vector", lambda e, sl=sl: e.reciprocal(
                A.rs[:, 4 * sl + 3:4 * sl + 4], A.rs[:, 4 * sl + 2:4 * sl + 3]),
                wait=[t_exp], sig="V")
            rc_col = A.rs[:, 4 * sl + 3:4 * sl + 4]
        pev = {}
        for g in range((nb + 7) // 8):
            cnt = min(8, nb - 8 * g)
            gb = g % 2
            w = [t_exp, P.setup_tok]
            if bank_free.get(("tb", gb)) is not None:
                w.append(bank_free[("tb", gb)])
            tok = None
            for j in range(cnt):
                jj = 8 * g + j
                tok = s.op("tensor", lambda e, gb=gb, j=j, jj=jj: e.transpose(
                    tbk[gb][:, 128 * j:128 * j + 128],
                    A.Psb[:, 128 * jj:128 * jj + 128], P.ident[:]), wait=w, sig="T")
                w = []
            t_pev = s.op("vector", lambda e, g=g, gb=gb, cnt=cnt: e.tensor_copy(
                A.PT[:, 1024 * g:1024 * g + 128 * cnt],
                tbk[gb][:, 0:128 * cnt]), wait=[tok], sig="V")
            bank_free[("tb", gb)] = t_pev
            pev[g] = t_pev
        C.psb_war = s.tok("T")
        t_av = None
        for j in range(nb):
            w = [pev[j // 8]]
            if j == 0 and bank_free.get(("av",)) is not None:
                w.append(bank_free[("av",)])
            t_av = s.op("tensor", lambda e, j=j, r0=r0, nb=nb: e.matmul(
                avb[:, 0:64], A.PT[:, 128 * j:128 * j + 128],
                A.vc[:, 128 * j + r0:128 * j + r0 + 64],
                start=(j == 0), stop=(j == nb - 1)), wait=w, sig="T")
        t_oev = s.op("vector", lambda e, rc_col=rc_col: e.tensor_scalar_mul(
            A.osb[:, 0:64], avb[:, 0:64], rc_col),
            wait=[t_av, t_rc], sig="V")
        bank_free[("av",)] = t_oev
        C.rs_free[sl] = t_oev
        gb = qt % 2
        w = [t_oev, P.setup_tok]
        if bank_free.get(("tb", gb)) is not None:
            w.append(bank_free[("tb", gb)])
        t_otr = s.op("tensor", lambda e, gb=gb: e.transpose(
            tbk[gb][0:64, 0:128], A.osb[:, 0:64], P.ident[:]), wait=w, sig="T")
        t_ofin = s.op("vector", lambda e, gb=gb, r0=r0, oc=oc, qt=qt: e.tensor_copy(
            A.oT[r0:r0 + 64, T * oc + 128 * qt: T * oc + 128 * qt + 128],
            tbk[gb][0:64, 0:128]), wait=[t_otr], sig="V")
        bank_free[("tb", gb)] = t_ofin


def _emit_out_proj(s, nc, P, A, prol, wosb, t_wo, attn_done,
                   bias_row=None, t_bias_load=None):
    with ExitStack() as esc2:
        yb = [esc2.enter_context(nc.psum_tensor(f"t_yb{i}", [128, 512], F32))
              for i in range(8)]
        ybfree = {}
        for tb in range(4):
            for fc in range(NDC):
                for tt in range(4):
                    for db in range(2):
                        bi = 2 * tt + db
                        gtt = 4 * tb + tt
                        w = [t_wo] + (attn_done if fc == 0 else [])
                        if fc == 0 and ybfree.get(bi) is not None:
                            w.append(ybfree[bi])
                        tok = s.op("tensor", lambda e, fc=fc, gtt=gtt, db=db, bi=bi: e.matmul(
                            yb[bi][:], A.oT[:, T * fc + 128 * gtt: T * fc + 128 * gtt + 128],
                            wosb[:, fc, 512 * db:512 * db + 512],
                            start=(fc == 0),
                            stop=(fc == NDC - 1 and bias_row is None)),
                            wait=w, sig="T")
                        if fc == NDC - 1:
                            if bias_row is not None:
                                tok = s.op("tensor", lambda e, db=db, bi=bi: e.matmul(
                                    yb[bi][:], P.ones_row[0:1, 0:128],
                                    bias_row[0:1, 512 * db: 512 * db + 512],
                                    start=False, stop=True),
                                    wait=[P.setup_tok, t_bias_load], sig="T")
                            pv, psc = prol.stats_done[gtt]
                            t_ev = s.op("vector", lambda e, gtt=gtt, db=db, bi=bi: e.tensor_add(
                                x_slice(P, gtt, 512 * db, 512), yb[bi][:],
                                x_slice(P, gtt, 512 * db, 512)),
                                wait=[tok, pv, psc], sig="V")
                            ybfree[bi] = t_ev


def emit_attn_a0(s, nc, es, P):
    A = NS()
    with ExitStack() as es1:
        A.scr = es1.enter_context(nc.sbuf_tensor("t_scr", [128, D], F32))
        A.xn = es1.enter_context(nc.sbuf_tensor("t_xn", [128, 2 * D], BF16))
        with ExitStack() as esp:
            A.psumb = [esp.enter_context(nc.psum_tensor(f"t_psb{i}", [128, 1024], BF16))
                       for i in range(2)]
            prol = emit_prologue(s, P, A)
    gsafe0 = s.toks("T", "V", "SC")
    A.dtab = es.enter_context(nc.sbuf_tensor("t_dtab", [128, 2432], F32))
    A.wq = [es.enter_context(nc.sbuf_tensor(f"t_wq{i}", [128, NDC * 128], BF16))
            for i in range(2)]
    A.wk = [es.enter_context(nc.sbuf_tensor(f"t_wk{i}", [128, NDC * 128], BF16))
            for i in range(2)]
    A.wv = [es.enter_context(nc.sbuf_tensor(f"t_wv{i}", [128, NDC * 128], BF16))
            for i in range(2)]
    A.wo = es.enter_context(nc.sbuf_tensor("t_wo", [128, NDC * 1024], BF16))
    A.qT = es.enter_context(nc.sbuf_tensor("t_qT", [128, T], BF16))
    A.kT = es.enter_context(nc.sbuf_tensor("t_kT", [128, T], BF16))
    A.vTs = es.enter_context(nc.sbuf_tensor("t_vTs", [128, T], BF16))
    A.vc = es.enter_context(nc.sbuf_tensor("t_vc", [128, NTT * 128], BF16))
    A.oT = es.enter_context(nc.sbuf_tensor("t_oT", [128, NDC * T], BF16))
    A.Ssb = es.enter_context(nc.sbuf_tensor("t_Ssb", [128, T], F32))
    A.Psb = es.enter_context(nc.sbuf_tensor("t_Psb", [128, T], BF16))
    A.PT = es.enter_context(nc.sbuf_tensor("t_PT", [128, NTT * 128], BF16))
    A.rs = es.enter_context(nc.sbuf_tensor("t_rs", [128, 16], F32))
    A.osb = es.enter_context(nc.sbuf_tensor("t_osb", [128, 64], BF16))
    t_io = s.op("gpsimd", lambda e: e.iota(
        A.dtab[:], pattern=[[-1, 2432]], base=1920, channel_multiplier=1,
        allow_small_or_imprecise_dtypes=True), wait=gsafe0, sig="P")
    t_dt = s.op("gpsimd", lambda e, _w=None: e.affine_select(
        out=A.dtab[:], in_=A.dtab[:], pattern=[[-1, 2432]],
        compare_op=ALU.is_ge, fill=1e33, base=1920, channel_multiplier=1),
        wait=[t_io], sig="P")
    t_wo = s.op("gpsimd", lambda e: e.dma_start(
        out=A.wo.ap().rearrange("p (c d) -> p c d", c=NDC),
        in_=P.W["a0_wo"].rearrange("(c p) d -> p c d", p=128)),
        wait=gsafe0, sig="d_a0wo", n=16)

    wosb = A.wo.ap().rearrange("p (c d) -> p c d", c=NDC)
    oT_tok = {}
    attn_T_done = {}
    proj_T_done = {}
    with ExitStack() as esc:
        C = NS(
            sb=[esc.enter_context(nc.psum_tensor(f"t_sp{i}", [128, 512], F32))
                for i in range(4)],
            avb=esc.enter_context(nc.psum_tensor("t_av", [128, 512], F32)),
            tbk=[esc.enter_context(nc.psum_tensor(f"t_tb{i}", [128, 1024], BF16))
                 for i in range(2)],
            bank_free={}, rs_free={sl: None for sl in range(4)},
            psb_war=None, prev_exp=None)
        sb, tbk, bank_free = C.sb, C.tbk, C.bank_free
        for oc in range(8):
            par = oc % 2
            wl = []
            for nm, buf in (("a0_wq", A.wq[par]), ("a0_wk", A.wk[par]), ("a0_wv", A.wv[par])):
                w = list(gsafe0) if oc < 2 else [proj_T_done[oc - 2]]
                wl.append(s.op("gpsimd", lambda e, nm=nm, buf=buf, oc=oc: e.dma_start(
                    out=buf.ap().rearrange("p (c o) -> p c o", c=NDC),
                    in_=P.W[nm][:, 128 * oc:128 * oc + 128].rearrange(
                        "(c p) o -> p c o", p=128)), wait=w,
                    sig=f"d_{nm[3:]}_{par}", n=16))
            war = [attn_T_done[oc - 1]] if oc >= 1 else []
            wqs = A.wq[par].ap().rearrange("p (c o) -> p c o", c=NDC)
            wks = A.wk[par].ap().rearrange("p (c o) -> p c o", c=NDC)
            wvs = A.wv[par].ap().rearrange("p (c o) -> p c o", c=NDC)
            for src_i, (wvw, wtok, dst, kind) in enumerate((
                    (wqs, wl[0], A.qT, "q"), (wks, wl[1], A.kT, "k"),
                    (wvs, wl[2], A.vTs, "v"))):
                for tblk in range(4):
                    bk = (tblk + src_i) % 2
                    bank = sb[bk]
                    w = [wtok, prol.done_V]
                    if bank_free.get(("s", bk)) is not None:
                        w.append(bank_free[("s", bk)])
                    tok = None
                    for c in range(NDC):
                        tok = s.op("tensor", lambda e, wvw=wvw, c=c, tblk=tblk, bank=bank: e.matmul(
                            bank[:], wvw[:, c, :], xnT_slice(P, c, 512 * tblk, 512),
                            start=(c == 0), stop=(c == NDC - 1)), wait=w, sig="T")
                        w = []
                    ww = [tok] + war
                    if kind == "q":
                        t_ev = s.op("scalar", lambda e, dst=dst, tblk=tblk, bank=bank: e.mul(
                            dst[:, 512 * tblk:512 * tblk + 512], bank[:], 0.125),
                            wait=ww, sig="SC")
                    elif kind == "k":
                        t_ev = s.op("scalar", lambda e, dst=dst, tblk=tblk, bank=bank: e.copy(
                            dst[:, 512 * tblk:512 * tblk + 512], bank[:]),
                            wait=ww, sig="SC")
                    else:
                        t_ev = s.op("vector", lambda e, dst=dst, tblk=tblk, bank=bank: e.tensor_copy(
                            dst[:, 512 * tblk:512 * tblk + 512], bank[:]),
                            wait=ww, sig="V")
                    bank_free[("s", bk)] = t_ev
            vT_done = s.tok("V")
            q_done = s.tok("SC")
            for g in range(2):
                w = [vT_done, P.setup_tok] + war
                if bank_free.get(("tb", g)) is not None:
                    w.append(bank_free[("tb", g)])
                tok = None
                for j in range(8):
                    tt = 8 * g + j
                    tok = s.op("tensor", lambda e, g=g, j=j, tt=tt: e.transpose(
                        tbk[g][:, 128 * j:128 * j + 128],
                        A.vTs[:, 128 * tt:128 * tt + 128], P.ident[:]),
                        wait=w, sig="T")
                    w = []
                t_ev = s.op("vector", lambda e, g=g: e.tensor_copy(
                    A.vc[:, 1024 * g:1024 * g + 1024], tbk[g][:]), wait=[tok], sig="V")
                bank_free[("tb", g)] = t_ev
            v_done = s.tok("V")
            proj_T_done[oc] = s.tok("T")
            for h2 in range(2):
                h = 2 * oc + h2
                _attn_head(s, P, A, C, oc, 64 * h2, q_done, v_done, t_dt,
                           causal=True, stt_scalar=-ALIBI_SLOPES[h])
            attn_T_done[oc] = s.tok("T")
            oT_tok[oc] = s.tok("V")
    attn_done = s.toks("T", "V", "SC")
    _emit_out_proj(s, nc, P, A, prol, wosb, t_wo, attn_done)


def emit_attn_a1(s, nc, es, P):
    A = NS()
    with ExitStack() as es1:
        A.scr = es1.enter_context(nc.sbuf_tensor("t_scr", [128, D], F32))
        A.xn = es1.enter_context(nc.sbuf_tensor("t_xn", [128, 2 * D], BF16))
        with ExitStack() as esp:
            A.psumb = [esp.enter_context(nc.psum_tensor(f"t_psb{i}", [128, 1024], BF16))
                       for i in range(2)]
            prol = emit_prologue(s, P, A)
    gsafe0 = s.toks("T", "V", "SC")
    A.dtab = es.enter_context(nc.sbuf_tensor("t_dtab", [128, 2432], BF16))
    A.wq = [es.enter_context(nc.sbuf_tensor(f"t_wq{i}", [128, NDC * 128], BF16))
            for i in range(2)]
    A.wk = [es.enter_context(nc.sbuf_tensor(f"t_wk{i}", [128, NDC * 128], BF16))
            for i in range(2)]
    A.wv = [es.enter_context(nc.sbuf_tensor(f"t_wv{i}", [128, NDC * 128], BF16))
            for i in range(2)]
    A.wo = es.enter_context(nc.sbuf_tensor("t_wo", [128, NDC * 1024], BF16))
    A.qT = es.enter_context(nc.sbuf_tensor("t_qT", [128, T], BF16))
    A.kT = es.enter_context(nc.sbuf_tensor("t_kT", [128, T], BF16))
    A.vTs = es.enter_context(nc.sbuf_tensor("t_vTs", [128, T], BF16))
    A.vc = es.enter_context(nc.sbuf_tensor("t_vc", [128, NTT * 128], BF16))
    A.oT = es.enter_context(nc.sbuf_tensor("t_oT", [128, NDC * T], BF16))
    A.Ssb = es.enter_context(nc.sbuf_tensor("t_Ssb", [128, T], F32))
    A.Psb = es.enter_context(nc.sbuf_tensor("t_Psb", [128, T], BF16))
    A.PT = es.enter_context(nc.sbuf_tensor("t_PT", [128, NTT * 128], BF16))
    A.rs = es.enter_context(nc.sbuf_tensor("t_rs", [128, 16], F32))
    A.osb = es.enter_context(nc.sbuf_tensor("t_osb", [128, 64], BF16))
    A.cosb = es.enter_context(nc.sbuf_tensor("t_cosb", [128, T], BF16))
    A.sinb = es.enter_context(nc.sbuf_tensor("t_sinb", [128, T], BF16))
    A.rot = es.enter_context(nc.sbuf_tensor("t_rot", [128, 512], BF16))
    A.qb = es.enter_context(nc.sbuf_tensor("t_qb", [128, 8], F32))
    A.kb = es.enter_context(nc.sbuf_tensor("t_kb", [128, 8], F32))
    A.vb = es.enter_context(nc.sbuf_tensor("t_vb", [128, 8], F32))
    A.pbrow = es.enter_context(nc.sbuf_tensor("t_pbrow", [1, D], BF16))
    t_mz = s.op("gpsimd", lambda e: e.memset(A.dtab[:], 0.0), wait=gsafe0, sig="P")
    t_tab = s.op("gpsimd", lambda e: e.affine_select(
        out=A.dtab[:], in_=A.dtab[:], pattern=[[-1, 2432]],
        compare_op=ALU.is_ge, fill=-1e38, base=1920, channel_multiplier=1),
        wait=[t_mz], sig="P")
    t_wo = s.op("gpsimd", lambda e: e.dma_start(
        out=A.wo.ap().rearrange("p (c d) -> p c d", c=NDC),
        in_=P.W["a1_wo"].rearrange("(c p) d -> p c d", p=128)),
        wait=gsafe0, sig="d_a1wo", n=16)
    t_cos = s.op("gpsimd", lambda e: e.dma_start(out=A.cosb[:], in_=P.W["a1_cos"]),
                 wait=gsafe0, sig="d_a1cos", n=16)
    t_sin = s.op("gpsimd", lambda e: e.dma_start(out=A.sinb[:], in_=P.W["a1_sin"]),
                 wait=gsafe0, sig="d_a1sin", n=16)
    t_qb = s.op("sync", lambda e: e.dma_start(out=A.qb[:], in_=P.W["a1_qb"]),
                wait=gsafe0, sig="d_a1qb", n=16)
    t_kb = s.op("sync", lambda e: e.dma_start(out=A.kb[:], in_=P.W["a1_kb"]),
                wait=gsafe0, sig="d_a1kb", n=16)
    t_vb = s.op("sync", lambda e: e.dma_start(out=A.vb[:], in_=P.W["a1_vb"]),
                wait=gsafe0, sig="d_a1vb", n=16)
    t_pb = s.op("gpsimd", lambda e: e.dma_start(out=A.pbrow[:], in_=P.W["a1_pb"]),
                wait=gsafe0, sig="d_a1pb", n=16)
    bload = [t_qb, t_kb, t_vb]

    wosb = A.wo.ap().rearrange("p (c d) -> p c d", c=NDC)
    attn_T_done = {}
    proj_T_done = {}
    with ExitStack() as esc:
        C = NS(
            sb=[esc.enter_context(nc.psum_tensor(f"t_sp{i}", [128, 512], F32))
                for i in range(4)],
            avb=esc.enter_context(nc.psum_tensor("t_av", [128, 512], F32)),
            tbk=[esc.enter_context(nc.psum_tensor(f"t_tb{i}", [128, 1024], BF16))
                 for i in range(2)],
            bank_free={}, rs_free={sl: None for sl in range(4)},
            psb_war=None, prev_exp=None)
        sb, tbk, bank_free = C.sb, C.tbk, C.bank_free
        for oc in range(8):
            par = oc % 2
            wl = []
            for nm, buf in (("a1_wq", A.wq[par]), ("a1_wk", A.wk[par]), ("a1_wv", A.wv[par])):
                w = list(gsafe0) if oc < 2 else [proj_T_done[oc - 2]]
                wl.append(s.op("gpsimd", lambda e, nm=nm, buf=buf, oc=oc: e.dma_start(
                    out=buf.ap().rearrange("p (c o) -> p c o", c=NDC),
                    in_=P.W[nm][:, 128 * oc:128 * oc + 128].rearrange(
                        "(c p) o -> p c o", p=128)), wait=w,
                    sig=f"d_{nm[3:]}_{par}", n=16))
            war = [attn_T_done[oc - 1]] if oc >= 1 else []
            wqs = A.wq[par].ap().rearrange("p (c o) -> p c o", c=NDC)
            wks = A.wk[par].ap().rearrange("p (c o) -> p c o", c=NDC)
            wvs = A.wv[par].ap().rearrange("p (c o) -> p c o", c=NDC)
            etoks = {}
            for src_i, (wvw, wtok, dst, kind) in enumerate((
                    (wqs, wl[0], A.qT, "q"), (wks, wl[1], A.kT, "k"),
                    (wvs, wl[2], A.vTs, "v"))):
                for tblk in range(4):
                    bk = (tblk + src_i) % 2
                    bank = sb[bk]
                    w = [wtok, prol.done_V]
                    if bank_free.get(("s", bk)) is not None:
                        w.append(bank_free[("s", bk)])
                    tok = None
                    for c in range(NDC):
                        tok = s.op("tensor", lambda e, wvw=wvw, c=c, tblk=tblk, bank=bank: e.matmul(
                            bank[:], wvw[:, c, :], xnT_slice(P, c, 512 * tblk, 512),
                            start=(c == 0), stop=(c == NDC - 1)), wait=w, sig="T")
                        w = []
                    ww = [tok] + bload + war
                    if kind == "q":
                        t_ev = s.op("vector", lambda e, dst=dst, tblk=tblk, bank=bank, oc=oc: e.tensor_scalar(
                            dst[:, 512 * tblk:512 * tblk + 512], bank[:],
                            A.qb[:, oc:oc + 1], 0.125, ALU.add, ALU.mult),
                            wait=ww, sig="V")
                    elif kind == "k":
                        t_ev = s.op("vector", lambda e, dst=dst, tblk=tblk, bank=bank, oc=oc: e.tensor_scalar(
                            dst[:, 512 * tblk:512 * tblk + 512], bank[:],
                            A.kb[:, oc:oc + 1], None, ALU.add),
                            wait=ww, sig="V")
                    else:
                        t_ev = s.op("vector", lambda e, dst=dst, tblk=tblk, bank=bank, oc=oc: e.tensor_scalar(
                            dst[:, 512 * tblk:512 * tblk + 512], bank[:],
                            A.vb[:, oc:oc + 1], None, ALU.add),
                            wait=ww, sig="V")
                    bank_free[("s", bk)] = t_ev
                    etoks[(kind, tblk)] = t_ev
            rot_free = None
            for kind, dst in (("q", A.qT), ("k", A.kT)):
                for tblk in range(4):
                    sl_ = slice(512 * tblk, 512 * tblk + 512)
                    last_ev = etoks[(kind, tblk)]
                    cps = []
                    for (d0, s0) in ((0, 32), (32, 0), (64, 96), (96, 64)):
                        w = [last_ev]
                        if rot_free is not None:
                            w.append(rot_free)
                        cps.append(s.op("vector", lambda e, dst=dst, d0=d0, s0=s0, sl_=sl_: e.tensor_copy(
                            A.rot[d0:d0 + 32, :], dst[s0:s0 + 32, sl_]),
                            wait=w, sig="V"))
                    t_m2 = s.op("vector", lambda e, sl_=sl_: e.tensor_mul(
                        A.rot[:], A.rot[:], A.sinb[:, sl_]),
                        wait=[cps[-1], t_sin], sig="V")
                    t_m1 = s.op("vector", lambda e, dst=dst, sl_=sl_: e.tensor_mul(
                        dst[:, sl_], dst[:, sl_], A.cosb[:, sl_]),
                        wait=[cps[-1], t_cos], sig="V")
                    rot_free = s.op("vector", lambda e, dst=dst, sl_=sl_: e.tensor_add(
                        dst[:, sl_], dst[:, sl_], A.rot[:]),
                        wait=[t_m1, t_m2], sig="V")
            vT_done = s.tok("V")
            for g in range(2):
                w = [vT_done, P.setup_tok] + war
                if bank_free.get(("tb", g)) is not None:
                    w.append(bank_free[("tb", g)])
                tok = None
                for j in range(8):
                    tt = 8 * g + j
                    tok = s.op("tensor", lambda e, g=g, j=j, tt=tt: e.transpose(
                        tbk[g][:, 128 * j:128 * j + 128],
                        A.vTs[:, 128 * tt:128 * tt + 128], P.ident[:]),
                        wait=w, sig="T")
                    w = []
                t_ev = s.op("vector", lambda e, g=g: e.tensor_copy(
                    A.vc[:, 1024 * g:1024 * g + 1024], tbk[g][:]), wait=[tok], sig="V")
                bank_free[("tb", g)] = t_ev
            v_done = s.tok("V")
            q_done = s.tok("V")
            proj_T_done[oc] = s.tok("T")
            for h2 in range(2):
                _attn_head(s, P, A, C, oc, 64 * h2, q_done, v_done, t_tab,
                           causal=True, stt_scalar=1.0)
            attn_T_done[oc] = s.tok("T")
    attn_done = s.toks("T", "V", "SC")
    _emit_out_proj(s, nc, P, A, prol, wosb, t_wo, attn_done,
                   bias_row=A.pbrow, t_bias_load=t_pb)


def _emit_out_proj_streamed(s, nc, es, P, A, prol, wov, attn_done, bias_row,
                            t_bias_load, pfx):
    wosb = [es.enter_context(nc.sbuf_tensor(f"t_wos{i}_{pfx}", [128, D], BF16))
            for i in range(2)]
    with ExitStack() as esc2:
        yb = [esc2.enter_context(nc.psum_tensor(f"t_yb{i}", [128, 512], F32))
              for i in range(8)]
        ybfree = {}
        wo_last_T = [None, None]
        for tb in range(4):
            mm_tok = None
            for ks in range(8):
                par = ks % 2
                fc = ks
                w = list(attn_done) if wo_last_T[par] is None else [wo_last_T[par]]
                wotok = s.op("gpsimd", lambda e, ks=ks, par=par: e.dma_start(
                    out=wosb[par][:],
                    in_=wov.rearrange("(ks p) d -> ks p d", p=128)[ks]),
                    wait=w, sig=f"d_{pfx}wos_{par}", n=16)
                for tt in range(4):
                    for db in range(2):
                        bi = 2 * tt + db
                        gtt = 4 * tb + tt
                        w = [wotok] + (attn_done if fc == 0 else [])
                        if fc == 0 and ybfree.get(bi) is not None:
                            w.append(ybfree[bi])
                        mm_tok = s.op("tensor", lambda e, fc=fc, gtt=gtt, db=db, par=par, bi=bi: e.matmul(
                            yb[bi][:],
                            A.oT[:, T * fc + 128 * gtt: T * fc + 128 * gtt + 128],
                            wosb[par][:, 512 * db: 512 * db + 512],
                            start=(fc == 0), stop=False), wait=w, sig="T")
                wo_last_T[par] = mm_tok
            for tt in range(4):
                for db in range(2):
                    bi = 2 * tt + db
                    gtt = 4 * tb + tt
                    t_bias = s.op("tensor", lambda e, db=db, bi=bi: e.matmul(
                        yb[bi][:], P.ones_row[0:1, 0:128],
                        bias_row[0:1, 512 * db: 512 * db + 512],
                        start=False, stop=True),
                        wait=[P.setup_tok, t_bias_load], sig="T")
                    pv, psc = prol.stats_done[gtt]
                    t_ev = s.op("vector", lambda e, gtt=gtt, db=db, bi=bi: e.tensor_add(
                        x_slice(P, gtt, 512 * db, 512), yb[bi][:],
                        x_slice(P, gtt, 512 * db, 512)),
                        wait=[t_bias, pv, psc], sig="V")
                    ybfree[bi] = t_ev


def emit_attn_b1(s, nc, es, P):
    A = NS()
    with ExitStack() as es1:
        A.scr = es1.enter_context(nc.sbuf_tensor("t_scr", [128, D], F32))
        A.xn = es1.enter_context(nc.sbuf_tensor("t_xn", [128, 2 * D], BF16))
        with ExitStack() as esp:
            A.psumb = [esp.enter_context(nc.psum_tensor(f"t_psb{i}", [128, 1024], BF16))
                       for i in range(2)]
            prol = emit_prologue(s, P, A)
    gsafe0 = s.toks("T", "V", "SC")
    QSCALE = 1.0 / float(np.sqrt(np.float32(1040.0)))
    A.ctq = es.enter_context(nc.sbuf_tensor("t_ctq", [128, 4 * T], BF16))
    A.ctkv = es.enter_context(nc.sbuf_tensor("t_ctkv", [128, 4 * T], BF16))
    A.dqb = es.enter_context(nc.sbuf_tensor("t_dqb", [128, 4], F32))
    A.dkvb = es.enter_context(nc.sbuf_tensor("t_dkvb", [128, 4], F32))
    ct_done = None
    with ExitStack() as esw:
        wdq = esw.enter_context(nc.sbuf_tensor("t_wdq", [128, NDC * 512], BF16))
        wdkv = esw.enter_context(nc.sbuf_tensor("t_wdkv", [128, NDC * 512], BF16))
        with ExitStack() as escp:
            pb = [escp.enter_context(nc.psum_tensor(f"t_pp{i}", [128, 512], F32))
                  for i in range(2)]
            t_wdq = s.op("gpsimd", lambda e: e.dma_start(
                out=wdq.ap().rearrange("p (c o) -> p c o", c=NDC),
                in_=P.W["b1_wdq"].rearrange("(c p) o -> p c o", p=128)),
                wait=gsafe0, sig="d_b1wdq", n=16)
            t_wdkv = s.op("gpsimd", lambda e: e.dma_start(
                out=wdkv.ap().rearrange("p (c o) -> p c o", c=NDC),
                in_=P.W["b1_wdkv"].rearrange("(c p) o -> p c o", p=128)),
                wait=gsafe0, sig="d_b1wdkv", n=16)
            t_dqb = s.op("sync", lambda e: e.dma_start(out=A.dqb[:], in_=P.W["b1_dqb"]),
                         wait=gsafe0, sig="d_b1dqb", n=16)
            t_dkvb = s.op("sync", lambda e: e.dma_start(out=A.dkvb[:], in_=P.W["b1_dkvb"]),
                          wait=gsafe0, sig="d_b1dkvb", n=16)
            wdqv = wdq.ap().rearrange("p (c o) -> p c o", c=NDC)
            wdkvv = wdkv.ap().rearrange("p (c o) -> p c o", c=NDC)
            bfree = [None, None]
            for wview, wtok, btok, bias, dst in (
                    (wdqv, t_wdq, t_dqb, A.dqb, A.ctq),
                    (wdkvv, t_wdkv, t_dkvb, A.dkvb, A.ctkv)):
                for qc in range(4):
                    for tblk in range(4):
                        bi = tblk % 2
                        w = [wtok, prol.done_V]
                        if bfree[bi] is not None:
                            w.append(bfree[bi])
                        tok = None
                        for c in range(NDC):
                            tok = s.op("tensor", lambda e, wview=wview, c=c, qc=qc, tblk=tblk, bi=bi: e.matmul(
                                pb[bi][:], wview[:, c, 128 * qc:128 * qc + 128],
                                xnT_slice(P, c, 512 * tblk, 512),
                                start=(c == 0), stop=(c == NDC - 1)), wait=w, sig="T")
                            w = []
                        import os as _os
                        if _os.environ.get("DBG_NOBIAS") == "1":
                            t_ev = s.op("vector", lambda e, dst=dst, qc=qc, tblk=tblk, bi=bi: e.tensor_copy(
                                dst[:, T * qc + 512 * tblk: T * qc + 512 * tblk + 512],
                                pb[bi][:]), wait=[tok, btok], sig="V")
                        else:
                            t_ev = s.op("vector", lambda e, dst=dst, bias=bias, qc=qc, tblk=tblk, bi=bi: e.tensor_scalar(
                                dst[:, T * qc + 512 * tblk: T * qc + 512 * tblk + 512],
                                pb[bi][:], bias[:, qc:qc + 1], None, ALU.add),
                                wait=[tok, btok], sig="V")
                        bfree[bi] = t_ev
            ct_done = s.toks("T", "V")
            import os as _os
            if _os.environ.get("DBG_CTONLY") == "1":
                P.dbg = {"d_ctq": (A.ctq.ap(), BF16), "d_ctkv": (A.ctkv.ap(), BF16),
                         "d_wdq": (wdq.ap(), BF16), "d_dqb": (A.dqb.ap(), F32)}
                return
    A.dtab = None
    A.wuq = [es.enter_context(nc.sbuf_tensor("t_wuq0", [128, 4 * 128], BF16))]
    A.wqr = [es.enter_context(nc.sbuf_tensor("t_wqr0", [128, 4 * 128], BF16))]
    A.wuk = [es.enter_context(nc.sbuf_tensor("t_wuk0", [128, 4 * 128], BF16))]
    A.wkr = [es.enter_context(nc.sbuf_tensor("t_wkr0", [128, NDC * 128], BF16))]
    A.wuv = [es.enter_context(nc.sbuf_tensor("t_wuv0", [128, 4 * 128], BF16))]
    A.ub = es.enter_context(nc.sbuf_tensor("t_ub", [128, 4 * 8], F32))
    A.krb = es.enter_context(nc.sbuf_tensor("t_krb", [128, 8], F32))
    A.qT = es.enter_context(nc.sbuf_tensor("t_qT", [128, T], BF16))
    A.kT = es.enter_context(nc.sbuf_tensor("t_kT", [128, T], BF16))
    A.tmpT = es.enter_context(nc.sbuf_tensor("t_tmpT", [128, T], BF16))
    A.vc = es.enter_context(nc.sbuf_tensor("t_vc", [128, NTT * 128], BF16))
    A.oT = es.enter_context(nc.sbuf_tensor("t_oT", [128, NDC * T], BF16))
    A.Psb = es.enter_context(nc.sbuf_tensor("t_Psb", [128, T], BF16))
    A.PT = es.enter_context(nc.sbuf_tensor("t_PT", [128, NTT * 128], BF16))
    A.rs = es.enter_context(nc.sbuf_tensor("t_rs", [128, 64], F32))
    A.osb = es.enter_context(nc.sbuf_tensor("t_osb", [128, 64], BF16))
    A.cosb = es.enter_context(nc.sbuf_tensor("t_cosb", [128, T], BF16))
    A.sinb = es.enter_context(nc.sbuf_tensor("t_sinb", [128, T], BF16))
    A.rot = es.enter_context(nc.sbuf_tensor("t_rot", [128, 512], BF16))
    A.borow = es.enter_context(nc.sbuf_tensor("t_borow", [1, D], BF16))
    t_ub = s.op("sync", lambda e: e.dma_start(out=A.ub[:], in_=P.W["b1_ub"]),
                wait=list(gsafe0) + list(ct_done), sig="d_b1ub", n=16)
    t_krb = s.op("sync", lambda e: e.dma_start(out=A.krb[:], in_=P.W["b1_krb"]),
                 wait=list(gsafe0) + list(ct_done), sig="d_b1krb", n=16)
    t_bo = s.op("gpsimd", lambda e: e.dma_start(out=A.borow[:], in_=P.W["b1_bo"]),
                wait=gsafe0, sig="d_b1bo", n=16)
    bload = [t_ub, t_krb]
    attn_T_done = {}
    proj_T_done = {}
    tab_tok = {}
    with ExitStack() as esc:
        C = NS(
            sb=[esc.enter_context(nc.psum_tensor(f"t_sp{i}", [128, 512], F32))
                for i in range(4)],
            avb=esc.enter_context(nc.psum_tensor("t_av", [128, 512], F32)),
            tbk=[esc.enter_context(nc.psum_tensor(f"t_tb{i}", [128, 1024], BF16))
                 for i in range(2)],
            bank_free={}, rs_free={sl: None for sl in range(4)},
            psb_war=None, prev_exp=None)
        sb, tbk, bank_free = C.sb, C.tbk, C.bank_free
        ctqv = A.ctq.ap().rearrange("p (c t) -> p c t", c=4)
        ctkvv = A.ctkv.ap().rearrange("p (c t) -> p c t", c=4)
        for oc in range(8):
            par = 0
            wl = {}
            for nm, buf, nch in (("b1_wuq", A.wuq[par], 4), ("b1_wqr", A.wqr[par], 4),
                                 ("b1_wuk", A.wuk[par], 4), ("b1_wkr", A.wkr[par], 8),
                                 ("b1_wuv", A.wuv[par], 4)):
                w = (list(gsafe0) + list(ct_done)) if oc == 0 else [proj_T_done[oc - 1]]
                wl[nm] = s.op("gpsimd", lambda e, nm=nm, buf=buf, nch=nch, oc=oc: e.dma_start(
                    out=buf.ap().rearrange("p (c o) -> p c o", c=nch),
                    in_=P.W[nm][:, 128 * oc:128 * oc + 128].rearrange(
                        "(c p) o -> p c o", p=128)), wait=w, sig=f"d_{nm}", n=16)
            wtb = [tab_tok[oc - 1]] if oc >= 1 else list(gsafe0)
            t_cos = s.op("gpsimd", lambda e, oc=oc: e.dma_start(
                out=A.cosb[:], in_=P.W["b1_cos"][oc]), wait=wtb, sig="d_b1cos", n=16)
            t_sin = s.op("gpsimd", lambda e, oc=oc: e.dma_start(
                out=A.sinb[:], in_=P.W["b1_sin"][oc]), wait=wtb, sig="d_b1sin", n=16)
            war = [attn_T_done[oc - 1]] if oc >= 1 else []
            wuqv = A.wuq[par].ap().rearrange("p (c o) -> p c o", c=4)
            wqrv = A.wqr[par].ap().rearrange("p (c o) -> p c o", c=4)
            wukv = A.wuk[par].ap().rearrange("p (c o) -> p c o", c=4)
            wkrv = A.wkr[par].ap().rearrange("p (c o) -> p c o", c=NDC)
            wuvv = A.wuv[par].ap().rearrange("p (c o) -> p c o", c=NDC // 2)
            jobs = (
                ("qc", A.qT, wuqv, 4, ctqv, A.ub, 0, QSCALE, False),
                ("qr", A.tmpT, wqrv, 4, ctqv, A.ub, 1, QSCALE, True),
                ("kc", A.kT, wukv, 4, ctkvv, A.ub, 2, 1.0, False),
                ("kr", A.tmpT, wkrv, 8, None, A.krb, 0, 1.0, True),
                ("v", A.tmpT, wuvv, 4, ctkvv, A.ub, 3, 1.0, False),
            )
            rot_free = None
            ev_toks = {}
            for (jn, dst, wview, nch, srcv, btile, bcol, scale, do_rope) in jobs:
                wtok = wl[{"qc": "b1_wuq", "qr": "b1_wqr", "kc": "b1_wuk",
                           "kr": "b1_wkr", "v": "b1_wuv"}[jn]]
                for tblk in range(4):
                    bi = tblk % 2
                    bank = sb[bi]
                    w = [wtok, prol.done_V] + bload
                    if jn in ("qc", "qr"):
                        w += list(ct_done) if oc == 0 else []
                    if bank_free.get(("s", bi)) is not None:
                        w.append(bank_free[("s", bi)])
                    tok = None
                    for c in range(nch):
                        if srcv is None:
                            rhs = xnT_slice(P, c, 512 * tblk, 512)
                        else:
                            rhs = None
                        tok = s.op("tensor", lambda e, wview=wview, c=c, tblk=tblk, bank=bank, srcv=srcv, nch=nch: e.matmul(
                            bank[:], wview[:, c, :],
                            (xnT_slice(P, c, 512 * tblk, 512) if srcv is None
                             else srcv[:, c, 512 * tblk:512 * tblk + 512]),
                            start=(c == 0), stop=(c == nch - 1)), wait=w, sig="T")
                        w = []
                    ww = [tok] + war
                    bcol_ap = btile[:, 8 * bcol + oc: 8 * bcol + oc + 1] if btile is A.ub \
                        else btile[:, oc:oc + 1]
                    if jn == "qc":
                        t_ev = s.op("vector", lambda e, dst=dst, tblk=tblk, bank=bank, bcol_ap=bcol_ap, scale=scale: e.tensor_scalar(
                            dst[:, 512 * tblk:512 * tblk + 512], bank[:],
                            bcol_ap, scale, ALU.add, ALU.mult), wait=ww, sig="V")
                    elif scale != 1.0:
                        t_ev = s.op("vector", lambda e, dst=dst, tblk=tblk, bank=bank, bcol_ap=bcol_ap, scale=scale: e.tensor_scalar(
                            dst[:, 512 * tblk:512 * tblk + 512], bank[:],
                            bcol_ap, scale, ALU.add, ALU.mult), wait=ww, sig="V")
                    else:
                        t_ev = s.op("vector", lambda e, dst=dst, tblk=tblk, bank=bank, bcol_ap=bcol_ap: e.tensor_scalar(
                            dst[:, 512 * tblk:512 * tblk + 512], bank[:],
                            bcol_ap, None, ALU.add), wait=ww, sig="V")
                    bank_free[("s", bi)] = t_ev
                    ev_toks[(jn, tblk)] = t_ev
                    if do_rope:
                        sl_ = slice(512 * tblk, 512 * tblk + 512)
                        wz = [t_ev, t_cos, t_sin]
                        if rot_free is not None:
                            wz.append(rot_free)
                        cps = []
                        for (d0, s0) in ((0, 32), (32, 0), (64, 96), (96, 64)):
                            cps.append(s.op("vector", lambda e, dst=dst, d0=d0, s0=s0, sl_=sl_: e.tensor_copy(
                                A.rot[d0:d0 + 32, :], dst[s0:s0 + 32, sl_]),
                                wait=wz if not cps else [], sig="V"))
                        t_m2 = s.op("vector", lambda e, sl_=sl_: e.tensor_mul(
                            A.rot[:], A.rot[:], A.sinb[:, sl_]), wait=[cps[-1]], sig="V")
                        t_m1 = s.op("vector", lambda e, dst=dst, sl_=sl_: e.tensor_mul(
                            dst[:, sl_], dst[:, sl_], A.cosb[:, sl_]),
                            wait=[cps[-1]], sig="V")
                        acc = A.qT if jn == "qr" else A.kT
                        base_tok = ev_toks[("qc" if jn == "qr" else "kc", tblk)]
                        rot_free = s.op("vector", lambda e, dst=dst, acc=acc, sl_=sl_: e.scalar_tensor_tensor(
                            dst[:, sl_], dst[:, sl_], 1.0, A.rot[:],
                            ALU.mult, ALU.add), wait=[t_m1, t_m2], sig="V")
                        s.op("vector", lambda e, dst=dst, acc=acc, sl_=sl_: e.tensor_add(
                            acc[:, sl_], acc[:, sl_], dst[:, sl_]),
                            wait=[rot_free, base_tok], sig="V")
                if jn == "v":
                    vT_done = s.tok("V")
                    for g in range(2):
                        w = [vT_done, P.setup_tok] + war
                        if bank_free.get(("tb", g)) is not None:
                            w.append(bank_free[("tb", g)])
                        tok = None
                        for j in range(8):
                            tt = 8 * g + j
                            tok = s.op("tensor", lambda e, g=g, j=j, tt=tt: e.transpose(
                                tbk[g][:, 128 * j:128 * j + 128],
                                A.tmpT[:, 128 * tt:128 * tt + 128], P.ident[:]),
                                wait=w, sig="T")
                            w = []
                        t_ev2 = s.op("vector", lambda e, g=g: e.tensor_copy(
                            A.vc[:, 1024 * g:1024 * g + 1024], tbk[g][:]),
                            wait=[tok], sig="V")
                        bank_free[("tb", g)] = t_ev2
            tab_tok[oc] = s.tok("V")
            v_done = s.tok("V")
            q_done = s.tok("V")
            proj_T_done[oc] = s.tok("T")
            for h2 in range(2):
                _attn_head(s, P, A, C, oc, 64 * h2, q_done, v_done, None,
                           causal=False, stt_scalar=None)
            attn_T_done[oc] = s.tok("T")
    attn_done = s.toks("T", "V", "SC")
    P.dbg = {"d_ctq": (A.ctq.ap(), BF16), "d_ctkv": (A.ctkv.ap(), BF16),
             "d_qT": (A.qT.ap(), BF16), "d_kT": (A.kT.ap(), BF16),
             "d_vc": (A.vc.ap(), BF16), "d_oT": (A.oT.ap(), BF16),
             "d_Psb": (A.Psb.ap(), BF16), "d_rs": (A.rs.ap(), F32)}
    _emit_out_proj_streamed(s, nc, es, P, A, prol, P.W["b1_wo"], attn_done,
                            A.borow, t_bo, "b1")


def emit_attn_b0(s, nc, es, P):
    DN = float(64.0 ** -0.25)
    RATIO = 0.0625
    EPS = 1e-4
    A = NS()
    with ExitStack() as es1:
        A.scr = es1.enter_context(nc.sbuf_tensor("t_scr", [128, D], F32))
        A.xn = es1.enter_context(nc.sbuf_tensor("t_xn", [128, 2 * D], BF16))
        with ExitStack() as esp:
            A.psumb = [esp.enter_context(nc.psum_tensor(f"t_psb{i}", [128, 1024], BF16))
                       for i in range(2)]
            prol = emit_prologue(s, P, A)
    gsafe0 = s.toks("T", "V", "SC")
    A.kT = es.enter_context(nc.sbuf_tensor("t_kT", [128, NDC * T], BF16))
    A.oT = es.enter_context(nc.sbuf_tensor("t_oT", [128, NDC * T], BF16))
    A.wk = [es.enter_context(nc.sbuf_tensor(f"t_wk{i}", [128, NDC * 128], BF16))
            for i in range(2)]
    A.wq = [es.enter_context(nc.sbuf_tensor(f"t_wq{i}", [128, NDC * 128], BF16))
            for i in range(2)]
    A.wv = [es.enter_context(nc.sbuf_tensor(f"t_wv{i}", [128, NDC * 128], BF16))
            for i in range(2)]
    A.ptsb = es.enter_context(nc.sbuf_tensor("t_ptsb", [128, 256], BF16))
    A.onescol = es.enter_context(nc.sbuf_tensor("t_onescol", [128, 1], BF16))
    A.qT = es.enter_context(nc.sbuf_tensor("t_qT", [128, T], BF16))
    A.sqc = es.enter_context(nc.sbuf_tensor("t_sqc", [128, T], BF16))
    A.vca = es.enter_context(nc.sbuf_tensor("t_vca", [128, NTT * 130], BF16))
    A.vTs = es.enter_context(nc.sbuf_tensor("t_vTs", [128, T], BF16))
    A.diagk = es.enter_context(nc.sbuf_tensor("t_diagk", [128, 256], F32))
    A.negdm = es.enter_context(nc.sbuf_tensor("t_negdm", [128, 256], F32))
    A.gmax = es.enter_context(nc.sbuf_tensor("t_gmax", [128, 8], F32))
    A.kp = es.enter_context(nc.sbuf_tensor("t_kp", [128, 256], BF16))
    A.qp = es.enter_context(nc.sbuf_tensor("t_qp", [128, 256], BF16))
    A.expsb = es.enter_context(nc.sbuf_tensor("t_expsb", [128, 256], F32))
    A.qpT = es.enter_context(nc.sbuf_tensor("t_qpT", [128, 256], BF16))
    A.ctx = es.enter_context(nc.sbuf_tensor("t_ctx", [128, 130], BF16))
    A.osb = es.enter_context(nc.sbuf_tensor("t_osb", [128, 64], BF16))
    A.ccs = es.enter_context(nc.sbuf_tensor("t_ccs", [128, 128], F32))
    A.borow = es.enter_context(nc.sbuf_tensor("t_borow", [1, D], BF16))
    cc_in = nc.dram_tensor("b0cc_in", [128, 128], F32)
    cc_out = nc.dram_tensor("b0cc_out", [128, 128], F32)
    t_pt = s.op("gpsimd", lambda e: e.dma_start(out=A.ptsb[:], in_=P.W["b0_pt"]),
                wait=gsafe0, sig="d_b0pt", n=16)
    t_bo = s.op("gpsimd", lambda e: e.dma_start(out=A.borow[:], in_=P.W["b0_bo"]),
                wait=gsafe0, sig="d_b0bo", n=16)
    t_vca1 = s.op("gpsimd", lambda e: e.memset(A.vca[:], 1.0), wait=gsafe0, sig="P")
    t_oc1 = s.op("gpsimd", lambda e: e.memset(A.onescol[:], 1.0), wait=gsafe0, sig="P")
    t_gm0 = s.op("vector", lambda e: e.memset(A.gmax[:, 0:1], -3e38),
                 wait=gsafe0, sig="V")
    proj_T_done = {}
    with ExitStack() as esc:
        sbk = [esc.enter_context(nc.psum_tensor(f"t_sp{i}", [128, 512], F32))
               for i in range(2)]
        dgb = esc.enter_context(nc.psum_tensor("t_dg", [128, 512], F32))
        ctxb = [esc.enter_context(nc.psum_tensor(f"t_cx{i}", [128, 512], F32))
                for i in range(2)]
        outb = esc.enter_context(nc.psum_tensor("t_ob", [128, 512], F32))
        tbk = [esc.enter_context(nc.psum_tensor(f"t_tb{i}", [128, 1024], BF16))
               for i in range(2)]
        bank_free = {}
        t_gmax = t_gm0
        t_diag_prev = None
        for oc in range(8):
            par = oc % 2
            w = list(gsafe0) if oc < 2 else [proj_T_done[oc - 2]]
            t_wk = s.op("gpsimd", lambda e, oc=oc, par=par: e.dma_start(
                out=A.wk[par].ap().rearrange("p (c o) -> p c o", c=NDC),
                in_=P.W["b0_wk"][:, 128 * oc:128 * oc + 128].rearrange(
                    "(c p) o -> p c o", p=128)), wait=w, sig=f"d_b0wk_{par}", n=16)
            wks = A.wk[par].ap().rearrange("p (c o) -> p c o", c=NDC)
            for tblk in range(4):
                bi = tblk % 2
                bank = sbk[bi]
                w = [t_wk, prol.done_V]
                if bank_free.get(("s", bi)) is not None:
                    w.append(bank_free[("s", bi)])
                tok = None
                for c in range(NDC):
                    tok = s.op("tensor", lambda e, wks=wks, c=c, tblk=tblk, bank=bank: e.matmul(
                        bank[:], wks[:, c, :], xnT_slice(P, c, 512 * tblk, 512),
                        start=(c == 0), stop=(c == NDC - 1)), wait=w, sig="T")
                    w = []
                t_ev = s.op("scalar", lambda e, oc=oc, tblk=tblk, bank=bank: e.mul(
                    A.kT[:, T * oc + 512 * tblk: T * oc + 512 * tblk + 512],
                    bank[:], DN), wait=[tok], sig="SC")
                bank_free[("s", bi)] = t_ev
            k_done = s.tok("SC")
            t_sq = s.op("scalar", lambda e, oc=oc: e.activation(
                out=A.sqc[:], in_=A.kT[:, T * oc:T * oc + T], func=AF.Square,
                bias=0.0, scale=1.0), wait=[k_done], sig="SC")
            for h2 in range(2):
                h = 2 * oc + h2
                r0 = 64 * h2
                for tt in range(NTT):
                    bi = tt % 2
                    bank = sbk[bi]
                    w = [k_done, t_pt]
                    if bank_free.get(("s", bi)) is not None:
                        w.append(bank_free[("s", bi)])
                    t_dd = s.op("tensor", lambda e, oc=oc, r0=r0, tt=tt, bank=bank: e.matmul(
                        bank[:, 0:256],
                        A.kT[r0:r0 + 64, T * oc + 128 * tt: T * oc + 128 * tt + 128],
                        A.ptsb[r0:r0 + 64, :], start=True, stop=True), wait=w, sig="T")
                    t_mx = s.op("vector", lambda e, bank=bank: e.reduce_max(
                        A.gmax[:, 1:2], bank[:, 0:256], axis=AX),
                        wait=[t_dd, t_gmax], sig="V")
                    bank_free[("s", bi)] = t_mx
                    t_gmax = s.op("vector", lambda e: e.tensor_max(
                        A.gmax[:, 0:1], A.gmax[:, 0:1], A.gmax[:, 1:2]),
                        wait=[t_mx], sig="V")
                    w = [t_sq]
                    if t_diag_prev is not None:
                        w.append(t_diag_prev)
                    t_dg = s.op("tensor", lambda e, oc=oc, r0=r0, tt=tt: e.matmul(
                        dgb[:, 0:1],
                        A.sqc[r0:r0 + 64, 128 * tt:128 * tt + 128],
                        A.onescol[r0:r0 + 64, 0:1],
                        start=True, stop=True), wait=w + [t_oc1], sig="T")
                    t_diag_prev = s.op("scalar", lambda e, h=h, tt=tt: e.mul(
                        A.diagk[:, 16 * h + tt:16 * h + tt + 1], dgb[:, 0:1], 0.5),
                        wait=[t_dg], sig="SC")
            proj_T_done[oc] = s.tok("T")
        t_c1 = s.op("vector", lambda e: e.tensor_copy(A.kp[:, 0:1], A.gmax[:, 0:1]),
                    wait=[t_gmax], sig="V")
        t_c2 = s.op("tensor", lambda e: e.transpose(
            tbk[0][0:1, 0:128], A.kp[:, 0:1], P.ident[:]),
            wait=[t_c1, P.setup_tok], sig="T")
        t_mz = s.op("vector", lambda e: e.memset(A.ccs[:], -3e38), wait=[t_gmax], sig="V")
        t_c8 = s.op("vector", lambda e: e.reduce_max(
            A.ccs[0:1, 0:1], tbk[0][0:1, 0:128], axis=AX),
            wait=[t_c2, t_mz], sig="V")
        t_d1 = s.op("gpsimd", lambda e: e.dma_start(out=cc_in.ap(), in_=A.ccs[:, 0:128]),
                    wait=[t_c8], sig="d_cc1", n=16)
        t_cc = s.op("gpsimd", lambda e: e.collective_compute(
            "AllReduce", ALU.max, replica_groups=[list(range(8))],
            ins=[cc_in.ap().opt()], outs=[cc_out.ap().opt()]),
            wait=[t_d1], sig="ccv", n=1)
        t_d2 = s.op("gpsimd", lambda e: e.dma_start(out=A.ccs[:, 0:128], in_=cc_out.ap()),
                    wait=[t_cc], sig="d_cc2", n=16)
        t_on = s.op("vector", lambda e: e.memset(A.expsb[0:1, 0:128], 1.0),
                    wait=[t_c8], sig="V")
        t_bc = s.op("tensor", lambda e: e.matmul(
            dgb[:, 1:2], A.expsb[0:1, 0:128], A.ccs[0:1, 0:1],
            start=True, stop=True), wait=[t_d2, t_on, t_diag_prev], sig="T")
        t_n1 = s.op("vector", lambda e: e.tensor_copy(A.gmax[:, 2:3], dgb[:, 1:2]),
                    wait=[t_bc], sig="V")
        t_n2 = s.op("vector", lambda e: e.tensor_scalar(
            A.negdm[:], A.diagk[:], A.gmax[:, 2:3], -1.0, ALU.add, ALU.mult),
            wait=[t_n1], sig="V")
        negdm_done = t_n2
        attn_T_done = {}
        for oc in range(8):
            par = oc % 2
            w = [negdm_done] if oc < 2 else [attn_T_done[oc - 2]]
            t_wq = s.op("gpsimd", lambda e, oc=oc, par=par: e.dma_start(
                out=A.wq[par].ap().rearrange("p (c o) -> p c o", c=NDC),
                in_=P.W["b0_wq"][:, 128 * oc:128 * oc + 128].rearrange(
                    "(c p) o -> p c o", p=128)), wait=w, sig=f"d_b0wq_{par}", n=16)
            t_wv = s.op("gpsimd", lambda e, oc=oc, par=par: e.dma_start(
                out=A.wv[par].ap().rearrange("p (c o) -> p c o", c=NDC),
                in_=P.W["b0_wv"][:, 128 * oc:128 * oc + 128].rearrange(
                    "(c p) o -> p c o", p=128)), wait=w, sig=f"d_b0wv_{par}", n=16)
            wqs = A.wq[par].ap().rearrange("p (c o) -> p c o", c=NDC)
            wvs = A.wv[par].ap().rearrange("p (c o) -> p c o", c=NDC)
            war = [attn_T_done[oc - 1]] if oc >= 1 else []
            for src_i, (wview, wtok, kind) in enumerate(((wqs, t_wq, "q"), (wvs, t_wv, "v"))):
                for tblk in range(4):
                    bi = tblk % 2
                    bank = sbk[bi]
                    w = [wtok, prol.done_V]
                    if bank_free.get(("s", bi)) is not None:
                        w.append(bank_free[("s", bi)])
                    tok = None
                    for c in range(NDC):
                        tok = s.op("tensor", lambda e, wview=wview, c=c, tblk=tblk, bank=bank: e.matmul(
                            bank[:], wview[:, c, :], xnT_slice(P, c, 512 * tblk, 512),
                            start=(c == 0), stop=(c == NDC - 1)), wait=w, sig="T")
                        w = []
                    if kind == "q":
                        t_ev = s.op("scalar", lambda e, tblk=tblk, bank=bank: e.mul(
                            A.qT[:, 512 * tblk:512 * tblk + 512], bank[:], DN),
                            wait=[tok] + war, sig="SC")
                    else:
                        t_ev = s.op("vector", lambda e, tblk=tblk, bank=bank: e.tensor_copy(
                            A.vTs[:, 512 * tblk:512 * tblk + 512], bank[:]),
                            wait=[tok] + war, sig="V")
                    bank_free[("s", bi)] = t_ev
            q_done = s.tok("SC")
            vT_done = s.tok("V")
            t_sq = s.op("scalar", lambda e: e.activation(
                out=A.sqc[:], in_=A.qT[:], func=AF.Square, bias=0.0, scale=1.0),
                wait=[q_done], sig="SC")
            for g in range(2):
                w = [vT_done, P.setup_tok, t_vca1] + war
                if bank_free.get(("tb", g)) is not None:
                    w.append(bank_free[("tb", g)])
                tok = None
                for j in range(8):
                    tt = 8 * g + j
                    tok = s.op("tensor", lambda e, g=g, j=j, tt=tt: e.transpose(
                        tbk[g][:, 128 * j:128 * j + 128],
                        A.vTs[:, 128 * tt:128 * tt + 128], P.ident[:]),
                        wait=w, sig="T")
                    w = []
                tks = []
                for j in range(8):
                    tt = 8 * g + j
                    tks.append(s.op("vector", lambda e, g=g, j=j, tt=tt: e.tensor_copy(
                        A.vca[:, 130 * tt:130 * tt + 64],
                        tbk[g][:, 128 * j:128 * j + 64]), wait=[tok], sig="V"))
                    tks.append(s.op("vector", lambda e, g=g, j=j, tt=tt: e.tensor_copy(
                        A.vca[:, 130 * tt + 65:130 * tt + 129],
                        tbk[g][:, 128 * j + 64:128 * j + 128]), wait=[tok], sig="V"))
                bank_free[("tb", g)] = tks[-1]
            v_done = s.tok("V")
            for h2 in range(2):
                h = 2 * oc + h2
                r0 = 64 * h2
                ctx_mm = None
                prev_kp_ev = None
                for tt in range(NTT):
                    bi = tt % 2
                    bank = sbk[bi]
                    w = [negdm_done, v_done]
                    if bank_free.get(("s", bi)) is not None:
                        w.append(bank_free[("s", bi)])
                    t_dd = s.op("tensor", lambda e, oc=oc, r0=r0, tt=tt, bank=bank: e.matmul(
                        bank[:, 0:256],
                        A.kT[r0:r0 + 64, T * oc + 128 * tt: T * oc + 128 * tt + 128],
                        A.ptsb[r0:r0 + 64, :], start=True, stop=True), wait=w, sig="T")
                    wv_ = [t_dd]
                    if prev_kp_ev is not None:
                        wv_.append(prev_kp_ev)
                    t_ex = s.op("scalar", lambda e, h=h, tt=tt, bank=bank: e.activation(
                        out=A.expsb[:, 0:256], in_=bank[:, 0:256], func=AF.Exp,
                        bias=A.negdm[:, 16 * h + tt:16 * h + tt + 1], scale=1.0),
                        wait=wv_, sig="SC")
                    bank_free[("s", bi)] = t_ex
                    t_kp = s.op("vector", lambda e: e.tensor_scalar(
                        A.kp[:], A.expsb[:, 0:256], RATIO, RATIO * EPS,
                        ALU.mult, ALU.add), wait=[t_ex], sig="V")
                    for mc in range(2):
                        w = [t_kp]
                        if tt == 0 and bank_free.get(("cx", mc)) is not None:
                            w.append(bank_free[("cx", mc)])
                        ctx_mm = s.op("tensor", lambda e, mc=mc, tt=tt, h2=h2: e.matmul(
                            ctxb[mc][:, 0:65], A.kp[:, 128 * mc:128 * mc + 128],
                            A.vca[:, 130 * tt + 65 * h2: 130 * tt + 65 * h2 + 65],
                            start=(tt == 0), stop=(tt == NTT - 1)), wait=w, sig="T")
                    prev_kp_ev = ctx_mm
                tcx = []
                for mc in range(2):
                    tcx.append(s.op("vector", lambda e, mc=mc: e.tensor_copy(
                        A.ctx[:, 65 * mc:65 * mc + 65], ctxb[mc][:, 0:65]),
                        wait=[ctx_mm], sig="V"))
                    bank_free[("cx", mc)] = tcx[-1]
                ctx_done = tcx[-1]
                prev_q = None
                prev_qp = None
                for tt in range(NTT):
                    bi = tt % 2
                    bank = sbk[bi]
                    w = [t_sq, q_done]
                    if bank_free.get(("s", bi)) is not None:
                        w.append(bank_free[("s", bi)])
                    t_dd = s.op("tensor", lambda e, r0=r0, tt=tt, bank=bank: e.matmul(
                        bank[:, 0:256],
                        A.qT[r0:r0 + 64, 128 * tt:128 * tt + 128],
                        A.ptsb[r0:r0 + 64, :], start=True, stop=True), wait=w, sig="T")
                    wdg = [t_sq, negdm_done]
                    if prev_q is not None:
                        wdg.append(prev_q)
                    t_dg = s.op("tensor", lambda e, r0=r0, tt=tt: e.matmul(
                        dgb[:, 0:1], A.sqc[r0:r0 + 64, 128 * tt:128 * tt + 128],
                        A.onescol[r0:r0 + 64, 0:1],
                        start=True, stop=True), wait=wdg, sig="T")
                    t_m = s.op("vector", lambda e, bank=bank: e.reduce_max(
                        A.gmax[:, 3:4], bank[:, 0:256], axis=AX), wait=[t_dd], sig="V")
                    t_d5 = s.op("scalar", lambda e: e.mul(
                        A.gmax[:, 4:5], dgb[:, 0:1], 0.5), wait=[t_dg], sig="SC")
                    t_b = s.op("vector", lambda e: e.tensor_add(
                        A.gmax[:, 5:6], A.gmax[:, 3:4], A.gmax[:, 4:5]),
                        wait=[t_m, t_d5], sig="V")
                    t_bn = s.op("vector", lambda e: e.tensor_scalar_mul(
                        A.gmax[:, 6:7], A.gmax[:, 5:6], -1.0), wait=[t_b], sig="V")
                    wex = [t_bn]
                    if prev_qp is not None:
                        wex.append(prev_qp)
                    t_ex = s.op("scalar", lambda e, bank=bank: e.activation(
                        out=A.expsb[:, 0:256], in_=bank[:, 0:256], func=AF.Exp,
                        bias=A.gmax[:, 6:7], scale=1.0), wait=wex, sig="SC")
                    bank_free[("s", bi)] = t_ex
                    t_qp = s.op("vector", lambda e: e.tensor_scalar(
                        A.qp[:], A.expsb[:, 0:256], RATIO, RATIO * EPS,
                        ALU.mult, ALU.add), wait=[t_ex], sig="V")
                    prev_qp = t_qp
                    w = [t_qp, P.setup_tok]
                    if bank_free.get(("tb", 0)) is not None:
                        w.append(bank_free[("tb", 0)])
                    tok = None
                    for mc in range(2):
                        tok = s.op("tensor", lambda e, mc=mc: e.transpose(
                            tbk[0][:, 128 * mc:128 * mc + 128],
                            A.qp[:, 128 * mc:128 * mc + 128], P.ident[:]),
                            wait=w, sig="T")
                        w = []
                    t_qpT = s.op("vector", lambda e: e.tensor_copy(
                        A.qpT[:], tbk[0][:, 0:256]), wait=[tok], sig="V")
                    bank_free[("tb", 0)] = t_qpT
                    t_om = None
                    for mc in range(2):
                        w = [t_qpT, ctx_done]
                        if mc == 0 and bank_free.get(("ob",)) is not None:
                            w.append(bank_free[("ob",)])
                        t_om = s.op("tensor", lambda e, mc=mc: e.matmul(
                            outb[:, 0:65], A.qpT[:, 128 * mc:128 * mc + 128],
                            A.ctx[:, 65 * mc:65 * mc + 65],
                            start=(mc == 0), stop=(mc == 1)), wait=w, sig="T")
                    t_rc = s.op("vector", lambda e: e.reciprocal(
                        A.gmax[:, 7:8], outb[:, 64:65]), wait=[t_om], sig="V")
                    t_oev = s.op("vector", lambda e: e.tensor_scalar_mul(
                        A.osb[:, 0:64], outb[:, 0:64], A.gmax[:, 7:8]),
                        wait=[t_rc], sig="V")
                    bank_free[("ob",)] = t_oev
                    prev_q = t_oev
                    w = [t_oev, P.setup_tok]
                    if bank_free.get(("tb", 1)) is not None:
                        w.append(bank_free[("tb", 1)])
                    t_otr = s.op("tensor", lambda e: e.transpose(
                        tbk[1][0:64, 0:128], A.osb[:, 0:64], P.ident[:]),
                        wait=w, sig="T")
                    t_ofin = s.op("vector", lambda e, r0=r0, oc=oc, tt=tt: e.tensor_copy(
                        A.oT[r0:r0 + 64, T * oc + 128 * tt: T * oc + 128 * tt + 128],
                        tbk[1][0:64, 0:128]), wait=[t_otr], sig="V")
                    bank_free[("tb", 1)] = t_ofin
            attn_T_done[oc] = s.tok("T")
    attn_done = s.toks("T", "V", "SC")
    _emit_out_proj_streamed(s, nc, es, P, A, prol, P.W["b0_wo"], attn_done,
                            A.borow, t_bo, "b0")


def emit_ffn_sub(s, nc, es, P, kind):
    A = NS()
    A.gw = es.enter_context(nc.sbuf_tensor("a_gw", [128, NTT * 8], F32))
    with ExitStack() as es1:
        A.scr = es1.enter_context(nc.sbuf_tensor("a_scr", [128, D], F32))
        A.xn = es1.enter_context(nc.sbuf_tensor("a_xn", [128, 2 * D], BF16))
        with ExitStack() as esp:
            A.psumb = [esp.enter_context(nc.psum_tensor(f"a_psb{i}", [128, 1024], BF16))
                       for i in range(2)]
            prol = emit_prologue(s, P, A)
        A.psumf = [es.enter_context(nc.psum_tensor(f"a_psf{i}", [128, 512], F32))
                   for i in range(8)]
        A.gws = es1.enter_context(nc.sbuf_tensor("a_gws", [128, 64], BF16))
        A.gst = es1.enter_context(nc.sbuf_tensor("a_gst", [128, NTT * 8], F32))
        A.glog = es1.enter_context(nc.sbuf_tensor("a_glog", [128, NTT * 8], F32))
        A.gtop = es1.enter_context(nc.sbuf_tensor("a_gtop", [128, NTT * 8], F32))
        A.gexp = es1.enter_context(nc.sbuf_tensor("a_gexp", [128, NTT * 8], F32))
        A.goh2 = es1.enter_context(nc.sbuf_tensor("a_goh2", [128, 8], F32))
        gw_name = "f0_gw" if kind == "ffn0" else "f1_gw"
        if _DEBUG_STAGE == "prologue":
            P.dbgA = A
            return
        gate_done, gbank = emit_moe_gating(s, P, A, prol, gw_name,
                                           topk=1 if kind == "ffn0" else 2)
    gsafe = s.toks("T", "V", "SC")
    P.dbgA = A
    if _DEBUG_STAGE in ("prologue", "gating"):
        return
    if kind == "ffn0":
        A.b1 = es.enter_context(nc.sbuf_tensor("a_b1", [128, 256], F32))
        A.b2row = es.enter_context(nc.sbuf_tensor("a_b2row", [1, D], BF16))
        A.w1sb = es.enter_context(nc.sbuf_tensor("a_w1sb", [128, NDC * 4096], BF16))
        A.w2sb = [es.enter_context(nc.sbuf_tensor(f"a_w2sb{i}", [128, 2 * D], BF16))
                  for i in range(2)]
        A.hT = es.enter_context(nc.sbuf_tensor("a_hT", [128, 32 * 512], BF16))
        emit_ffn0(s, P, A, prol, gate_done, gbank, gsafe)
    else:
        A.bg = es.enter_context(nc.sbuf_tensor("a_bg", [128, 128], F32))
        A.bu = es.enter_context(nc.sbuf_tensor("a_bu", [128, 128], F32))
        A.borow = es.enter_context(nc.sbuf_tensor("a_borow", [1, D], BF16))
        A.wgsb = es.enter_context(nc.sbuf_tensor("a_wgsb", [128, NDC * 2048], BF16))
        A.wusb = es.enter_context(nc.sbuf_tensor("a_wusb", [128, NDC * 2048], BF16))
        A.wosb = [es.enter_context(nc.sbuf_tensor(f"a_wosb{i}", [128, 2 * D], BF16))
                  for i in range(2)]
        A.gscr = es.enter_context(nc.sbuf_tensor("a_gscr", [128, 1024], BF16))
        A.hT = es.enter_context(nc.sbuf_tensor("a_hT1", [128, 16 * 512], BF16))
        emit_ffn1(s, P, A, prol, gate_done, gbank, gsafe)


WEIGHT_SPECS = {
    "f0_gw": ([128, 64], ("ffn0a", "ffn0b")),
    "f0_b1": ([128, 256], ("ffn0a", "ffn0b")),
    "f0_b2": ([1, 8 * D], ("ffn0a", "ffn0b")),
    "f0_w1": ([8, 1024, 4096], ("ffn0a", "ffn0b")),
    "f0_w2": ([8, 4096, 1024], ("ffn0a", "ffn0b")),
    "f1_gw": ([128, 64], ("ffn1a", "ffn1b")),
    "f1_bg": ([128, 128], ("ffn1a", "ffn1b")),
    "f1_bu": ([128, 128], ("ffn1a", "ffn1b")),
    "f1_bo": ([1, 8 * D], ("ffn1a", "ffn1b")),
    "f1_wg": ([8, 1024, 2048], ("ffn1a", "ffn1b")),
    "f1_wu": ([8, 1024, 2048], ("ffn1a", "ffn1b")),
    "f1_wo": ([8, 2048, 1024], ("ffn1a", "ffn1b")),
    "a0_wq": ([1024, 1024], ("attn_a0",)),
    "a0_wk": ([1024, 1024], ("attn_a0",)),
    "a0_wv": ([1024, 1024], ("attn_a0",)),
    "a0_wo": ([1024, 1024], ("attn_a0",)),
    "a1_wq": ([1024, 1024], ("attn_a1",)),
    "a1_wk": ([1024, 1024], ("attn_a1",)),
    "a1_wv": ([1024, 1024], ("attn_a1",)),
    "a1_wo": ([1024, 1024], ("attn_a1",)),
    "a1_qb": ([128, 8], ("attn_a1",)),
    "a1_kb": ([128, 8], ("attn_a1",)),
    "a1_vb": ([128, 8], ("attn_a1",)),
    "a1_pb": ([1, D], ("attn_a1",)),
    "a1_cos": ([128, T], ("attn_a1",)),
    "a1_sin": ([128, T], ("attn_a1",)),
    "b1_wdq": ([1024, 512], ("attn_b1",)),
    "b1_wdkv": ([1024, 512], ("attn_b1",)),
    "b1_wuq": ([512, 1024], ("attn_b1",)),
    "b1_wqr": ([512, 1024], ("attn_b1",)),
    "b1_wuk": ([512, 1024], ("attn_b1",)),
    "b1_wkr": ([1024, 1024], ("attn_b1",)),
    "b1_wuv": ([512, 1024], ("attn_b1",)),
    "b1_wo": ([1024, 1024], ("attn_b1",)),
    "b1_dqb": ([128, 4], ("attn_b1",)),
    "b1_dkvb": ([128, 4], ("attn_b1",)),
    "b1_ub": ([128, 32], ("attn_b1",)),
    "b1_krb": ([128, 8], ("attn_b1",)),
    "b1_bo": ([1, D], ("attn_b1",)),
    "b1_cos": ([8, 128, T], ("attn_b1",)),
    "b1_sin": ([8, 128, T], ("attn_b1",)),
    "b0_wq": ([1024, 1024], ("attn_b0",)),
    "b0_wk": ([1024, 1024], ("attn_b0",)),
    "b0_wv": ([1024, 1024], ("attn_b0",)),
    "b0_wo": ([1024, 1024], ("attn_b0",)),
    "b0_bo": ([1, D], ("attn_b0",)),
    "b0_pt": ([128, 256], ("attn_b0",)),
}


def build_program(sub_ids):
    nc = bass.Bass("TRN2", target_bir_lowering=False)
    xin = nc.dram_tensor("x_in", [T, D], F32, kind="ExternalInput").ap()
    xout = nc.dram_tensor("x_out", [T, D], F32, kind="ExternalOutput").ap()
    needed = [n for n, (shp, subs) in WEIGHT_SPECS.items()
              if any(sid in subs for sid in sub_ids)]
    P = NS()
    P.W = {n: nc.dram_tensor(n, WEIGHT_SPECS[n][0], F32, kind="ExternalInput").ap()
           for n in needed}
    s = Sched()
    es_p = ExitStack()
    P.x_sb = es_p.enter_context(nc.sbuf_tensor("x_sb", [128, NTT * D], F32))
    P.xnT = es_p.enter_context(nc.sbuf_tensor("xnT", [128, NDC * T], BF16))
    P.ident = es_p.enter_context(nc.sbuf_tensor("ident", [128, 128], BF16))
    P.ones_row = es_p.enter_context(nc.sbuf_tensor("ones_row", [1, 128], BF16))
    P.eps_t = es_p.enter_context(nc.sbuf_tensor("eps_t", [128, 1], F32))
    P.stat = es_p.enter_context(nc.sbuf_tensor("stat", [128, NTT * 8], F32))
    emit_setup(s, P, xin)
    ncp = NCProxy(nc)
    conv = [n for n in CONVERT_BF16 if n in P.W]
    if conv:
        emit_bf16_convert(s, ncp, P, conv)
        s.fence()
    for sid in sub_ids:
        with ExitStack() as es:
            if sid in ("ffn0a", "ffn0b"):
                emit_ffn_sub(s, ncp, es, P, "ffn0")
            elif sid in ("ffn1a", "ffn1b"):
                emit_ffn_sub(s, ncp, es, P, "ffn1")
            elif sid == "attn_a0":
                emit_attn_a0(s, ncp, es, P)
            elif sid == "attn_a1":
                emit_attn_a1(s, ncp, es, P)
            elif sid == "attn_b1":
                emit_attn_b1(s, ncp, es, P)
            elif sid == "attn_b0":
                emit_attn_b0(s, ncp, es, P)
            else:
                raise KeyError(sid)
            s.fence()
    s.op("sync", lambda e: e.dma_start(
        out=xout.rearrange("(tt p) d -> p tt d", p=128),
        in_=P.x_sb.ap().rearrange("p (tt d) -> p tt d", tt=NTT)), sig="d_out", n=16)
    if _DEBUG_DUMP:
        dumps = {"d_xnT": (P.xnT.ap(), BF16)}
        dumps.update(getattr(P, "dbg", {}))
        A = getattr(P, "dbgA", None)
        if A is not None and hasattr(A, "gw") and _DEBUG_STAGE != "prologue":
            dumps["d_gw"] = (A.gw.ap(), F32)
        if A is not None and hasattr(A, "hT") and _DEBUG_STAGE == "full":
            dumps["d_hT"] = (A.hT.ap(), BF16)
            dumps["d_w1"] = (A.w1sb.ap(), BF16)
        for nm, (ap, dt) in dumps.items():
            od = nc.dram_tensor(nm, list(ap.shape), dt, kind="ExternalOutput").ap()
            s.op("sync", lambda e, od=od, ap=ap: e.dma_start(out=od, in_=ap),
                 sig="d_out", n=16)
    final = s.tok("d_out")
    s.op("gpsimd", lambda e: e.engine_nop(), wait=[final])
    replay(nc, s, es_p)
    P._es = es_p
    return nc, needed


def _prep_weights(inp):
    W = {}
    W["f0_gw"] = np.ascontiguousarray(
        inp["L0_ffn_gate_w"].reshape(8, 128, 8).transpose(1, 0, 2).reshape(128, 64))
    W["f0_b1"] = np.ascontiguousarray(
        inp["L0_ffn_b1"].reshape(8, 32, 128).transpose(2, 0, 1).reshape(128, 256))
    W["f0_b2"] = np.ascontiguousarray(inp["L0_ffn_b2"].reshape(1, 8 * D))
    W["f0_w1"] = np.ascontiguousarray(inp["L0_ffn_w1"])
    W["f0_w2"] = np.ascontiguousarray(inp["L0_ffn_w2"])
    W["f1_gw"] = np.ascontiguousarray(
        inp["L1_ffn_W_router"].reshape(8, 128, 8).transpose(1, 0, 2).reshape(128, 64))
    W["f1_bg"] = np.ascontiguousarray(
        inp["L1_ffn_b_gate"].reshape(8, 16, 128).transpose(2, 0, 1).reshape(128, 128))
    W["f1_bu"] = np.ascontiguousarray(
        inp["L1_ffn_b_up"].reshape(8, 16, 128).transpose(2, 0, 1).reshape(128, 128))
    W["f1_bo"] = np.ascontiguousarray(inp["L1_ffn_b_out"].reshape(1, 8 * D))
    W["f1_wg"] = np.ascontiguousarray(inp["L1_ffn_W_gate"])
    W["f1_wu"] = np.ascontiguousarray(inp["L1_ffn_W_up"])
    W["f1_wo"] = np.ascontiguousarray(inp["L1_ffn_W_out"])
    W["a0_wq"] = np.ascontiguousarray(np.asarray(inp["L0_attn_a_Wq"]).T)
    W["a0_wk"] = np.ascontiguousarray(np.asarray(inp["L0_attn_a_Wk"]).T)
    W["a0_wv"] = np.ascontiguousarray(np.asarray(inp["L0_attn_a_Wv"]).T)
    W["a0_wo"] = np.ascontiguousarray(np.asarray(inp["L0_attn_a_Wo"]).T)
    qkv_w = np.asarray(inp["L1_attn_a_qkv_w"])
    qkv_b = np.asarray(inp["L1_attn_a_qkv_b"])
    W["a1_wq"] = np.ascontiguousarray(qkv_w[0:1024].T)
    W["a1_wk"] = np.ascontiguousarray(qkv_w[1024:2048].T)
    W["a1_wv"] = np.ascontiguousarray(qkv_w[2048:3072].T)
    W["a1_wo"] = np.ascontiguousarray(np.asarray(inp["L1_attn_a_proj_w"]).T)
    W["a1_qb"] = np.ascontiguousarray(qkv_b[0:1024].reshape(8, 128).T)
    W["a1_kb"] = np.ascontiguousarray(qkv_b[1024:2048].reshape(8, 128).T)
    W["a1_vb"] = np.ascontiguousarray(qkv_b[2048:3072].reshape(8, 128).T)
    W["a1_pb"] = np.ascontiguousarray(np.asarray(inp["L1_attn_a_proj_b"]).reshape(1, D))
    invf = (np.float32(1.0) /
            np.power(np.float32(10000.0),
                     np.arange(0, 64, 2, dtype=np.float32) / np.float32(64)))
    tpos = np.arange(T, dtype=np.float32)
    fr = tpos[:, None] * invf[None, :]
    cos32 = np.cos(fr).astype(np.float32)
    sin32 = np.sin(fr).astype(np.float32)
    ctab = np.zeros((128, T), np.float32)
    stab = np.zeros((128, T), np.float32)
    for r in range(128):
        j = r % 64
        ctab[r] = cos32[:, j % 32]
        stab[r] = (-sin32[:, j]) if j < 32 else sin32[:, j - 32]
    W["a1_cos"] = ctab
    W["a1_sin"] = stab
    g = lambda n: np.asarray(inp["L1_attn_b_" + n])
    perm = np.zeros(1024, np.int64)
    for h in range(16):
        for i in range(32):
            perm[64 * h + i] = 64 * h + 2 * i
            perm[64 * h + 32 + i] = 64 * h + 2 * i + 1
    W["b1_wdq"] = np.ascontiguousarray(g("Wd_q_w"))
    W["b1_wdkv"] = np.ascontiguousarray(g("Wd_kv_w"))
    W["b1_wuq"] = np.ascontiguousarray(g("Wu_q_w")[:, perm])
    W["b1_wqr"] = np.ascontiguousarray(g("Wq_r_w")[:, perm])
    W["b1_wuk"] = np.ascontiguousarray(g("Wu_k_w")[:, perm])
    W["b1_wkr"] = np.ascontiguousarray(g("Wk_r_w")[:, perm])
    W["b1_wuv"] = np.ascontiguousarray(g("Wu_v_w"))
    W["b1_wo"] = np.ascontiguousarray(g("Wo_w"))
    W["b1_dqb"] = np.ascontiguousarray(g("Wd_q_b").reshape(4, 128).T)
    W["b1_dkvb"] = np.ascontiguousarray(g("Wd_kv_b").reshape(4, 128).T)
    ub = np.zeros((128, 32), np.float32)
    ub[:, 0:8] = g("Wu_q_b")[perm].reshape(8, 128).T
    ub[:, 8:16] = g("Wq_r_b")[perm].reshape(8, 128).T
    ub[:, 16:24] = g("Wu_k_b")[perm].reshape(8, 128).T
    ub[:, 24:32] = g("Wu_v_b").reshape(8, 128).T
    W["b1_ub"] = ub
    W["b1_krb"] = np.ascontiguousarray(g("Wk_r_b")[perm].reshape(8, 128).T)
    W["b1_bo"] = np.ascontiguousarray(g("Wo_b").reshape(1, D))
    ids = np.arange(512, dtype=np.float32)
    theta = np.power(np.float32(10000.0), np.float32(-2.0) * ids) / np.float32(512.0)
    tpos = np.arange(T, dtype=np.float32)
    rarg = tpos[:, None] * theta[None, :]
    cosf = np.cos(rarg).astype(np.float32)
    sinf = np.sin(rarg).astype(np.float32)
    bc = np.zeros((8, 128, T), np.float32)
    bs = np.zeros((8, 128, T), np.float32)
    for oc in range(8):
        for r in range(128):
            h = 2 * oc + (1 if r >= 64 else 0)
            j = r % 64
            if j < 32:
                i = 32 * h + j
                bc[oc, r] = cosf[:, i]
                bs[oc, r] = -sinf[:, i]
            else:
                i = 32 * h + (j - 32)
                bc[oc, r] = cosf[:, i]
                bs[oc, r] = sinf[:, i]
    W["b1_cos"] = bc
    W["b1_sin"] = bs
    W["b0_wq"] = np.ascontiguousarray(inp["L0_attn_b_Wq"])
    W["b0_wk"] = np.ascontiguousarray(inp["L0_attn_b_Wk"])
    W["b0_wv"] = np.ascontiguousarray(inp["L0_attn_b_Wv"])
    W["b0_wo"] = np.ascontiguousarray(inp["L0_attn_b_Wo"])
    W["b0_bo"] = np.ascontiguousarray(np.asarray(inp["L0_attn_b_bo"]).reshape(1, D))
    pt = np.asarray(inp["L0_attn_b_P"]).T
    W["b0_pt"] = np.ascontiguousarray(np.vstack([pt, pt]))
    return {k: np.asarray(v, dtype=np.float32) for k, v in W.items()}


LAUNCH_PLAN = [["attn_a0", "ffn0a", "attn_b0", "ffn0b", "attn_a1", "ffn1a", "attn_b1", "ffn1b"]]
_DEBUG_STAGE = "full"
_DEBUG_DUMP = False
_DEBUG_NEXPERTS = 8
_DEBUG_NTB = 4
_DEBUG_NOGELU = False


def run_subs(plan, x_state, W):
    for group in plan:
        nc, needed = build_program(group)
        in_maps = []
        for c in range(8):
            m = {"x_in": np.ascontiguousarray(x_state[c])}
            for n in needed:
                m[n] = W[n]
            in_maps.append(m)
        res = run_bass_kernel_spmd(nc, in_maps, core_ids=list(range(8)))
        x_state = [np.asarray(res.results[c]["x_out"]) for c in range(8)]
    return x_state


def kernel(**inputs):
    W = _prep_weights(inputs)
    x = np.asarray(inputs["L0_attn_a_x"], dtype=np.float32)
    state = [np.ascontiguousarray(x[c]) for c in range(8)]
    state = run_subs(LAUNCH_PLAN, state, W)
    return np.stack(state, axis=0).astype(np.float32)


# version 05
# speedup vs baseline: 1.1773x; 1.1695x over previous
import numpy as np
from contextlib import ExitStack

import concourse.bass as bass
import concourse.mybir as mybir
from concourse.bass_utils import run_bass_kernel_spmd

F32 = mybir.dt.float32
BF16 = mybir.dt.bfloat16
AF = mybir.ActivationFunctionType
ALU = mybir.AluOpType
AX = mybir.AxisListType.X

T = 2048
D = 1024
NTT = 16
NDC = 8

ENGINES = ("sync", "tensor", "vector", "scalar", "gpsimd")
OWN_SEM = {"tensor": "T", "vector": "V", "scalar": "SC", "gpsimd": "P"}


class Sched:
    def __init__(self):
        self.ops = []
        from collections import defaultdict
        self.cnt = defaultdict(int)

    def op(self, eng, fn, wait=(), sig=None, n=1, note=""):
        tok = None
        w = tuple((sm, v) for (sm, v) in wait if v > 0)
        if sig is not None:
            self.cnt[sig] += n
            tok = (sig, self.cnt[sig])
        self.ops.append((eng, w, fn, (sig, n) if sig else None, note))
        return tok

    def tok(self, sem):
        return (sem, self.cnt[sem])

    def toks(self, *sems):
        return [self.tok(sm) for sm in sems]

    def fence(self):
        self.ops.append(("FENCE", dict(self.cnt), None, None, ""))


def replay(nc, s, es_p):
    names = []
    for item in s.ops:
        if item[0] == "FENCE":
            continue
        _, waits, _, sig, _ = item
        for sm, v in waits:
            if sm not in names:
                names.append(sm)
        if sig is not None and sig[0] not in names:
            names.append(sig[0])
    sems = {sm: es_p.enter_context(nc.semaphore(sm)) for sm in names}
    per_eng = {e: [] for e in ENGINES}
    pending_fence = {e: None for e in ENGINES}
    for item in s.ops:
        if item[0] == "FENCE":
            for e in ENGINES:
                pending_fence[e] = dict(item[1])
            continue
        eng = item[0]
        per_eng[eng].append((pending_fence[eng], item))
        pending_fence[eng] = None
    with nc.Block() as block:
        for ename in ENGINES:
            items = per_eng[ename]

            def body(e, ename=ename, items=items):
                waited = {}
                own = OWN_SEM.get(ename)
                for extra, (eng, waits, fn, sig, note) in items:
                    allw = []
                    if extra is not None:
                        allw.extend((sm, v) for sm, v in extra.items() if sm != own)
                    allw.extend(waits)
                    for sm, v in allw:
                        if v > 0 and waited.get(sm, 0) < v:
                            e.wait_ge(sems[sm], v)
                            waited[sm] = v
                    inst = fn(e)
                    if sig is not None:
                        inst.then_inc(sems[sig[0]], sig[1])
            getattr(block, ename)(body)


class NS:
    def __init__(self, **kw):
        self.__dict__.update(kw)


class NCProxy:
    def __init__(self, nc):
        self._nc = nc
        self._n = {}

    def _uniq(self, name):
        k = self._n.get(name, 0)
        self._n[name] = k + 1
        return name if k == 0 else f"{name}_u{k}"

    def sbuf_tensor(self, name, shape, dtype, **kw):
        return self._nc.sbuf_tensor(self._uniq(name), shape, dtype, **kw)

    def psum_tensor(self, name, shape, dtype, **kw):
        return self._nc.psum_tensor(self._uniq(name), shape, dtype, **kw)

    def dram_tensor(self, name, shape, dtype, **kw):
        return self._nc.dram_tensor(self._uniq(name), shape, dtype, **kw)

    def __getattr__(self, a):
        return getattr(self._nc, a)


def x_slice(P, tt, c0=0, n=D):
    return P.x_sb[:, D * tt + c0: D * tt + c0 + n]


def xnT_slice(P, c, t0, n):
    return P.xnT[:, T * c + t0: T * c + t0 + n]


def emit_setup(s, P, xin_ap):
    P.xload = s.op("sync", lambda e: e.dma_start(
        out=P.x_sb.ap().rearrange("p (tt d) -> p tt d", tt=NTT),
        in_=xin_ap.rearrange("(tt p) d -> p tt d", p=128)), sig="d_x", n=16)
    t_ms = s.op("gpsimd", lambda e: e.memset(P.ident[:], 1.0), sig="P")
    s.op("gpsimd", lambda e: e.affine_select(
        out=P.ident[:], in_=P.ident[:], pattern=[[-1, 128]],
        compare_op=ALU.is_equal, fill=0.0, base=0, channel_multiplier=1),
        wait=[t_ms])
    s.op("gpsimd", lambda e: e.memset(P.ones_row[:], 1.0))
    t_mf = s.op("gpsimd", lambda e: e.memset(P.identf[:], 1.0), sig="P")
    s.op("gpsimd", lambda e: e.affine_select(
        out=P.identf[:], in_=P.identf[:], pattern=[[-1, 128]],
        compare_op=ALU.is_equal, fill=0.0, base=0, channel_multiplier=1),
        wait=[t_mf])
    P.setup_tok = s.op("gpsimd", lambda e: e.memset(P.eps_t[:], 1e-5), sig="P")


def emit_prologue(s, P, A):
    xload = P.xload
    setup = P.setup_tok
    st = P.stat
    xnT3 = P.xnT.ap().rearrange("p (c t) -> p c t", c=NDC)
    ev_tok = {}
    last_bank_ev = [None, None]
    stats = {}
    for tt in range(NTT):
        xt = x_slice(P, tt)
        c = 8 * tt
        slot = tt % 2
        t_sum = s.op("vector", lambda e, xt=xt, c=c: e.reduce_sum(
            st[:, c:c + 1], xt, axis=AX), wait=[xload], sig="V")
        t_ssq = s.op("scalar", lambda e, xt=xt, c=c: e.activation(
            out=A.scr[:, 0:D], in_=xt, func=AF.Square, bias=0.0, scale=1.0,
            accum_out=st[:, c + 1:c + 2]), wait=[xload], sig="SC")
        t_mean = s.op("vector", lambda e, c=c: e.tensor_scalar_mul(
            st[:, c + 2:c + 3], st[:, c:c + 1], 1.0 / D), wait=[t_sum], sig="V")
        t_msq = s.op("vector", lambda e, c=c: e.tensor_mul(
            st[:, c + 6:c + 7], st[:, c + 2:c + 3], st[:, c + 2:c + 3]),
            wait=[t_mean], sig="V")
        t_var = s.op("vector", lambda e, c=c: e.scalar_tensor_tensor(
            st[:, c + 3:c + 4], st[:, c + 1:c + 2], 1.0 / D, st[:, c + 6:c + 7],
            ALU.mult, ALU.subtract), wait=[t_msq, t_ssq], sig="V")
        t_ln = s.op("scalar", lambda e, c=c: e.activation(
            out=st[:, c + 4:c + 5], in_=st[:, c + 3:c + 4], func=AF.Ln,
            bias=P.eps_t[:, 0:1], scale=1.0), wait=[t_var, setup], sig="SC")
        t_rstd = s.op("scalar", lambda e, c=c: e.activation(
            out=st[:, c + 5:c + 6], in_=st[:, c + 4:c + 5], func=AF.Exp,
            bias=0.0, scale=-0.5), wait=[t_ln], sig="SC")
        wapp = [t_rstd]
        if tt >= 2:
            wapp.append(ev_tok[tt - 2][0])
        t_xn = s.op("vector", lambda e, xt=xt, slot=slot, c=c: e.tensor_scalar(
            A.xn[:, D * slot:D * slot + D], xt, st[:, c + 2:c + 3],
            st[:, c + 5:c + 6], ALU.subtract, ALU.mult), wait=wapp, sig="V")
        bank = A.psumb[slot]
        tok_T = None
        for cc in range(NDC):
            w = [t_xn, setup]
            if last_bank_ev[slot] is not None:
                w.append(last_bank_ev[slot])
            tok_T = s.op("tensor", lambda e, slot=slot, cc=cc, bank=bank: e.transpose(
                bank[:, 128 * cc:128 * cc + 128],
                A.xn[:, D * slot + 128 * cc: D * slot + 128 * cc + 128],
                P.ident[:]), wait=w, sig="T")
        t_ev = s.op("vector", lambda e, tt=tt, bank=bank: e.tensor_copy(
            xnT3[:, :, 128 * tt:128 * tt + 128],
            bank.ap().rearrange("p (c t) -> p c t", c=NDC)), wait=[tok_T], sig="V")
        last_bank_ev[slot] = t_ev
        ev_tok[tt] = (tok_T, t_ev)
        stats[tt] = (t_xn, t_ssq)
    return NS(done_V=s.tok("V"), done_SC=s.tok("SC"), done_T=s.tok("T"),
              stats_done=stats)


CONVERT_BF16 = ("f0_w1", "f0_w2", "f1_wg", "f1_wu", "f1_wo")


def emit_bf16_convert(s, nc, P, names):
    P.WB = {}
    with ExitStack() as esc:
        stg = [esc.enter_context(nc.sbuf_tensor(f"cv_stg{i}", [128, 4096], BF16))
               for i in range(2)]
        st_tok = [None, None]
        for nm in names:
            shp = list(P.W[nm].shape)
            total = int(np.prod(shp))
            wb = nc.dram_tensor(nm + "_bf", shp, BF16)
            P.WB[nm] = wb.ap()
            n = total // (128 * 4096)
            flat_in = P.W[nm].rearrange(
                " ".join(f"s{i}" for i in range(len(shp))) + " -> (" +
                " ".join(f"s{i}" for i in range(len(shp))) + ")").rearrange(
                "(n p f) -> n p f", p=128, f=4096)
            flat_out = wb.ap().rearrange(
                " ".join(f"s{i}" for i in range(len(shp))) + " -> (" +
                " ".join(f"s{i}" for i in range(len(shp))) + ")").rearrange(
                "(n p f) -> n p f", p=128, f=4096)
            for i in range(n):
                par = i % 2
                w = [st_tok[par]] if st_tok[par] is not None else []
                t_in = s.op("gpsimd", lambda e, nm=nm, i=i, par=par, flat_in=flat_in: e.dma_start(
                    out=stg[par][:], in_=flat_in[i]), wait=w, sig=f"d_cvi{par}", n=16)
                st_tok[par] = s.op("sync", lambda e, nm=nm, i=i, par=par, flat_out=flat_out: e.dma_start(
                    out=flat_out[i], in_=stg[par][:]), wait=[t_in], sig=f"d_cvo{par}", n=16)


def emit_moe_gating(s, P, A, prol, gw_name, topk):
    t_load = s.op("gpsimd", lambda e: e.dma_start(
        out=A.gws[:], in_=P.W[gw_name]), sig="d_gw", n=16)
    st = A.gst
    banktok = [None, None]
    for tt in range(NTT):
        par = tt % 2
        bank = A.psumf[par]
        w = [t_load, prol.done_V]
        if banktok[par] is not None:
            w.append(banktok[par])
        tok = None
        for c in range(NDC):
            tok = s.op("tensor", lambda e, c=c, tt=tt, bank=bank: e.matmul(
                bank[:, 0:8], xnT_slice(P, c, 128 * tt, 128), A.gws[:, 8 * c:8 * c + 8],
                start=(c == 0), stop=(c == NDC - 1)), wait=w, sig="T")
            w = []
        c0 = 8 * tt
        t_lg = s.op("vector", lambda e, tt=tt, bank=bank: e.tensor_copy(
            A.glog[:, 8 * tt:8 * tt + 8], bank[:, 0:8]), wait=[tok], sig="V")
        banktok[par] = t_lg
        t_max = s.op("vector", lambda e, tt=tt: e.max(
            A.gtop[:, 8 * tt:8 * tt + 8], A.glog[:, 8 * tt:8 * tt + 8]),
            wait=[t_lg], sig="V")
        t_neg = s.op("vector", lambda e, tt=tt, c0=c0: e.tensor_scalar_mul(
            st[:, c0 + 2:c0 + 3], A.gtop[:, 8 * tt:8 * tt + 1], -1.0),
            wait=[t_max], sig="V")
        t_exp = s.op("scalar", lambda e, tt=tt, c0=c0: e.activation(
            out=A.gexp[:, 8 * tt:8 * tt + 8], in_=A.glog[:, 8 * tt:8 * tt + 8],
            func=AF.Exp, bias=st[:, c0 + 2:c0 + 3], scale=1.0,
            accum_out=st[:, c0 + 3:c0 + 4]), wait=[t_neg], sig="SC")
        if topk == 1:
            t_r = s.op("vector", lambda e, c0=c0: e.reciprocal(
                st[:, c0 + 4:c0 + 5], st[:, c0 + 3:c0 + 4]), wait=[t_exp], sig="V")
            t_oh = s.op("vector", lambda e, tt=tt: e.tensor_scalar(
                A.oh[:, 8 * tt:8 * tt + 8], A.glog[:, 8 * tt:8 * tt + 8],
                A.gtop[:, 8 * tt:8 * tt + 1], None, ALU.is_equal),
                wait=[t_max], sig="V")
            t_ohb = s.op("vector", lambda e, tt=tt: e.tensor_copy(
                A.ohb[:, 8 * tt:8 * tt + 8], A.oh[:, 8 * tt:8 * tt + 8]),
                wait=[t_oh], sig="V")
            s.op("vector", lambda e, tt=tt, c0=c0: e.tensor_scalar_mul(
                A.gw[:, 8 * tt:8 * tt + 8], A.oh[:, 8 * tt:8 * tt + 8],
                st[:, c0 + 4:c0 + 5]), wait=[t_r, t_oh], sig="V")
        else:
            t_r = s.op("vector", lambda e, c0=c0: e.reciprocal(
                st[:, c0 + 4:c0 + 5], st[:, c0 + 3:c0 + 4]), wait=[t_exp], sig="V")
            t_p = s.op("vector", lambda e, tt=tt, c0=c0: e.tensor_scalar_mul(
                A.gexp[:, 8 * tt:8 * tt + 8], A.gexp[:, 8 * tt:8 * tt + 8],
                st[:, c0 + 4:c0 + 5]), wait=[t_r], sig="V")
            t_m8 = s.op("vector", lambda e, tt=tt: e.max(
                A.gtop[:, 8 * tt:8 * tt + 8], A.gexp[:, 8 * tt:8 * tt + 8]),
                wait=[t_p], sig="V")
            t_d0 = s.op("vector", lambda e, tt=tt, c0=c0: e.tensor_add(
                st[:, c0 + 7:c0 + 8], A.gtop[:, 8 * tt:8 * tt + 1],
                A.gtop[:, 8 * tt + 1:8 * tt + 2]), wait=[t_m8], sig="V")
            t_d1 = s.op("vector", lambda e, c0=c0: e.tensor_scalar_add(
                st[:, c0 + 7:c0 + 8], st[:, c0 + 7:c0 + 8], 1e-8),
                wait=[t_d0], sig="V")
            t_rr = s.op("vector", lambda e, c0=c0: e.reciprocal(
                st[:, c0 + 4:c0 + 5], st[:, c0 + 7:c0 + 8]), wait=[t_d1], sig="V")
            t_a1 = s.op("vector", lambda e, tt=tt, c0=c0: e.tensor_mul(
                st[:, c0 + 5:c0 + 6], A.gtop[:, 8 * tt:8 * tt + 1],
                st[:, c0 + 4:c0 + 5]), wait=[t_rr], sig="V")
            t_a2 = s.op("vector", lambda e, tt=tt, c0=c0: e.tensor_mul(
                st[:, c0 + 6:c0 + 7], A.gtop[:, 8 * tt + 1:8 * tt + 2],
                st[:, c0 + 4:c0 + 5]), wait=[t_rr], sig="V")
            t_o1 = s.op("vector", lambda e, tt=tt, c0=c0: e.tensor_scalar(
                A.gw[:, 8 * tt:8 * tt + 8], A.gexp[:, 8 * tt:8 * tt + 8],
                A.gtop[:, 8 * tt:8 * tt + 1], st[:, c0 + 5:c0 + 6],
                ALU.is_equal, ALU.mult), wait=[t_a1], sig="V")
            t_o2 = s.op("vector", lambda e, tt=tt, c0=c0: e.tensor_scalar(
                A.goh2[:, 0:8], A.gexp[:, 8 * tt:8 * tt + 8],
                A.gtop[:, 8 * tt + 1:8 * tt + 2], st[:, c0 + 6:c0 + 7],
                ALU.is_equal, ALU.mult), wait=[t_a2, t_o1], sig="V")
            s.op("vector", lambda e, tt=tt: e.tensor_add(
                A.gw[:, 8 * tt:8 * tt + 8], A.gw[:, 8 * tt:8 * tt + 8],
                A.goh2[:, 0:8]), wait=[t_o2], sig="V")
    return s.tok("V"), banktok


def emit_ffn0(s, P, A, prol, gate_done, gbank, gsafe):
    t_b1 = s.op("sync", lambda e: e.dma_start(out=A.b1[:], in_=P.W["f0_b1"]),
                wait=gsafe, sig="d_f0b1", n=16)
    NFC, NQ = 32, 4
    NTB, TB = 4, 512
    NKS = 16
    w1v = P.WB["f0_w1"]
    w2v = P.WB["f0_w2"]
    w1sb = A.w1sb.ap().rearrange("p (c f) -> p c f", c=NDC)
    bank_free = {i: None for i in range(8)}
    bank_free[0] = gbank[0]
    bank_free[1] = gbank[1]
    last_phaseA_T = {q: None for q in range(NQ)}
    w2_last_T = [None, None]
    b2_last_T = None
    hT_last_T = None
    for e_i in range(_DEBUG_NEXPERTS):
        w1tok = []
        for q in range(NQ):
            w = list(gsafe) if e_i == 0 else []
            if last_phaseA_T[q] is not None:
                w.append(last_phaseA_T[q])
            w1tok.append(s.op("sync", lambda e, e_i=e_i, q=q: e.dma_start(
                out=w1sb[:, :, 1024 * q:1024 * (q + 1)],
                in_=w1v[e_i].rearrange("(c p) f -> p c f", p=128)[:, :, 1024 * q:1024 * (q + 1)]),
                wait=w, sig=f"d_f0w1_{q}", n=16))
        wb2 = [b2_last_T] if b2_last_T is not None else list(gsafe)
        t_b2 = s.op("gpsimd", lambda e, e_i=e_i: e.dma_start(
            out=A.b2row[:], in_=P.W["f0_b2"][:, D * e_i:D * (e_i + 1)]),
            wait=wb2, sig="d_f0b2", n=16)
        for tb in range(min(NTB, _DEBUG_NTB)):
            for fc in range(NFC):
                bank_i = fc % 8
                bank = A.psumf[bank_i]
                w = [w1tok[fc // 8], prol.done_V]
                if bank_free[bank_i] is not None:
                    w.append(bank_free[bank_i])
                tok = None
                for c in range(NDC):
                    tok = s.op("tensor", lambda e, fc=fc, c=c, tb=tb, bank=bank: e.matmul(
                        bank[:], w1sb[:, c, 128 * fc:128 * fc + 128],
                        xnT_slice(P, c, TB * tb, TB),
                        start=(c == 0), stop=(c == NDC - 1)), wait=w, sig="T")
                    w = []
                wh = [tok, t_b1]
                if hT_last_T is not None:
                    wh.append(hT_last_T)
                if _DEBUG_NOGELU:
                    t_ev = s.op("scalar", lambda e, fc=fc, bank=bank: e.activation(
                        out=A.hT[:, 512 * fc:512 * fc + 512], in_=bank[:],
                        func=AF.Copy, bias=0.0, scale=1.0), wait=wh, sig="SC")
                else:
                    t_ev = s.op("scalar", lambda e, e_i=e_i, fc=fc, bank=bank: e.activation(
                        out=A.hT[:, 512 * fc:512 * fc + 512], in_=bank[:],
                        func=AF.Gelu, bias=A.b1[:, 32 * e_i + fc:32 * e_i + fc + 1],
                        scale=1.0), wait=wh, sig="SC")
                bank_free[bank_i] = t_ev
            phaseA_done = s.tok("SC")
            first_mm_waits = [phaseA_done, gate_done]
            for i in range(8):
                if bank_free[i] is not None:
                    first_mm_waits.append(bank_free[i])
            mm_tok = None
            for ks in range(NKS):
                par = ks % 2
                w = [] if w2_last_T[par] is not None or e_i > 0 else list(gsafe)
                if w2_last_T[par] is not None:
                    w.append(w2_last_T[par])
                w2tok = s.op("sync", lambda e, e_i=e_i, ks=ks, par=par: e.dma_start(
                    out=A.w2sb[par].ap().rearrange("p (kc d) -> p kc d", kc=2),
                    in_=w2v[e_i].rearrange("(ks kc p) d -> ks p kc d", kc=2, p=128)[ks]),
                    wait=w, sig=f"d_f0w2_{par}", n=16)
                for kc in range(2):
                    fc = 2 * ks + kc
                    for tt in range(4):
                        for db in range(2):
                            bi = tt * 2 + db
                            w = [w2tok] + (first_mm_waits if fc == 0 else [])
                            mm_tok = s.op("tensor", lambda e, fc=fc, kc=kc, tt=tt, db=db, par=par, bi=bi: e.matmul(
                                A.psumf[bi][:],
                                A.hT[:, 512 * fc + 128 * tt: 512 * fc + 128 * tt + 128],
                                A.w2sb[par][:, 1024 * kc + 512 * db: 1024 * kc + 512 * db + 512],
                                start=(fc == 0), stop=False), wait=w, sig="T")
                w2_last_T[par] = mm_tok
            hT_last_T = mm_tok
            for tt in range(4):
                for db in range(2):
                    bi = tt * 2 + db
                    t_bias = s.op("tensor", lambda e, db=db, bi=bi: e.matmul(
                        A.psumf[bi][:], P.ones_row[0:1, 0:128],
                        A.b2row[0:1, 512 * db: 512 * db + 512],
                        start=False, stop=True),
                        wait=[P.setup_tok, t_b2], sig="T")
                    gtt = 4 * tb + tt
                    pv, psc = prol.stats_done[gtt]
                    t_ev = s.op("vector", lambda e, e_i=e_i, gtt=gtt, db=db, bi=bi: e.scalar_tensor_tensor(
                        x_slice(P, gtt, 512 * db, 512), A.psumf[bi][:],
                        A.gw[:, 8 * gtt + e_i: 8 * gtt + e_i + 1],
                        x_slice(P, gtt, 512 * db, 512),
                        ALU.mult, ALU.add), wait=[t_bias, pv, psc, gate_done], sig="V")
                    bank_free[bi] = t_ev
            b2_last_T = s.tok("T")
        for q in range(NQ):
            last_phaseA_T[q] = s.tok("T")


def emit_ffn1(s, P, A, prol, gate_done, gbank, gsafe):
    t_bg = s.op("sync", lambda e: e.dma_start(out=A.bg[:], in_=P.W["f1_bg"]),
                wait=gsafe, sig="d_f1bg", n=16)
    t_bu = s.op("sync", lambda e: e.dma_start(out=A.bu[:], in_=P.W["f1_bu"]),
                wait=gsafe, sig="d_f1bu", n=16)
    NMC = 16
    NTB, TB = 4, 512
    NKS = 8
    wgv, wuv, wov = P.WB["f1_wg"], P.WB["f1_wu"], P.WB["f1_wo"]
    wgsb = A.wgsb.ap().rearrange("p (c f) -> p c f", c=NDC)
    wusb = A.wusb.ap().rearrange("p (c f) -> p c f", c=NDC)
    bank_free = {i: None for i in range(8)}
    bank_free[0] = gbank[0]
    bank_free[1] = gbank[1]
    last_phaseA_T = {q: None for q in range(2)}
    wo_last_T = [None, None]
    bo_last_T = None
    hT_last_T = None
    gscr_tok = [None, None]
    for e_i in range(8):
        wgtok, wutok = [], []
        for q in range(2):
            w = list(gsafe) if e_i == 0 else []
            if last_phaseA_T[q] is not None:
                w.append(last_phaseA_T[q])
            wgtok.append(s.op("sync", lambda e, e_i=e_i, q=q: e.dma_start(
                out=wgsb[:, :, 1024 * q:1024 * (q + 1)],
                in_=wgv[e_i].rearrange("(c p) f -> p c f", p=128)[:, :, 1024 * q:1024 * (q + 1)]),
                wait=w, sig=f"d_f1wg_{q}", n=16))
            wutok.append(s.op("sync", lambda e, e_i=e_i, q=q: e.dma_start(
                out=wusb[:, :, 1024 * q:1024 * (q + 1)],
                in_=wuv[e_i].rearrange("(c p) f -> p c f", p=128)[:, :, 1024 * q:1024 * (q + 1)]),
                wait=w, sig=f"d_f1wu_{q}", n=16))
        wbo = [bo_last_T] if bo_last_T is not None else list(gsafe)
        t_bo = s.op("gpsimd", lambda e, e_i=e_i: e.dma_start(
            out=A.borow[:], in_=P.W["f1_bo"][:, D * e_i:D * (e_i + 1)]),
            wait=wbo, sig="d_f1bo", n=16)
        for tb in range(NTB):
            for mc in range(NMC):
                bg_i = 2 * (mc % 4)
                bu_i = bg_i + 1
                bank_g, bank_u = A.psumf[bg_i], A.psumf[bu_i]
                wq = mc // 8
                wg_ = [wgtok[wq], prol.done_V]
                if bank_free[bg_i] is not None:
                    wg_.append(bank_free[bg_i])
                tokg = None
                for c in range(NDC):
                    tokg = s.op("tensor", lambda e, mc=mc, c=c, tb=tb, bank=bank_g: e.matmul(
                        bank[:], wgsb[:, c, 128 * mc:128 * mc + 128],
                        xnT_slice(P, c, TB * tb, TB),
                        start=(c == 0), stop=(c == NDC - 1)), wait=wg_, sig="T")
                    wg_ = []
                wu_ = [wutok[wq]]
                if bank_free[bu_i] is not None:
                    wu_.append(bank_free[bu_i])
                toku = None
                for c in range(NDC):
                    toku = s.op("tensor", lambda e, mc=mc, c=c, tb=tb, bank=bank_u: e.matmul(
                        bank[:], wusb[:, c, 128 * mc:128 * mc + 128],
                        xnT_slice(P, c, TB * tb, TB),
                        start=(c == 0), stop=(c == NDC - 1)), wait=wu_, sig="T")
                    wu_ = []
                slot = mc % 2
                wsg = [tokg, t_bg]
                if gscr_tok[slot] is not None:
                    wsg.append(gscr_tok[slot])
                t_g = s.op("scalar", lambda e, e_i=e_i, mc=mc, slot=slot, bank=bank_g: e.activation(
                    out=A.gscr[:, 512 * slot:512 * slot + 512], in_=bank[:],
                    func=AF.Silu, bias=A.bg[:, 16 * e_i + mc:16 * e_i + mc + 1],
                    scale=1.0), wait=wsg, sig="SC")
                bank_free[bg_i] = t_g
                wh = [toku, t_g, t_bu]
                if hT_last_T is not None:
                    wh.append(hT_last_T)
                t_h = s.op("vector", lambda e, e_i=e_i, mc=mc, slot=slot, bank=bank_u: e.scalar_tensor_tensor(
                    A.hT[:, 512 * mc:512 * mc + 512], bank[:],
                    A.bu[:, 16 * e_i + mc:16 * e_i + mc + 1],
                    A.gscr[:, 512 * slot:512 * slot + 512],
                    ALU.add, ALU.mult), wait=wh, sig="V")
                bank_free[bu_i] = t_h
                gscr_tok[slot] = t_h
            phaseA_done = s.tok("V")
            first_mm_waits = [phaseA_done, gate_done]
            for i in range(8):
                if bank_free[i] is not None:
                    first_mm_waits.append(bank_free[i])
            mm_tok = None
            for ks in range(NKS):
                par = ks % 2
                w = [] if wo_last_T[par] is not None or e_i > 0 else list(gsafe)
                if wo_last_T[par] is not None:
                    w.append(wo_last_T[par])
                wotok = s.op("sync", lambda e, e_i=e_i, ks=ks, par=par: e.dma_start(
                    out=A.wosb[par].ap().rearrange("p (kc d) -> p kc d", kc=2),
                    in_=wov[e_i].rearrange("(ks kc p) d -> ks p kc d", kc=2, p=128)[ks]),
                    wait=w, sig=f"d_f1wo_{par}", n=16)
                for kc in range(2):
                    fc = 2 * ks + kc
                    for tt in range(4):
                        for db in range(2):
                            bi = tt * 2 + db
                            w = [wotok] + (first_mm_waits if fc == 0 else [])
                            mm_tok = s.op("tensor", lambda e, fc=fc, kc=kc, tt=tt, db=db, par=par, bi=bi: e.matmul(
                                A.psumf[bi][:],
                                A.hT[:, 512 * fc + 128 * tt: 512 * fc + 128 * tt + 128],
                                A.wosb[par][:, 1024 * kc + 512 * db: 1024 * kc + 512 * db + 512],
                                start=(fc == 0), stop=False), wait=w, sig="T")
                wo_last_T[par] = mm_tok
            hT_last_T = mm_tok
            for tt in range(4):
                for db in range(2):
                    bi = tt * 2 + db
                    t_bias = s.op("tensor", lambda e, db=db, bi=bi: e.matmul(
                        A.psumf[bi][:], P.ones_row[0:1, 0:128],
                        A.borow[0:1, 512 * db: 512 * db + 512],
                        start=False, stop=True),
                        wait=[P.setup_tok, t_bo], sig="T")
                    gtt = 4 * tb + tt
                    pv, psc = prol.stats_done[gtt]
                    t_ev = s.op("vector", lambda e, e_i=e_i, gtt=gtt, db=db, bi=bi: e.scalar_tensor_tensor(
                        x_slice(P, gtt, 512 * db, 512), A.psumf[bi][:],
                        A.gw[:, 8 * gtt + e_i: 8 * gtt + e_i + 1],
                        x_slice(P, gtt, 512 * db, 512),
                        ALU.mult, ALU.add), wait=[t_bias, pv, psc, gate_done], sig="V")
                    bank_free[bi] = t_ev
            bo_last_T = s.tok("T")
        for q in range(2):
            last_phaseA_T[q] = s.tok("T")


ALIBI_SLOPES = [2.0 ** (-0.5 * (h + 1)) for h in range(16)]


def _attn_head(s, P, A, C, oc, r0, q_done, v_done, t_tab, causal, stt_scalar):
    sb, avb, tbk, bank_free = C.sb, C.avb, C.tbk, C.bank_free
    for qt in range(NTT):
        nkb = (qt // 4 + 1) if causal else 4
        W_ = nkb * 512
        nb = (qt + 1) if causal else NTT
        sl = qt % 4
        mm_toks = []
        for kb in range(nkb):
            w = [q_done, v_done]
            if bank_free.get(("s", kb)) is not None:
                w.append(bank_free[("s", kb)])
            mm_toks.append(s.op("tensor", lambda e, r0=r0, qt=qt, kb=kb: e.matmul(
                sb[kb][:], A.qT[r0:r0 + 64, 128 * qt:128 * qt + 128],
                A.kT[r0:r0 + 64, 512 * kb:512 * kb + 512],
                start=True, stop=True), wait=w, sig="T"))
        if stt_scalar is None:
            c0 = 16 * sl
            wf = []
            if C.rs_free[sl] is not None:
                wf.append(C.rs_free[sl])
            bm = []
            for kb in range(nkb):
                bm.append(s.op("vector", lambda e, kb=kb, c0=c0: e.reduce_max(
                    A.rs[:, c0 + 4 + kb:c0 + 5 + kb], sb[kb][:], axis=AX),
                    wait=[mm_toks[kb]] + wf, sig="V"))
            t_max = s.op("vector", lambda e, c0=c0, nkb=nkb: e.reduce_max(
                A.rs[:, c0:c0 + 1], A.rs[:, c0 + 4:c0 + 4 + nkb], axis=AX),
                wait=[bm[-1]], sig="V")
            t_neg = s.op("vector", lambda e, c0=c0: e.tensor_scalar_mul(
                A.rs[:, c0 + 1:c0 + 2], A.rs[:, c0:c0 + 1], -1.0),
                wait=[t_max], sig="V")
            exps = []
            for kb in range(nkb):
                wexp = [t_neg]
                if C.psb_war is not None and kb == 0:
                    wexp.append(C.psb_war)
                exps.append(s.op("scalar", lambda e, kb=kb, c0=c0: e.activation(
                    out=A.Psb[:, 512 * kb:512 * kb + 512], in_=sb[kb][:],
                    func=AF.Exp, bias=A.rs[:, c0 + 1:c0 + 2], scale=1.0,
                    accum_out=A.rs[:, c0 + 8 + kb:c0 + 9 + kb]), wait=wexp, sig="SC"))
                bank_free[("s", kb)] = exps[-1]
            C.prev_exp = exps[-1]
            t_sum = s.op("vector", lambda e, c0=c0, nkb=nkb: e.reduce_sum(
                A.rs[:, c0 + 2:c0 + 3], A.rs[:, c0 + 8:c0 + 8 + nkb], axis=AX),
                wait=[exps[-1]], sig="V")
            t_rc = s.op("vector", lambda e, c0=c0: e.reciprocal(
                A.rs[:, c0 + 3:c0 + 4], A.rs[:, c0 + 2:c0 + 3]),
                wait=[t_sum], sig="V")
            t_exp = exps[-1]
            rc_col = A.rs[:, c0 + 3:c0 + 4]
        else:
            stt_toks = []
            for kb in range(nkb):
                i0 = 1920 - 128 * qt + 512 * kb
                w = [mm_toks[kb]]
                if t_tab is not None:
                    w.append(t_tab)
                if C.prev_exp is not None:
                    w.append(C.prev_exp)
                stt_toks.append(s.op("vector", lambda e, kb=kb, i0=i0: e.scalar_tensor_tensor(
                    A.Ssb[:, 512 * kb:512 * kb + 512],
                    A.dtab[:, i0:i0 + 512], stt_scalar, sb[kb][:],
                    ALU.mult, ALU.add), wait=w, sig="V"))
                bank_free[("s", kb)] = stt_toks[-1]
            w = [stt_toks[-1]]
            if C.rs_free[sl] is not None:
                w.append(C.rs_free[sl])
            t_max = s.op("vector", lambda e, sl=sl, W_=W_: e.reduce_max(
                A.rs[:, 4 * sl:4 * sl + 1], A.Ssb[:, 0:W_], axis=AX), wait=w, sig="V")
            t_neg = s.op("vector", lambda e, sl=sl: e.tensor_scalar_mul(
                A.rs[:, 4 * sl + 1:4 * sl + 2], A.rs[:, 4 * sl:4 * sl + 1], -1.0),
                wait=[t_max], sig="V")
            wexp = [t_neg]
            if C.psb_war is not None:
                wexp.append(C.psb_war)
            t_exp = s.op("scalar", lambda e, sl=sl, W_=W_: e.activation(
                out=A.Psb[:, 0:W_], in_=A.Ssb[:, 0:W_], func=AF.Exp,
                bias=A.rs[:, 4 * sl + 1:4 * sl + 2], scale=1.0,
                accum_out=A.rs[:, 4 * sl + 2:4 * sl + 3]), wait=wexp, sig="SC")
            C.prev_exp = t_exp
            t_rc = s.op("vector", lambda e, sl=sl: e.reciprocal(
                A.rs[:, 4 * sl + 3:4 * sl + 4], A.rs[:, 4 * sl + 2:4 * sl + 3]),
                wait=[t_exp], sig="V")
            rc_col = A.rs[:, 4 * sl + 3:4 * sl + 4]
        pev = {}
        for g in range((nb + 7) // 8):
            cnt = min(8, nb - 8 * g)
            gb = g % 2
            w = [t_exp, P.setup_tok]
            if bank_free.get(("tb", gb)) is not None:
                w.append(bank_free[("tb", gb)])
            tok = None
            for j in range(cnt):
                jj = 8 * g + j
                tok = s.op("tensor", lambda e, gb=gb, j=j, jj=jj: e.transpose(
                    tbk[gb][:, 128 * j:128 * j + 128],
                    A.Psb[:, 128 * jj:128 * jj + 128], P.ident[:]), wait=w, sig="T")
                w = []
            t_pev = s.op("vector", lambda e, g=g, gb=gb, cnt=cnt: e.tensor_copy(
                A.PT[:, 1024 * g:1024 * g + 128 * cnt],
                tbk[gb][:, 0:128 * cnt]), wait=[tok], sig="V")
            bank_free[("tb", gb)] = t_pev
            pev[g] = t_pev
        C.psb_war = s.tok("T")
        t_av = None
        for j in range(nb):
            w = [pev[j // 8]]
            if j == 0 and bank_free.get(("av",)) is not None:
                w.append(bank_free[("av",)])
            t_av = s.op("tensor", lambda e, j=j, r0=r0, nb=nb: e.matmul(
                avb[:, 0:64], A.PT[:, 128 * j:128 * j + 128],
                A.vc[:, 128 * j + r0:128 * j + r0 + 64],
                start=(j == 0), stop=(j == nb - 1)), wait=w, sig="T")
        t_oev = s.op("vector", lambda e, rc_col=rc_col: e.tensor_scalar_mul(
            A.osb[:, 0:64], avb[:, 0:64], rc_col),
            wait=[t_av, t_rc], sig="V")
        bank_free[("av",)] = t_oev
        C.rs_free[sl] = t_oev
        gb = qt % 2
        w = [t_oev, P.setup_tok]
        if bank_free.get(("tb", gb)) is not None:
            w.append(bank_free[("tb", gb)])
        t_otr = s.op("tensor", lambda e, gb=gb: e.transpose(
            tbk[gb][0:64, 0:128], A.osb[:, 0:64], P.ident[:]), wait=w, sig="T")
        t_ofin = s.op("vector", lambda e, gb=gb, r0=r0, oc=oc, qt=qt: e.tensor_copy(
            A.oT[r0:r0 + 64, T * oc + 128 * qt: T * oc + 128 * qt + 128],
            tbk[gb][0:64, 0:128]), wait=[t_otr], sig="V")
        bank_free[("tb", gb)] = t_ofin


def _emit_out_proj(s, nc, P, A, prol, wosb, t_wo, attn_done,
                   bias_row=None, t_bias_load=None):
    with ExitStack() as esc2:
        yb = [esc2.enter_context(nc.psum_tensor(f"t_yb{i}", [128, 512], F32))
              for i in range(8)]
        ybfree = {}
        for tb in range(4):
            for fc in range(NDC):
                for tt in range(4):
                    for db in range(2):
                        bi = 2 * tt + db
                        gtt = 4 * tb + tt
                        w = [t_wo] + (attn_done if fc == 0 else [])
                        if fc == 0 and ybfree.get(bi) is not None:
                            w.append(ybfree[bi])
                        tok = s.op("tensor", lambda e, fc=fc, gtt=gtt, db=db, bi=bi: e.matmul(
                            yb[bi][:], A.oT[:, T * fc + 128 * gtt: T * fc + 128 * gtt + 128],
                            wosb[:, fc, 512 * db:512 * db + 512],
                            start=(fc == 0),
                            stop=(fc == NDC - 1 and bias_row is None)),
                            wait=w, sig="T")
                        if fc == NDC - 1:
                            if bias_row is not None:
                                tok = s.op("tensor", lambda e, db=db, bi=bi: e.matmul(
                                    yb[bi][:], P.ones_row[0:1, 0:128],
                                    bias_row[0:1, 512 * db: 512 * db + 512],
                                    start=False, stop=True),
                                    wait=[P.setup_tok, t_bias_load], sig="T")
                            pv, psc = prol.stats_done[gtt]
                            t_ev = s.op("vector", lambda e, gtt=gtt, db=db, bi=bi: e.tensor_add(
                                x_slice(P, gtt, 512 * db, 512), yb[bi][:],
                                x_slice(P, gtt, 512 * db, 512)),
                                wait=[tok, pv, psc], sig="V")
                            ybfree[bi] = t_ev


def emit_attn_a0(s, nc, es, P):
    A = NS()
    with ExitStack() as es1:
        A.scr = es1.enter_context(nc.sbuf_tensor("t_scr", [128, D], F32))
        A.xn = es1.enter_context(nc.sbuf_tensor("t_xn", [128, 2 * D], BF16))
        with ExitStack() as esp:
            A.psumb = [esp.enter_context(nc.psum_tensor(f"t_psb{i}", [128, 1024], BF16))
                       for i in range(2)]
            prol = emit_prologue(s, P, A)
    gsafe0 = s.toks("T", "V", "SC")
    A.dtab = es.enter_context(nc.sbuf_tensor("t_dtab", [128, 2432], F32))
    A.wq = [es.enter_context(nc.sbuf_tensor(f"t_wq{i}", [128, NDC * 128], BF16))
            for i in range(2)]
    A.wk = [es.enter_context(nc.sbuf_tensor(f"t_wk{i}", [128, NDC * 128], BF16))
            for i in range(2)]
    A.wv = [es.enter_context(nc.sbuf_tensor(f"t_wv{i}", [128, NDC * 128], BF16))
            for i in range(2)]
    A.wo = es.enter_context(nc.sbuf_tensor("t_wo", [128, NDC * 1024], BF16))
    A.qT = es.enter_context(nc.sbuf_tensor("t_qT", [128, T], BF16))
    A.kT = es.enter_context(nc.sbuf_tensor("t_kT", [128, T], BF16))
    A.vTs = es.enter_context(nc.sbuf_tensor("t_vTs", [128, T], BF16))
    A.vc = es.enter_context(nc.sbuf_tensor("t_vc", [128, NTT * 128], BF16))
    A.oT = es.enter_context(nc.sbuf_tensor("t_oT", [128, NDC * T], BF16))
    A.Ssb = es.enter_context(nc.sbuf_tensor("t_Ssb", [128, T], F32))
    A.Psb = es.enter_context(nc.sbuf_tensor("t_Psb", [128, T], BF16))
    A.PT = es.enter_context(nc.sbuf_tensor("t_PT", [128, NTT * 128], BF16))
    A.rs = es.enter_context(nc.sbuf_tensor("t_rs", [128, 16], F32))
    A.osb = es.enter_context(nc.sbuf_tensor("t_osb", [128, 64], BF16))
    t_io = s.op("gpsimd", lambda e: e.iota(
        A.dtab[:], pattern=[[-1, 2432]], base=1920, channel_multiplier=1,
        allow_small_or_imprecise_dtypes=True), wait=gsafe0, sig="P")
    t_dt = s.op("gpsimd", lambda e, _w=None: e.affine_select(
        out=A.dtab[:], in_=A.dtab[:], pattern=[[-1, 2432]],
        compare_op=ALU.is_ge, fill=1e33, base=1920, channel_multiplier=1),
        wait=[t_io], sig="P")
    t_wo = s.op("gpsimd", lambda e: e.dma_start(
        out=A.wo.ap().rearrange("p (c d) -> p c d", c=NDC),
        in_=P.W["a0_wo"].rearrange("(c p) d -> p c d", p=128)),
        wait=gsafe0, sig="d_a0wo", n=16)

    wosb = A.wo.ap().rearrange("p (c d) -> p c d", c=NDC)
    oT_tok = {}
    attn_T_done = {}
    proj_T_done = {}
    with ExitStack() as esc:
        C = NS(
            sb=[esc.enter_context(nc.psum_tensor(f"t_sp{i}", [128, 512], F32))
                for i in range(4)],
            avb=esc.enter_context(nc.psum_tensor("t_av", [128, 512], F32)),
            tbk=[esc.enter_context(nc.psum_tensor(f"t_tb{i}", [128, 1024], BF16))
                 for i in range(2)],
            bank_free={}, rs_free={sl: None for sl in range(4)},
            psb_war=None, prev_exp=None)
        sb, tbk, bank_free = C.sb, C.tbk, C.bank_free
        for oc in range(8):
            par = oc % 2
            wl = []
            for nm, buf in (("a0_wq", A.wq[par]), ("a0_wk", A.wk[par]), ("a0_wv", A.wv[par])):
                w = list(gsafe0) if oc < 2 else [proj_T_done[oc - 2]]
                wl.append(s.op("gpsimd", lambda e, nm=nm, buf=buf, oc=oc: e.dma_start(
                    out=buf.ap().rearrange("p (c o) -> p c o", c=NDC),
                    in_=P.W[nm][:, 128 * oc:128 * oc + 128].rearrange(
                        "(c p) o -> p c o", p=128)), wait=w,
                    sig=f"d_{nm[3:]}_{par}", n=16))
            war = [attn_T_done[oc - 1]] if oc >= 1 else []
            wqs = A.wq[par].ap().rearrange("p (c o) -> p c o", c=NDC)
            wks = A.wk[par].ap().rearrange("p (c o) -> p c o", c=NDC)
            wvs = A.wv[par].ap().rearrange("p (c o) -> p c o", c=NDC)
            for src_i, (wvw, wtok, dst, kind) in enumerate((
                    (wqs, wl[0], A.qT, "q"), (wks, wl[1], A.kT, "k"),
                    (wvs, wl[2], A.vTs, "v"))):
                for tblk in range(4):
                    bk = (tblk + src_i) % 2
                    bank = sb[bk]
                    w = [wtok, prol.done_V]
                    if bank_free.get(("s", bk)) is not None:
                        w.append(bank_free[("s", bk)])
                    tok = None
                    for c in range(NDC):
                        tok = s.op("tensor", lambda e, wvw=wvw, c=c, tblk=tblk, bank=bank: e.matmul(
                            bank[:], wvw[:, c, :], xnT_slice(P, c, 512 * tblk, 512),
                            start=(c == 0), stop=(c == NDC - 1)), wait=w, sig="T")
                        w = []
                    ww = [tok] + war
                    if kind == "q":
                        t_ev = s.op("scalar", lambda e, dst=dst, tblk=tblk, bank=bank: e.mul(
                            dst[:, 512 * tblk:512 * tblk + 512], bank[:], 0.125),
                            wait=ww, sig="SC")
                    elif kind == "k":
                        t_ev = s.op("scalar", lambda e, dst=dst, tblk=tblk, bank=bank: e.copy(
                            dst[:, 512 * tblk:512 * tblk + 512], bank[:]),
                            wait=ww, sig="SC")
                    else:
                        t_ev = s.op("vector", lambda e, dst=dst, tblk=tblk, bank=bank: e.tensor_copy(
                            dst[:, 512 * tblk:512 * tblk + 512], bank[:]),
                            wait=ww, sig="V")
                    bank_free[("s", bk)] = t_ev
            vT_done = s.tok("V")
            q_done = s.tok("SC")
            for g in range(2):
                w = [vT_done, P.setup_tok] + war
                if bank_free.get(("tb", g)) is not None:
                    w.append(bank_free[("tb", g)])
                tok = None
                for j in range(8):
                    tt = 8 * g + j
                    tok = s.op("tensor", lambda e, g=g, j=j, tt=tt: e.transpose(
                        tbk[g][:, 128 * j:128 * j + 128],
                        A.vTs[:, 128 * tt:128 * tt + 128], P.ident[:]),
                        wait=w, sig="T")
                    w = []
                t_ev = s.op("vector", lambda e, g=g: e.tensor_copy(
                    A.vc[:, 1024 * g:1024 * g + 1024], tbk[g][:]), wait=[tok], sig="V")
                bank_free[("tb", g)] = t_ev
            v_done = s.tok("V")
            proj_T_done[oc] = s.tok("T")
            for h2 in range(2):
                h = 2 * oc + h2
                _attn_head(s, P, A, C, oc, 64 * h2, q_done, v_done, t_dt,
                           causal=True, stt_scalar=-ALIBI_SLOPES[h])
            attn_T_done[oc] = s.tok("T")
            oT_tok[oc] = s.tok("V")
    attn_done = s.toks("T", "V", "SC")
    _emit_out_proj(s, nc, P, A, prol, wosb, t_wo, attn_done)


def emit_attn_a1(s, nc, es, P):
    A = NS()
    with ExitStack() as es1:
        A.scr = es1.enter_context(nc.sbuf_tensor("t_scr", [128, D], F32))
        A.xn = es1.enter_context(nc.sbuf_tensor("t_xn", [128, 2 * D], BF16))
        with ExitStack() as esp:
            A.psumb = [esp.enter_context(nc.psum_tensor(f"t_psb{i}", [128, 1024], BF16))
                       for i in range(2)]
            prol = emit_prologue(s, P, A)
    gsafe0 = s.toks("T", "V", "SC")
    A.dtab = es.enter_context(nc.sbuf_tensor("t_dtab", [128, 2432], BF16))
    A.wq = [es.enter_context(nc.sbuf_tensor(f"t_wq{i}", [128, NDC * 128], BF16))
            for i in range(2)]
    A.wk = [es.enter_context(nc.sbuf_tensor(f"t_wk{i}", [128, NDC * 128], BF16))
            for i in range(2)]
    A.wv = [es.enter_context(nc.sbuf_tensor(f"t_wv{i}", [128, NDC * 128], BF16))
            for i in range(2)]
    A.wo = es.enter_context(nc.sbuf_tensor("t_wo", [128, NDC * 1024], BF16))
    A.qT = es.enter_context(nc.sbuf_tensor("t_qT", [128, T], BF16))
    A.kT = es.enter_context(nc.sbuf_tensor("t_kT", [128, T], BF16))
    A.vTs = es.enter_context(nc.sbuf_tensor("t_vTs", [128, T], BF16))
    A.vc = es.enter_context(nc.sbuf_tensor("t_vc", [128, NTT * 128], BF16))
    A.oT = es.enter_context(nc.sbuf_tensor("t_oT", [128, NDC * T], BF16))
    A.Ssb = es.enter_context(nc.sbuf_tensor("t_Ssb", [128, T], F32))
    A.Psb = es.enter_context(nc.sbuf_tensor("t_Psb", [128, T], BF16))
    A.PT = es.enter_context(nc.sbuf_tensor("t_PT", [128, NTT * 128], BF16))
    A.rs = es.enter_context(nc.sbuf_tensor("t_rs", [128, 16], F32))
    A.osb = es.enter_context(nc.sbuf_tensor("t_osb", [128, 64], BF16))
    A.cosb = es.enter_context(nc.sbuf_tensor("t_cosb", [128, T], BF16))
    A.sinb = es.enter_context(nc.sbuf_tensor("t_sinb", [128, T], BF16))
    A.rot = es.enter_context(nc.sbuf_tensor("t_rot", [128, 512], BF16))
    A.qb = es.enter_context(nc.sbuf_tensor("t_qb", [128, 8], F32))
    A.kb = es.enter_context(nc.sbuf_tensor("t_kb", [128, 8], F32))
    A.vb = es.enter_context(nc.sbuf_tensor("t_vb", [128, 8], F32))
    A.pbrow = es.enter_context(nc.sbuf_tensor("t_pbrow", [1, D], BF16))
    t_mz = s.op("gpsimd", lambda e: e.memset(A.dtab[:], 0.0), wait=gsafe0, sig="P")
    t_tab = s.op("gpsimd", lambda e: e.affine_select(
        out=A.dtab[:], in_=A.dtab[:], pattern=[[-1, 2432]],
        compare_op=ALU.is_ge, fill=-1e38, base=1920, channel_multiplier=1),
        wait=[t_mz], sig="P")
    t_wo = s.op("gpsimd", lambda e: e.dma_start(
        out=A.wo.ap().rearrange("p (c d) -> p c d", c=NDC),
        in_=P.W["a1_wo"].rearrange("(c p) d -> p c d", p=128)),
        wait=gsafe0, sig="d_a1wo", n=16)
    t_cos = s.op("gpsimd", lambda e: e.dma_start(out=A.cosb[:], in_=P.W["a1_cos"]),
                 wait=gsafe0, sig="d_a1cos", n=16)
    t_sin = s.op("gpsimd", lambda e: e.dma_start(out=A.sinb[:], in_=P.W["a1_sin"]),
                 wait=gsafe0, sig="d_a1sin", n=16)
    t_qb = s.op("sync", lambda e: e.dma_start(out=A.qb[:], in_=P.W["a1_qb"]),
                wait=gsafe0, sig="d_a1qb", n=16)
    t_kb = s.op("sync", lambda e: e.dma_start(out=A.kb[:], in_=P.W["a1_kb"]),
                wait=gsafe0, sig="d_a1kb", n=16)
    t_vb = s.op("sync", lambda e: e.dma_start(out=A.vb[:], in_=P.W["a1_vb"]),
                wait=gsafe0, sig="d_a1vb", n=16)
    t_pb = s.op("gpsimd", lambda e: e.dma_start(out=A.pbrow[:], in_=P.W["a1_pb"]),
                wait=gsafe0, sig="d_a1pb", n=16)
    bload = [t_qb, t_kb, t_vb]

    wosb = A.wo.ap().rearrange("p (c d) -> p c d", c=NDC)
    attn_T_done = {}
    proj_T_done = {}
    with ExitStack() as esc:
        C = NS(
            sb=[esc.enter_context(nc.psum_tensor(f"t_sp{i}", [128, 512], F32))
                for i in range(4)],
            avb=esc.enter_context(nc.psum_tensor("t_av", [128, 512], F32)),
            tbk=[esc.enter_context(nc.psum_tensor(f"t_tb{i}", [128, 1024], BF16))
                 for i in range(2)],
            bank_free={}, rs_free={sl: None for sl in range(4)},
            psb_war=None, prev_exp=None)
        sb, tbk, bank_free = C.sb, C.tbk, C.bank_free
        for oc in range(8):
            par = oc % 2
            wl = []
            for nm, buf in (("a1_wq", A.wq[par]), ("a1_wk", A.wk[par]), ("a1_wv", A.wv[par])):
                w = list(gsafe0) if oc < 2 else [proj_T_done[oc - 2]]
                wl.append(s.op("gpsimd", lambda e, nm=nm, buf=buf, oc=oc: e.dma_start(
                    out=buf.ap().rearrange("p (c o) -> p c o", c=NDC),
                    in_=P.W[nm][:, 128 * oc:128 * oc + 128].rearrange(
                        "(c p) o -> p c o", p=128)), wait=w,
                    sig=f"d_{nm[3:]}_{par}", n=16))
            war = [attn_T_done[oc - 1]] if oc >= 1 else []
            wqs = A.wq[par].ap().rearrange("p (c o) -> p c o", c=NDC)
            wks = A.wk[par].ap().rearrange("p (c o) -> p c o", c=NDC)
            wvs = A.wv[par].ap().rearrange("p (c o) -> p c o", c=NDC)
            etoks = {}
            for src_i, (wvw, wtok, dst, kind) in enumerate((
                    (wqs, wl[0], A.qT, "q"), (wks, wl[1], A.kT, "k"),
                    (wvs, wl[2], A.vTs, "v"))):
                for tblk in range(4):
                    bk = (tblk + src_i) % 2
                    bank = sb[bk]
                    w = [wtok, prol.done_V]
                    if bank_free.get(("s", bk)) is not None:
                        w.append(bank_free[("s", bk)])
                    tok = None
                    for c in range(NDC):
                        tok = s.op("tensor", lambda e, wvw=wvw, c=c, tblk=tblk, bank=bank: e.matmul(
                            bank[:], wvw[:, c, :], xnT_slice(P, c, 512 * tblk, 512),
                            start=(c == 0), stop=(c == NDC - 1)), wait=w, sig="T")
                        w = []
                    ww = [tok] + bload + war
                    if kind == "q":
                        t_ev = s.op("vector", lambda e, dst=dst, tblk=tblk, bank=bank, oc=oc: e.tensor_scalar(
                            dst[:, 512 * tblk:512 * tblk + 512], bank[:],
                            A.qb[:, oc:oc + 1], 0.125, ALU.add, ALU.mult),
                            wait=ww, sig="V")
                    elif kind == "k":
                        t_ev = s.op("vector", lambda e, dst=dst, tblk=tblk, bank=bank, oc=oc: e.tensor_scalar(
                            dst[:, 512 * tblk:512 * tblk + 512], bank[:],
                            A.kb[:, oc:oc + 1], None, ALU.add),
                            wait=ww, sig="V")
                    else:
                        t_ev = s.op("vector", lambda e, dst=dst, tblk=tblk, bank=bank, oc=oc: e.tensor_scalar(
                            dst[:, 512 * tblk:512 * tblk + 512], bank[:],
                            A.vb[:, oc:oc + 1], None, ALU.add),
                            wait=ww, sig="V")
                    bank_free[("s", bk)] = t_ev
                    etoks[(kind, tblk)] = t_ev
            rot_free = None
            for kind, dst in (("q", A.qT), ("k", A.kT)):
                for tblk in range(4):
                    sl_ = slice(512 * tblk, 512 * tblk + 512)
                    last_ev = etoks[(kind, tblk)]
                    cps = []
                    for (d0, s0) in ((0, 32), (32, 0), (64, 96), (96, 64)):
                        w = [last_ev]
                        if rot_free is not None:
                            w.append(rot_free)
                        cps.append(s.op("vector", lambda e, dst=dst, d0=d0, s0=s0, sl_=sl_: e.tensor_copy(
                            A.rot[d0:d0 + 32, :], dst[s0:s0 + 32, sl_]),
                            wait=w, sig="V"))
                    t_m2 = s.op("vector", lambda e, sl_=sl_: e.tensor_mul(
                        A.rot[:], A.rot[:], A.sinb[:, sl_]),
                        wait=[cps[-1], t_sin], sig="V")
                    t_m1 = s.op("vector", lambda e, dst=dst, sl_=sl_: e.tensor_mul(
                        dst[:, sl_], dst[:, sl_], A.cosb[:, sl_]),
                        wait=[cps[-1], t_cos], sig="V")
                    rot_free = s.op("vector", lambda e, dst=dst, sl_=sl_: e.tensor_add(
                        dst[:, sl_], dst[:, sl_], A.rot[:]),
                        wait=[t_m1, t_m2], sig="V")
            vT_done = s.tok("V")
            for g in range(2):
                w = [vT_done, P.setup_tok] + war
                if bank_free.get(("tb", g)) is not None:
                    w.append(bank_free[("tb", g)])
                tok = None
                for j in range(8):
                    tt = 8 * g + j
                    tok = s.op("tensor", lambda e, g=g, j=j, tt=tt: e.transpose(
                        tbk[g][:, 128 * j:128 * j + 128],
                        A.vTs[:, 128 * tt:128 * tt + 128], P.ident[:]),
                        wait=w, sig="T")
                    w = []
                t_ev = s.op("vector", lambda e, g=g: e.tensor_copy(
                    A.vc[:, 1024 * g:1024 * g + 1024], tbk[g][:]), wait=[tok], sig="V")
                bank_free[("tb", g)] = t_ev
            v_done = s.tok("V")
            q_done = s.tok("V")
            proj_T_done[oc] = s.tok("T")
            for h2 in range(2):
                _attn_head(s, P, A, C, oc, 64 * h2, q_done, v_done, t_tab,
                           causal=True, stt_scalar=1.0)
            attn_T_done[oc] = s.tok("T")
    attn_done = s.toks("T", "V", "SC")
    _emit_out_proj(s, nc, P, A, prol, wosb, t_wo, attn_done,
                   bias_row=A.pbrow, t_bias_load=t_pb)


def _emit_out_proj_streamed(s, nc, es, P, A, prol, wov, attn_done, bias_row,
                            t_bias_load, pfx):
    wosb = [es.enter_context(nc.sbuf_tensor(f"t_wos{i}_{pfx}", [128, D], BF16))
            for i in range(2)]
    with ExitStack() as esc2:
        yb = [esc2.enter_context(nc.psum_tensor(f"t_yb{i}", [128, 512], F32))
              for i in range(8)]
        ybfree = {}
        wo_last_T = [None, None]
        for tb in range(4):
            mm_tok = None
            for ks in range(8):
                par = ks % 2
                fc = ks
                w = list(attn_done) if wo_last_T[par] is None else [wo_last_T[par]]
                wotok = s.op("gpsimd", lambda e, ks=ks, par=par: e.dma_start(
                    out=wosb[par][:],
                    in_=wov.rearrange("(ks p) d -> ks p d", p=128)[ks]),
                    wait=w, sig=f"d_{pfx}wos_{par}", n=16)
                for tt in range(4):
                    for db in range(2):
                        bi = 2 * tt + db
                        gtt = 4 * tb + tt
                        w = [wotok] + (attn_done if fc == 0 else [])
                        if fc == 0 and ybfree.get(bi) is not None:
                            w.append(ybfree[bi])
                        mm_tok = s.op("tensor", lambda e, fc=fc, gtt=gtt, db=db, par=par, bi=bi: e.matmul(
                            yb[bi][:],
                            A.oT[:, T * fc + 128 * gtt: T * fc + 128 * gtt + 128],
                            wosb[par][:, 512 * db: 512 * db + 512],
                            start=(fc == 0), stop=False), wait=w, sig="T")
                wo_last_T[par] = mm_tok
            for tt in range(4):
                for db in range(2):
                    bi = 2 * tt + db
                    gtt = 4 * tb + tt
                    t_bias = s.op("tensor", lambda e, db=db, bi=bi: e.matmul(
                        yb[bi][:], P.ones_row[0:1, 0:128],
                        bias_row[0:1, 512 * db: 512 * db + 512],
                        start=False, stop=True),
                        wait=[P.setup_tok, t_bias_load], sig="T")
                    pv, psc = prol.stats_done[gtt]
                    t_ev = s.op("vector", lambda e, gtt=gtt, db=db, bi=bi: e.tensor_add(
                        x_slice(P, gtt, 512 * db, 512), yb[bi][:],
                        x_slice(P, gtt, 512 * db, 512)),
                        wait=[t_bias, pv, psc], sig="V")
                    ybfree[bi] = t_ev


def emit_attn_b1(s, nc, es, P):
    A = NS()
    with ExitStack() as es1:
        A.scr = es1.enter_context(nc.sbuf_tensor("t_scr", [128, D], F32))
        A.xn = es1.enter_context(nc.sbuf_tensor("t_xn", [128, 2 * D], BF16))
        with ExitStack() as esp:
            A.psumb = [esp.enter_context(nc.psum_tensor(f"t_psb{i}", [128, 1024], BF16))
                       for i in range(2)]
            prol = emit_prologue(s, P, A)
    gsafe0 = s.toks("T", "V", "SC")
    QSCALE = 1.0 / float(np.sqrt(np.float32(1040.0)))
    A.ctq = es.enter_context(nc.sbuf_tensor("t_ctq", [128, 4 * T], BF16))
    A.ctkv = es.enter_context(nc.sbuf_tensor("t_ctkv", [128, 4 * T], BF16))
    A.dqb = es.enter_context(nc.sbuf_tensor("t_dqb", [128, 4], F32))
    A.dkvb = es.enter_context(nc.sbuf_tensor("t_dkvb", [128, 4], F32))
    ct_done = None
    with ExitStack() as esw:
        wdq = esw.enter_context(nc.sbuf_tensor("t_wdq", [128, NDC * 512], BF16))
        wdkv = esw.enter_context(nc.sbuf_tensor("t_wdkv", [128, NDC * 512], BF16))
        with ExitStack() as escp:
            pb = [escp.enter_context(nc.psum_tensor(f"t_pp{i}", [128, 512], F32))
                  for i in range(2)]
            t_wdq = s.op("gpsimd", lambda e: e.dma_start(
                out=wdq.ap().rearrange("p (c o) -> p c o", c=NDC),
                in_=P.W["b1_wdq"].rearrange("(c p) o -> p c o", p=128)),
                wait=gsafe0, sig="d_b1wdq", n=16)
            t_wdkv = s.op("gpsimd", lambda e: e.dma_start(
                out=wdkv.ap().rearrange("p (c o) -> p c o", c=NDC),
                in_=P.W["b1_wdkv"].rearrange("(c p) o -> p c o", p=128)),
                wait=gsafe0, sig="d_b1wdkv", n=16)
            t_dqb = s.op("sync", lambda e: e.dma_start(out=A.dqb[:], in_=P.W["b1_dqb"]),
                         wait=gsafe0, sig="d_b1dqb", n=16)
            t_dkvb = s.op("sync", lambda e: e.dma_start(out=A.dkvb[:], in_=P.W["b1_dkvb"]),
                          wait=gsafe0, sig="d_b1dkvb", n=16)
            wdqv = wdq.ap().rearrange("p (c o) -> p c o", c=NDC)
            wdkvv = wdkv.ap().rearrange("p (c o) -> p c o", c=NDC)
            bfree = [None, None]
            for wview, wtok, btok, bias, dst in (
                    (wdqv, t_wdq, t_dqb, A.dqb, A.ctq),
                    (wdkvv, t_wdkv, t_dkvb, A.dkvb, A.ctkv)):
                for qc in range(4):
                    for tblk in range(4):
                        bi = tblk % 2
                        w = [wtok, prol.done_V]
                        if bfree[bi] is not None:
                            w.append(bfree[bi])
                        tok = None
                        for c in range(NDC):
                            tok = s.op("tensor", lambda e, wview=wview, c=c, qc=qc, tblk=tblk, bi=bi: e.matmul(
                                pb[bi][:], wview[:, c, 128 * qc:128 * qc + 128],
                                xnT_slice(P, c, 512 * tblk, 512),
                                start=(c == 0), stop=(c == NDC - 1)), wait=w, sig="T")
                            w = []
                        import os as _os
                        if _os.environ.get("DBG_NOBIAS") == "1":
                            t_ev = s.op("vector", lambda e, dst=dst, qc=qc, tblk=tblk, bi=bi: e.tensor_copy(
                                dst[:, T * qc + 512 * tblk: T * qc + 512 * tblk + 512],
                                pb[bi][:]), wait=[tok, btok], sig="V")
                        else:
                            t_ev = s.op("vector", lambda e, dst=dst, bias=bias, qc=qc, tblk=tblk, bi=bi: e.tensor_scalar(
                                dst[:, T * qc + 512 * tblk: T * qc + 512 * tblk + 512],
                                pb[bi][:], bias[:, qc:qc + 1], None, ALU.add),
                                wait=[tok, btok], sig="V")
                        bfree[bi] = t_ev
            ct_done = s.toks("T", "V")
            import os as _os
            if _os.environ.get("DBG_CTONLY") == "1":
                P.dbg = {"d_ctq": (A.ctq.ap(), BF16), "d_ctkv": (A.ctkv.ap(), BF16),
                         "d_wdq": (wdq.ap(), BF16), "d_dqb": (A.dqb.ap(), F32)}
                return
    A.dtab = None
    A.wuq = [es.enter_context(nc.sbuf_tensor("t_wuq0", [128, 4 * 128], BF16))]
    A.wqr = [es.enter_context(nc.sbuf_tensor("t_wqr0", [128, 4 * 128], BF16))]
    A.wuk = [es.enter_context(nc.sbuf_tensor("t_wuk0", [128, 4 * 128], BF16))]
    A.wkr = [es.enter_context(nc.sbuf_tensor("t_wkr0", [128, NDC * 128], BF16))]
    A.wuv = [es.enter_context(nc.sbuf_tensor("t_wuv0", [128, 4 * 128], BF16))]
    A.ub = es.enter_context(nc.sbuf_tensor("t_ub", [128, 4 * 8], F32))
    A.krb = es.enter_context(nc.sbuf_tensor("t_krb", [128, 8], F32))
    A.qT = es.enter_context(nc.sbuf_tensor("t_qT", [128, T], BF16))
    A.kT = es.enter_context(nc.sbuf_tensor("t_kT", [128, T], BF16))
    A.tmpT = es.enter_context(nc.sbuf_tensor("t_tmpT", [128, T], BF16))
    A.vc = es.enter_context(nc.sbuf_tensor("t_vc", [128, NTT * 128], BF16))
    A.oT = es.enter_context(nc.sbuf_tensor("t_oT", [128, NDC * T], BF16))
    A.Psb = es.enter_context(nc.sbuf_tensor("t_Psb", [128, T], BF16))
    A.PT = es.enter_context(nc.sbuf_tensor("t_PT", [128, NTT * 128], BF16))
    A.rs = es.enter_context(nc.sbuf_tensor("t_rs", [128, 64], F32))
    A.osb = es.enter_context(nc.sbuf_tensor("t_osb", [128, 64], BF16))
    A.cosb = es.enter_context(nc.sbuf_tensor("t_cosb", [128, T], BF16))
    A.sinb = es.enter_context(nc.sbuf_tensor("t_sinb", [128, T], BF16))
    A.rot = es.enter_context(nc.sbuf_tensor("t_rot", [128, 512], BF16))
    A.borow = es.enter_context(nc.sbuf_tensor("t_borow", [1, D], BF16))
    t_ub = s.op("sync", lambda e: e.dma_start(out=A.ub[:], in_=P.W["b1_ub"]),
                wait=list(gsafe0) + list(ct_done), sig="d_b1ub", n=16)
    t_krb = s.op("sync", lambda e: e.dma_start(out=A.krb[:], in_=P.W["b1_krb"]),
                 wait=list(gsafe0) + list(ct_done), sig="d_b1krb", n=16)
    t_bo = s.op("gpsimd", lambda e: e.dma_start(out=A.borow[:], in_=P.W["b1_bo"]),
                wait=gsafe0, sig="d_b1bo", n=16)
    bload = [t_ub, t_krb]
    attn_T_done = {}
    proj_T_done = {}
    tab_tok = {}
    with ExitStack() as esc:
        C = NS(
            sb=[esc.enter_context(nc.psum_tensor(f"t_sp{i}", [128, 512], F32))
                for i in range(4)],
            avb=esc.enter_context(nc.psum_tensor("t_av", [128, 512], F32)),
            tbk=[esc.enter_context(nc.psum_tensor(f"t_tb{i}", [128, 1024], BF16))
                 for i in range(2)],
            bank_free={}, rs_free={sl: None for sl in range(4)},
            psb_war=None, prev_exp=None)
        sb, tbk, bank_free = C.sb, C.tbk, C.bank_free
        ctqv = A.ctq.ap().rearrange("p (c t) -> p c t", c=4)
        ctkvv = A.ctkv.ap().rearrange("p (c t) -> p c t", c=4)
        for oc in range(8):
            par = 0
            wl = {}
            for nm, buf, nch in (("b1_wuq", A.wuq[par], 4), ("b1_wqr", A.wqr[par], 4),
                                 ("b1_wuk", A.wuk[par], 4), ("b1_wkr", A.wkr[par], 8),
                                 ("b1_wuv", A.wuv[par], 4)):
                w = (list(gsafe0) + list(ct_done)) if oc == 0 else [proj_T_done[oc - 1]]
                wl[nm] = s.op("gpsimd", lambda e, nm=nm, buf=buf, nch=nch, oc=oc: e.dma_start(
                    out=buf.ap().rearrange("p (c o) -> p c o", c=nch),
                    in_=P.W[nm][:, 128 * oc:128 * oc + 128].rearrange(
                        "(c p) o -> p c o", p=128)), wait=w, sig=f"d_{nm}", n=16)
            wtb = [tab_tok[oc - 1]] if oc >= 1 else list(gsafe0)
            t_cos = s.op("gpsimd", lambda e, oc=oc: e.dma_start(
                out=A.cosb[:], in_=P.W["b1_cos"][oc]), wait=wtb, sig="d_b1cos", n=16)
            t_sin = s.op("gpsimd", lambda e, oc=oc: e.dma_start(
                out=A.sinb[:], in_=P.W["b1_sin"][oc]), wait=wtb, sig="d_b1sin", n=16)
            war = [attn_T_done[oc - 1]] if oc >= 1 else []
            wuqv = A.wuq[par].ap().rearrange("p (c o) -> p c o", c=4)
            wqrv = A.wqr[par].ap().rearrange("p (c o) -> p c o", c=4)
            wukv = A.wuk[par].ap().rearrange("p (c o) -> p c o", c=4)
            wkrv = A.wkr[par].ap().rearrange("p (c o) -> p c o", c=NDC)
            wuvv = A.wuv[par].ap().rearrange("p (c o) -> p c o", c=NDC // 2)
            jobs = (
                ("qc", A.qT, wuqv, 4, ctqv, A.ub, 0, QSCALE, False),
                ("qr", A.tmpT, wqrv, 4, ctqv, A.ub, 1, QSCALE, True),
                ("kc", A.kT, wukv, 4, ctkvv, A.ub, 2, 1.0, False),
                ("kr", A.tmpT, wkrv, 8, None, A.krb, 0, 1.0, True),
                ("v", A.tmpT, wuvv, 4, ctkvv, A.ub, 3, 1.0, False),
            )
            rot_free = None
            ev_toks = {}
            for (jn, dst, wview, nch, srcv, btile, bcol, scale, do_rope) in jobs:
                wtok = wl[{"qc": "b1_wuq", "qr": "b1_wqr", "kc": "b1_wuk",
                           "kr": "b1_wkr", "v": "b1_wuv"}[jn]]
                for tblk in range(4):
                    bi = tblk % 2
                    bank = sb[bi]
                    w = [wtok, prol.done_V] + bload
                    if jn in ("qc", "qr"):
                        w += list(ct_done) if oc == 0 else []
                    if bank_free.get(("s", bi)) is not None:
                        w.append(bank_free[("s", bi)])
                    tok = None
                    for c in range(nch):
                        if srcv is None:
                            rhs = xnT_slice(P, c, 512 * tblk, 512)
                        else:
                            rhs = None
                        tok = s.op("tensor", lambda e, wview=wview, c=c, tblk=tblk, bank=bank, srcv=srcv, nch=nch: e.matmul(
                            bank[:], wview[:, c, :],
                            (xnT_slice(P, c, 512 * tblk, 512) if srcv is None
                             else srcv[:, c, 512 * tblk:512 * tblk + 512]),
                            start=(c == 0), stop=(c == nch - 1)), wait=w, sig="T")
                        w = []
                    ww = [tok] + war
                    bcol_ap = btile[:, 8 * bcol + oc: 8 * bcol + oc + 1] if btile is A.ub \
                        else btile[:, oc:oc + 1]
                    if jn == "qc":
                        t_ev = s.op("vector", lambda e, dst=dst, tblk=tblk, bank=bank, bcol_ap=bcol_ap, scale=scale: e.tensor_scalar(
                            dst[:, 512 * tblk:512 * tblk + 512], bank[:],
                            bcol_ap, scale, ALU.add, ALU.mult), wait=ww, sig="V")
                    elif scale != 1.0:
                        t_ev = s.op("vector", lambda e, dst=dst, tblk=tblk, bank=bank, bcol_ap=bcol_ap, scale=scale: e.tensor_scalar(
                            dst[:, 512 * tblk:512 * tblk + 512], bank[:],
                            bcol_ap, scale, ALU.add, ALU.mult), wait=ww, sig="V")
                    else:
                        t_ev = s.op("vector", lambda e, dst=dst, tblk=tblk, bank=bank, bcol_ap=bcol_ap: e.tensor_scalar(
                            dst[:, 512 * tblk:512 * tblk + 512], bank[:],
                            bcol_ap, None, ALU.add), wait=ww, sig="V")
                    bank_free[("s", bi)] = t_ev
                    ev_toks[(jn, tblk)] = t_ev
                    if do_rope:
                        sl_ = slice(512 * tblk, 512 * tblk + 512)
                        wz = [t_ev, t_cos, t_sin]
                        if rot_free is not None:
                            wz.append(rot_free)
                        cps = []
                        for (d0, s0) in ((0, 32), (32, 0), (64, 96), (96, 64)):
                            cps.append(s.op("vector", lambda e, dst=dst, d0=d0, s0=s0, sl_=sl_: e.tensor_copy(
                                A.rot[d0:d0 + 32, :], dst[s0:s0 + 32, sl_]),
                                wait=wz if not cps else [], sig="V"))
                        t_m2 = s.op("vector", lambda e, sl_=sl_: e.tensor_mul(
                            A.rot[:], A.rot[:], A.sinb[:, sl_]), wait=[cps[-1]], sig="V")
                        t_m1 = s.op("vector", lambda e, dst=dst, sl_=sl_: e.tensor_mul(
                            dst[:, sl_], dst[:, sl_], A.cosb[:, sl_]),
                            wait=[cps[-1]], sig="V")
                        acc = A.qT if jn == "qr" else A.kT
                        base_tok = ev_toks[("qc" if jn == "qr" else "kc", tblk)]
                        rot_free = s.op("vector", lambda e, dst=dst, acc=acc, sl_=sl_: e.scalar_tensor_tensor(
                            dst[:, sl_], dst[:, sl_], 1.0, A.rot[:],
                            ALU.mult, ALU.add), wait=[t_m1, t_m2], sig="V")
                        s.op("vector", lambda e, dst=dst, acc=acc, sl_=sl_: e.tensor_add(
                            acc[:, sl_], acc[:, sl_], dst[:, sl_]),
                            wait=[rot_free, base_tok], sig="V")
                if jn == "v":
                    vT_done = s.tok("V")
                    for g in range(2):
                        w = [vT_done, P.setup_tok] + war
                        if bank_free.get(("tb", g)) is not None:
                            w.append(bank_free[("tb", g)])
                        tok = None
                        for j in range(8):
                            tt = 8 * g + j
                            tok = s.op("tensor", lambda e, g=g, j=j, tt=tt: e.transpose(
                                tbk[g][:, 128 * j:128 * j + 128],
                                A.tmpT[:, 128 * tt:128 * tt + 128], P.ident[:]),
                                wait=w, sig="T")
                            w = []
                        t_ev2 = s.op("vector", lambda e, g=g: e.tensor_copy(
                            A.vc[:, 1024 * g:1024 * g + 1024], tbk[g][:]),
                            wait=[tok], sig="V")
                        bank_free[("tb", g)] = t_ev2
            tab_tok[oc] = s.tok("V")
            v_done = s.tok("V")
            q_done = s.tok("V")
            proj_T_done[oc] = s.tok("T")
            for h2 in range(2):
                _attn_head(s, P, A, C, oc, 64 * h2, q_done, v_done, None,
                           causal=False, stt_scalar=None)
            attn_T_done[oc] = s.tok("T")
    attn_done = s.toks("T", "V", "SC")
    P.dbg = {"d_ctq": (A.ctq.ap(), BF16), "d_ctkv": (A.ctkv.ap(), BF16),
             "d_qT": (A.qT.ap(), BF16), "d_kT": (A.kT.ap(), BF16),
             "d_vc": (A.vc.ap(), BF16), "d_oT": (A.oT.ap(), BF16),
             "d_Psb": (A.Psb.ap(), BF16), "d_rs": (A.rs.ap(), F32)}
    _emit_out_proj_streamed(s, nc, es, P, A, prol, P.W["b1_wo"], attn_done,
                            A.borow, t_bo, "b1")


def emit_attn_b0(s, nc, es, P):
    DN = float(64.0 ** -0.25)
    RATIO = 0.0625
    EPS = 1e-4
    A = NS()
    with ExitStack() as es1:
        A.scr = es1.enter_context(nc.sbuf_tensor("t_scr", [128, D], F32))
        A.xn = es1.enter_context(nc.sbuf_tensor("t_xn", [128, 2 * D], BF16))
        with ExitStack() as esp:
            A.psumb = [esp.enter_context(nc.psum_tensor(f"t_psb{i}", [128, 1024], BF16))
                       for i in range(2)]
            prol = emit_prologue(s, P, A)
    gsafe0 = s.toks("T", "V", "SC")
    A.kT = es.enter_context(nc.sbuf_tensor("t_kT", [128, NDC * T], BF16))
    A.oT = es.enter_context(nc.sbuf_tensor("t_oT", [128, NDC * T], BF16))
    A.wk = [es.enter_context(nc.sbuf_tensor(f"t_wk{i}", [128, NDC * 128], BF16))
            for i in range(2)]
    A.wq = [es.enter_context(nc.sbuf_tensor(f"t_wq{i}", [128, NDC * 128], BF16))
            for i in range(2)]
    A.wv = [es.enter_context(nc.sbuf_tensor(f"t_wv{i}", [128, NDC * 128], BF16))
            for i in range(2)]
    A.ptsb = es.enter_context(nc.sbuf_tensor("t_ptsb", [128, 256], BF16))
    A.onescol = es.enter_context(nc.sbuf_tensor("t_onescol", [128, 1], BF16))
    A.qT = es.enter_context(nc.sbuf_tensor("t_qT", [128, T], BF16))
    A.sqc = es.enter_context(nc.sbuf_tensor("t_sqc", [128, T], BF16))
    A.vca = es.enter_context(nc.sbuf_tensor("t_vca", [128, NTT * 130], BF16))
    A.vTs = es.enter_context(nc.sbuf_tensor("t_vTs", [128, T], BF16))
    A.diagk = es.enter_context(nc.sbuf_tensor("t_diagk", [128, 256], F32))
    A.negdm = es.enter_context(nc.sbuf_tensor("t_negdm", [128, 256], F32))
    A.gmax = es.enter_context(nc.sbuf_tensor("t_gmax", [128, 8], F32))
    A.kp = es.enter_context(nc.sbuf_tensor("t_kp", [128, 256], BF16))
    A.qp = es.enter_context(nc.sbuf_tensor("t_qp", [128, 256], BF16))
    A.expsb = es.enter_context(nc.sbuf_tensor("t_expsb", [128, 256], F32))
    A.qpT = es.enter_context(nc.sbuf_tensor("t_qpT", [128, 256], BF16))
    A.ctx = es.enter_context(nc.sbuf_tensor("t_ctx", [128, 130], BF16))
    A.osb = es.enter_context(nc.sbuf_tensor("t_osb", [128, 64], BF16))
    A.ccs = es.enter_context(nc.sbuf_tensor("t_ccs", [128, 128], F32))
    A.borow = es.enter_context(nc.sbuf_tensor("t_borow", [1, D], BF16))
    cc_in = nc.dram_tensor("b0cc_in", [128, 128], F32)
    cc_out = nc.dram_tensor("b0cc_out", [128, 128], F32)
    t_pt = s.op("gpsimd", lambda e: e.dma_start(out=A.ptsb[:], in_=P.W["b0_pt"]),
                wait=gsafe0, sig="d_b0pt", n=16)
    t_bo = s.op("gpsimd", lambda e: e.dma_start(out=A.borow[:], in_=P.W["b0_bo"]),
                wait=gsafe0, sig="d_b0bo", n=16)
    t_vca1 = s.op("gpsimd", lambda e: e.memset(A.vca[:], 1.0), wait=gsafe0, sig="P")
    t_oc1 = s.op("gpsimd", lambda e: e.memset(A.onescol[:], 1.0), wait=gsafe0, sig="P")
    t_gm0 = s.op("vector", lambda e: e.memset(A.gmax[:, 0:1], -3e38),
                 wait=gsafe0, sig="V")
    proj_T_done = {}
    with ExitStack() as esc:
        sbk = [esc.enter_context(nc.psum_tensor(f"t_sp{i}", [128, 512], F32))
               for i in range(2)]
        dgb = esc.enter_context(nc.psum_tensor("t_dg", [128, 512], F32))
        ctxb = [esc.enter_context(nc.psum_tensor(f"t_cx{i}", [128, 512], F32))
                for i in range(2)]
        outb = esc.enter_context(nc.psum_tensor("t_ob", [128, 512], F32))
        tbk = [esc.enter_context(nc.psum_tensor(f"t_tb{i}", [128, 1024], BF16))
               for i in range(2)]
        bank_free = {}
        t_gmax = t_gm0
        t_diag_prev = None
        for oc in range(8):
            par = oc % 2
            w = list(gsafe0) if oc < 2 else [proj_T_done[oc - 2]]
            t_wk = s.op("gpsimd", lambda e, oc=oc, par=par: e.dma_start(
                out=A.wk[par].ap().rearrange("p (c o) -> p c o", c=NDC),
                in_=P.W["b0_wk"][:, 128 * oc:128 * oc + 128].rearrange(
                    "(c p) o -> p c o", p=128)), wait=w, sig=f"d_b0wk_{par}", n=16)
            wks = A.wk[par].ap().rearrange("p (c o) -> p c o", c=NDC)
            for tblk in range(4):
                bi = tblk % 2
                bank = sbk[bi]
                w = [t_wk, prol.done_V]
                if bank_free.get(("s", bi)) is not None:
                    w.append(bank_free[("s", bi)])
                tok = None
                for c in range(NDC):
                    tok = s.op("tensor", lambda e, wks=wks, c=c, tblk=tblk, bank=bank: e.matmul(
                        bank[:], wks[:, c, :], xnT_slice(P, c, 512 * tblk, 512),
                        start=(c == 0), stop=(c == NDC - 1)), wait=w, sig="T")
                    w = []
                t_ev = s.op("scalar", lambda e, oc=oc, tblk=tblk, bank=bank: e.mul(
                    A.kT[:, T * oc + 512 * tblk: T * oc + 512 * tblk + 512],
                    bank[:], DN), wait=[tok], sig="SC")
                bank_free[("s", bi)] = t_ev
            k_done = s.tok("SC")
            t_sq = s.op("scalar", lambda e, oc=oc: e.activation(
                out=A.sqc[:], in_=A.kT[:, T * oc:T * oc + T], func=AF.Square,
                bias=0.0, scale=1.0), wait=[k_done], sig="SC")
            for h2 in range(2):
                h = 2 * oc + h2
                r0 = 64 * h2
                for tt in range(NTT):
                    bi = tt % 2
                    bank = sbk[bi]
                    w = [k_done, t_pt]
                    if bank_free.get(("s", bi)) is not None:
                        w.append(bank_free[("s", bi)])
                    t_dd = s.op("tensor", lambda e, oc=oc, r0=r0, tt=tt, bank=bank: e.matmul(
                        bank[:, 0:256],
                        A.kT[r0:r0 + 64, T * oc + 128 * tt: T * oc + 128 * tt + 128],
                        A.ptsb[r0:r0 + 64, :], start=True, stop=True), wait=w, sig="T")
                    t_mx = s.op("vector", lambda e, bank=bank: e.reduce_max(
                        A.gmax[:, 1:2], bank[:, 0:256], axis=AX),
                        wait=[t_dd, t_gmax], sig="V")
                    bank_free[("s", bi)] = t_mx
                    t_gmax = s.op("vector", lambda e: e.tensor_max(
                        A.gmax[:, 0:1], A.gmax[:, 0:1], A.gmax[:, 1:2]),
                        wait=[t_mx], sig="V")
                    w = [t_sq]
                    if t_diag_prev is not None:
                        w.append(t_diag_prev)
                    t_dg = s.op("tensor", lambda e, oc=oc, r0=r0, tt=tt: e.matmul(
                        dgb[:, 0:1],
                        A.sqc[r0:r0 + 64, 128 * tt:128 * tt + 128],
                        A.onescol[r0:r0 + 64, 0:1],
                        start=True, stop=True), wait=w + [t_oc1], sig="T")
                    t_diag_prev = s.op("scalar", lambda e, h=h, tt=tt: e.mul(
                        A.diagk[:, 16 * h + tt:16 * h + tt + 1], dgb[:, 0:1], 0.5),
                        wait=[t_dg], sig="SC")
            proj_T_done[oc] = s.tok("T")
        t_c1 = s.op("vector", lambda e: e.tensor_copy(A.kp[:, 0:1], A.gmax[:, 0:1]),
                    wait=[t_gmax], sig="V")
        t_c2 = s.op("tensor", lambda e: e.transpose(
            tbk[0][0:1, 0:128], A.kp[:, 0:1], P.ident[:]),
            wait=[t_c1, P.setup_tok], sig="T")
        t_mz = s.op("vector", lambda e: e.memset(A.ccs[:], -3e38), wait=[t_gmax], sig="V")
        t_c8 = s.op("vector", lambda e: e.reduce_max(
            A.ccs[0:1, 0:1], tbk[0][0:1, 0:128], axis=AX),
            wait=[t_c2, t_mz], sig="V")
        t_d1 = s.op("gpsimd", lambda e: e.dma_start(out=cc_in.ap(), in_=A.ccs[:, 0:128]),
                    wait=[t_c8], sig="d_cc1", n=16)
        t_cc = s.op("gpsimd", lambda e: e.collective_compute(
            "AllReduce", ALU.max, replica_groups=[list(range(8))],
            ins=[cc_in.ap().opt()], outs=[cc_out.ap().opt()]),
            wait=[t_d1], sig="ccv", n=1)
        t_d2 = s.op("gpsimd", lambda e: e.dma_start(out=A.ccs[:, 0:128], in_=cc_out.ap()),
                    wait=[t_cc], sig="d_cc2", n=16)
        t_on = s.op("vector", lambda e: e.memset(A.expsb[0:1, 0:128], 1.0),
                    wait=[t_c8], sig="V")
        t_bc = s.op("tensor", lambda e: e.matmul(
            dgb[:, 1:2], A.expsb[0:1, 0:128], A.ccs[0:1, 0:1],
            start=True, stop=True), wait=[t_d2, t_on, t_diag_prev], sig="T")
        t_n1 = s.op("vector", lambda e: e.tensor_copy(A.gmax[:, 2:3], dgb[:, 1:2]),
                    wait=[t_bc], sig="V")
        t_n2 = s.op("vector", lambda e: e.tensor_scalar(
            A.negdm[:], A.diagk[:], A.gmax[:, 2:3], -1.0, ALU.add, ALU.mult),
            wait=[t_n1], sig="V")
        negdm_done = t_n2
        attn_T_done = {}
        for oc in range(8):
            par = oc % 2
            w = [negdm_done] if oc < 2 else [attn_T_done[oc - 2]]
            t_wq = s.op("gpsimd", lambda e, oc=oc, par=par: e.dma_start(
                out=A.wq[par].ap().rearrange("p (c o) -> p c o", c=NDC),
                in_=P.W["b0_wq"][:, 128 * oc:128 * oc + 128].rearrange(
                    "(c p) o -> p c o", p=128)), wait=w, sig=f"d_b0wq_{par}", n=16)
            t_wv = s.op("gpsimd", lambda e, oc=oc, par=par: e.dma_start(
                out=A.wv[par].ap().rearrange("p (c o) -> p c o", c=NDC),
                in_=P.W["b0_wv"][:, 128 * oc:128 * oc + 128].rearrange(
                    "(c p) o -> p c o", p=128)), wait=w, sig=f"d_b0wv_{par}", n=16)
            wqs = A.wq[par].ap().rearrange("p (c o) -> p c o", c=NDC)
            wvs = A.wv[par].ap().rearrange("p (c o) -> p c o", c=NDC)
            war = [attn_T_done[oc - 1]] if oc >= 1 else []
            for src_i, (wview, wtok, kind) in enumerate(((wqs, t_wq, "q"), (wvs, t_wv, "v"))):
                for tblk in range(4):
                    bi = tblk % 2
                    bank = sbk[bi]
                    w = [wtok, prol.done_V]
                    if bank_free.get(("s", bi)) is not None:
                        w.append(bank_free[("s", bi)])
                    tok = None
                    for c in range(NDC):
                        tok = s.op("tensor", lambda e, wview=wview, c=c, tblk=tblk, bank=bank: e.matmul(
                            bank[:], wview[:, c, :], xnT_slice(P, c, 512 * tblk, 512),
                            start=(c == 0), stop=(c == NDC - 1)), wait=w, sig="T")
                        w = []
                    if kind == "q":
                        t_ev = s.op("scalar", lambda e, tblk=tblk, bank=bank: e.mul(
                            A.qT[:, 512 * tblk:512 * tblk + 512], bank[:], DN),
                            wait=[tok] + war, sig="SC")
                    else:
                        t_ev = s.op("vector", lambda e, tblk=tblk, bank=bank: e.tensor_copy(
                            A.vTs[:, 512 * tblk:512 * tblk + 512], bank[:]),
                            wait=[tok] + war, sig="V")
                    bank_free[("s", bi)] = t_ev
            q_done = s.tok("SC")
            vT_done = s.tok("V")
            t_sq = s.op("scalar", lambda e: e.activation(
                out=A.sqc[:], in_=A.qT[:], func=AF.Square, bias=0.0, scale=1.0),
                wait=[q_done], sig="SC")
            for g in range(2):
                w = [vT_done, P.setup_tok, t_vca1] + war
                if bank_free.get(("tb", g)) is not None:
                    w.append(bank_free[("tb", g)])
                tok = None
                for j in range(8):
                    tt = 8 * g + j
                    tok = s.op("tensor", lambda e, g=g, j=j, tt=tt: e.transpose(
                        tbk[g][:, 128 * j:128 * j + 128],
                        A.vTs[:, 128 * tt:128 * tt + 128], P.ident[:]),
                        wait=w, sig="T")
                    w = []
                tks = []
                for j in range(8):
                    tt = 8 * g + j
                    tks.append(s.op("vector", lambda e, g=g, j=j, tt=tt: e.tensor_copy(
                        A.vca[:, 130 * tt:130 * tt + 64],
                        tbk[g][:, 128 * j:128 * j + 64]), wait=[tok], sig="V"))
                    tks.append(s.op("vector", lambda e, g=g, j=j, tt=tt: e.tensor_copy(
                        A.vca[:, 130 * tt + 65:130 * tt + 129],
                        tbk[g][:, 128 * j + 64:128 * j + 128]), wait=[tok], sig="V"))
                bank_free[("tb", g)] = tks[-1]
            v_done = s.tok("V")
            for h2 in range(2):
                h = 2 * oc + h2
                r0 = 64 * h2
                ctx_mm = None
                prev_kp_ev = None
                for tt in range(NTT):
                    bi = tt % 2
                    bank = sbk[bi]
                    w = [negdm_done, v_done]
                    if bank_free.get(("s", bi)) is not None:
                        w.append(bank_free[("s", bi)])
                    t_dd = s.op("tensor", lambda e, oc=oc, r0=r0, tt=tt, bank=bank: e.matmul(
                        bank[:, 0:256],
                        A.kT[r0:r0 + 64, T * oc + 128 * tt: T * oc + 128 * tt + 128],
                        A.ptsb[r0:r0 + 64, :], start=True, stop=True), wait=w, sig="T")
                    wv_ = [t_dd]
                    if prev_kp_ev is not None:
                        wv_.append(prev_kp_ev)
                    t_ex = s.op("scalar", lambda e, h=h, tt=tt, bank=bank: e.activation(
                        out=A.expsb[:, 0:256], in_=bank[:, 0:256], func=AF.Exp,
                        bias=A.negdm[:, 16 * h + tt:16 * h + tt + 1], scale=1.0),
                        wait=wv_, sig="SC")
                    bank_free[("s", bi)] = t_ex
                    t_kp = s.op("vector", lambda e: e.tensor_scalar(
                        A.kp[:], A.expsb[:, 0:256], RATIO, RATIO * EPS,
                        ALU.mult, ALU.add), wait=[t_ex], sig="V")
                    for mc in range(2):
                        w = [t_kp]
                        if tt == 0 and bank_free.get(("cx", mc)) is not None:
                            w.append(bank_free[("cx", mc)])
                        ctx_mm = s.op("tensor", lambda e, mc=mc, tt=tt, h2=h2: e.matmul(
                            ctxb[mc][:, 0:65], A.kp[:, 128 * mc:128 * mc + 128],
                            A.vca[:, 130 * tt + 65 * h2: 130 * tt + 65 * h2 + 65],
                            start=(tt == 0), stop=(tt == NTT - 1)), wait=w, sig="T")
                    prev_kp_ev = ctx_mm
                tcx = []
                for mc in range(2):
                    tcx.append(s.op("vector", lambda e, mc=mc: e.tensor_copy(
                        A.ctx[:, 65 * mc:65 * mc + 65], ctxb[mc][:, 0:65]),
                        wait=[ctx_mm], sig="V"))
                    bank_free[("cx", mc)] = tcx[-1]
                ctx_done = tcx[-1]
                prev_q = None
                prev_qp = None
                for tt in range(NTT):
                    bi = tt % 2
                    bank = sbk[bi]
                    w = [t_sq, q_done]
                    if bank_free.get(("s", bi)) is not None:
                        w.append(bank_free[("s", bi)])
                    t_dd = s.op("tensor", lambda e, r0=r0, tt=tt, bank=bank: e.matmul(
                        bank[:, 0:256],
                        A.qT[r0:r0 + 64, 128 * tt:128 * tt + 128],
                        A.ptsb[r0:r0 + 64, :], start=True, stop=True), wait=w, sig="T")
                    wdg = [t_sq, negdm_done]
                    if prev_q is not None:
                        wdg.append(prev_q)
                    t_dg = s.op("tensor", lambda e, r0=r0, tt=tt: e.matmul(
                        dgb[:, 0:1], A.sqc[r0:r0 + 64, 128 * tt:128 * tt + 128],
                        A.onescol[r0:r0 + 64, 0:1],
                        start=True, stop=True), wait=wdg, sig="T")
                    t_m = s.op("vector", lambda e, bank=bank: e.reduce_max(
                        A.gmax[:, 3:4], bank[:, 0:256], axis=AX), wait=[t_dd], sig="V")
                    t_d5 = s.op("scalar", lambda e: e.mul(
                        A.gmax[:, 4:5], dgb[:, 0:1], 0.5), wait=[t_dg], sig="SC")
                    t_b = s.op("vector", lambda e: e.tensor_add(
                        A.gmax[:, 5:6], A.gmax[:, 3:4], A.gmax[:, 4:5]),
                        wait=[t_m, t_d5], sig="V")
                    t_bn = s.op("vector", lambda e: e.tensor_scalar_mul(
                        A.gmax[:, 6:7], A.gmax[:, 5:6], -1.0), wait=[t_b], sig="V")
                    wex = [t_bn]
                    if prev_qp is not None:
                        wex.append(prev_qp)
                    t_ex = s.op("scalar", lambda e, bank=bank: e.activation(
                        out=A.expsb[:, 0:256], in_=bank[:, 0:256], func=AF.Exp,
                        bias=A.gmax[:, 6:7], scale=1.0), wait=wex, sig="SC")
                    bank_free[("s", bi)] = t_ex
                    t_qp = s.op("vector", lambda e: e.tensor_scalar(
                        A.qp[:], A.expsb[:, 0:256], RATIO, RATIO * EPS,
                        ALU.mult, ALU.add), wait=[t_ex], sig="V")
                    prev_qp = t_qp
                    w = [t_qp, P.setup_tok]
                    if bank_free.get(("tb", 0)) is not None:
                        w.append(bank_free[("tb", 0)])
                    tok = None
                    for mc in range(2):
                        tok = s.op("tensor", lambda e, mc=mc: e.transpose(
                            tbk[0][:, 128 * mc:128 * mc + 128],
                            A.qp[:, 128 * mc:128 * mc + 128], P.ident[:]),
                            wait=w, sig="T")
                        w = []
                    t_qpT = s.op("vector", lambda e: e.tensor_copy(
                        A.qpT[:], tbk[0][:, 0:256]), wait=[tok], sig="V")
                    bank_free[("tb", 0)] = t_qpT
                    t_om = None
                    for mc in range(2):
                        w = [t_qpT, ctx_done]
                        if mc == 0 and bank_free.get(("ob",)) is not None:
                            w.append(bank_free[("ob",)])
                        t_om = s.op("tensor", lambda e, mc=mc: e.matmul(
                            outb[:, 0:65], A.qpT[:, 128 * mc:128 * mc + 128],
                            A.ctx[:, 65 * mc:65 * mc + 65],
                            start=(mc == 0), stop=(mc == 1)), wait=w, sig="T")
                    t_rc = s.op("vector", lambda e: e.reciprocal(
                        A.gmax[:, 7:8], outb[:, 64:65]), wait=[t_om], sig="V")
                    t_oev = s.op("vector", lambda e: e.tensor_scalar_mul(
                        A.osb[:, 0:64], outb[:, 0:64], A.gmax[:, 7:8]),
                        wait=[t_rc], sig="V")
                    bank_free[("ob",)] = t_oev
                    prev_q = t_oev
                    w = [t_oev, P.setup_tok]
                    if bank_free.get(("tb", 1)) is not None:
                        w.append(bank_free[("tb", 1)])
                    t_otr = s.op("tensor", lambda e: e.transpose(
                        tbk[1][0:64, 0:128], A.osb[:, 0:64], P.ident[:]),
                        wait=w, sig="T")
                    t_ofin = s.op("vector", lambda e, r0=r0, oc=oc, tt=tt: e.tensor_copy(
                        A.oT[r0:r0 + 64, T * oc + 128 * tt: T * oc + 128 * tt + 128],
                        tbk[1][0:64, 0:128]), wait=[t_otr], sig="V")
                    bank_free[("tb", 1)] = t_ofin
            attn_T_done[oc] = s.tok("T")
    attn_done = s.toks("T", "V", "SC")
    _emit_out_proj_streamed(s, nc, es, P, A, prol, P.W["b0_wo"], attn_done,
                            A.borow, t_bo, "b0")


def emit_ffn0_sparse(s, P, A, prol, gate_done, gbank, gsafe):
    CAP = 512
    t_b1 = s.op("sync", lambda e: e.dma_start(out=A.b1[:], in_=P.W["f0_b1"]),
                wait=gsafe, sig="d_f0b1", n=16)
    w1v = P.WB["f0_w1"]
    w2v = P.WB["f0_w2"]
    t_l1 = s.op("gpsimd", lambda e: e.memset(A.ltri[:], 1.0), wait=gsafe, sig="P")
    t_lt = s.op("gpsimd", lambda e: e.affine_select(
        out=A.ltri[:], in_=A.ltri[:], pattern=[[1, 128]],
        compare_op=ALU.is_ge, fill=0.0, base=0, channel_multiplier=-1),
        wait=[t_l1], sig="P")
    t_ao = s.op("gpsimd", lambda e: e.memset(A.allones[:], 1.0), wait=gsafe, sig="P")
    t_ei = s.op("gpsimd", lambda e: e.iota(
        A.eiota[:], pattern=[[1, 8]], base=0, channel_multiplier=0,
        allow_small_or_imprecise_dtypes=True), wait=gsafe, sig="P")
    cumbank = A.psumf[2]
    prev_cum = None
    for tt in range(NTT):
        w = [t_lt, t_ao, gate_done]
        if prev_cum is not None:
            w.append(prev_cum)
        tok = None
        for tp in range(tt):
            tok = s.op("tensor", lambda e, tp=tp: e.matmul(
                cumbank[:, 0:8], A.allones[:], A.ohb[:, 8 * tp:8 * tp + 8],
                start=(tp == 0), stop=False), wait=w, sig="T")
            w = []
        tok = s.op("tensor", lambda e, tt=tt: e.matmul(
            cumbank[:, 0:8], A.ltri[:], A.ohb[:, 8 * tt:8 * tt + 8],
            start=(tt == 0), stop=True), wait=w, sig="T")
        t_cum = s.op("vector", lambda e: e.tensor_copy(A.cums[:], cumbank[:, 0:8]),
                     wait=[tok], sig="V")
        prev_cum = t_cum
        t1 = s.op("vector", lambda e, tt=tt: e.tensor_mul(
            A.tmp8[:], A.oh[:, 8 * tt:8 * tt + 8], A.cums[:]), wait=[t_cum], sig="V")
        t2 = s.op("vector", lambda e: e.reduce_sum(
            A.pscr[:, 0:1], A.tmp8[:], axis=AX), wait=[t1], sig="V")
        t3 = s.op("vector", lambda e, tt=tt: e.tensor_mul(
            A.tmp8[:], A.oh[:, 8 * tt:8 * tt + 8], A.eiota[:]),
            wait=[t2, t_ei], sig="V")
        t4 = s.op("vector", lambda e: e.reduce_sum(
            A.pscr[:, 1:2], A.tmp8[:], axis=AX), wait=[t3], sig="V")
        t5 = s.op("vector", lambda e: e.scalar_tensor_tensor(
            A.pscr[:, 2:3], A.pscr[:, 1:2], float(CAP), A.pscr[:, 0:1],
            ALU.mult, ALU.add), wait=[t4], sig="V")
        t6 = s.op("vector", lambda e: e.tensor_scalar_add(
            A.pscr[:, 2:3], A.pscr[:, 2:3], -1.0), wait=[t5], sig="V")
        t7 = s.op("vector", lambda e: e.tensor_scalar(
            A.pscr[:, 3:4], A.pscr[:, 1:2], float(CAP), float(CAP - 1),
            ALU.mult, ALU.add), wait=[t4], sig="V")
        t8 = s.op("vector", lambda e, tt=tt: e.tensor_tensor(
            A.pos[:, tt:tt + 1], A.pscr[:, 2:3], A.pscr[:, 3:4], ALU.min),
            wait=[t6, t7], sig="V")
        t9 = s.op("vector", lambda e, tt=tt: e.reduce_sum(
            A.wsel[:, tt:tt + 1], A.gw[:, 8 * tt:8 * tt + 8], axis=AX),
            wait=[gate_done], sig="V")
    pos_done = s.tok("V")
    bank_free = {i: None for i in range(8)}
    bank_free[0] = gbank[0]
    bank_free[1] = gbank[1]
    bank_free[2] = prev_cum
    w1_war = [None, None]
    w2_last_T = [None, None]
    b2_last_T = None
    hT_last_T = None
    xg_last_T = None
    y_last_T = None
    siota_readers = None
    xn_war = {}
    pt_war = {}
    ptw_war = None
    permW_war = None
    for e_i in range(8):
        def load_w1q(q, extra):
            par = q % 2
            w = list(extra)
            if w1_war[par] is not None:
                w.append(w1_war[par])
            return s.op("sync", lambda e, e_i=e_i, q=q, par=par: e.dma_start(
                out=A.w1q[par].ap().rearrange("p (c f) -> p c f", c=NDC),
                in_=w1v[e_i].rearrange("(c p) f -> p c f", p=128)[:, :, 1024 * q:1024 * (q + 1)]),
                wait=w, sig=f"d_f0w1_{par}", n=16)
        w1tok = [load_w1q(0, gsafe if e_i == 0 else []),
                 load_w1q(1, gsafe if e_i == 0 else [])]
        wb2 = [b2_last_T] if b2_last_T is not None else list(gsafe)
        t_b2 = s.op("gpsimd", lambda e, e_i=e_i: e.dma_start(
            out=A.b2row[:], in_=P.W["f0_b2"][:, D * e_i:D * (e_i + 1)]),
            wait=wb2, sig="d_f0b2", n=16)
        wsi = [pos_done] if siota_readers is None else [siota_readers]
        t_si = s.op("gpsimd", lambda e, e_i=e_i: e.iota(
            A.siota[:], pattern=[[1, CAP]], base=CAP * e_i, channel_multiplier=0,
            allow_small_or_imprecise_dtypes=True), wait=wsi, sig="P")
        fwd_last = None
        for tt in range(NTT):
            slot = tt % 2
            pv, psc = prol.stats_done[tt]
            wx = [pv, psc]
            if xn_war.get(slot) is not None:
                wx.append(xn_war[slot])
            c0 = 8 * tt
            t_xn = s.op("vector", lambda e, tt=tt, slot=slot, c0=c0: e.tensor_scalar(
                A.xnroll[:, D * slot:D * slot + D], x_slice(P, tt),
                P.stat[:, c0 + 2:c0 + 3], P.stat[:, c0 + 5:c0 + 6],
                ALU.subtract, ALU.mult), wait=wx, sig="V")
            wp = [t_si, pos_done]
            if pt_war.get(slot) is not None:
                wp.append(pt_war[slot])
            t_pt = s.op("vector", lambda e, tt=tt, slot=slot: e.tensor_scalar(
                A.permT[:, CAP * slot:CAP * slot + CAP], A.siota[:],
                A.pos[:, tt:tt + 1], None, ALU.is_equal), wait=wp, sig="V")
            for dc in range(8):
                w = [t_xn, t_pt]
                if tt == 0 and bank_free.get(dc) is not None:
                    w.append(bank_free[dc])
                fwd_last = s.op("tensor", lambda e, tt=tt, slot=slot, dc=dc: e.matmul(
                    A.psumf[dc][:],
                    A.xnroll[:, D * slot + 128 * dc: D * slot + 128 * dc + 128],
                    A.permT[:, CAP * slot:CAP * slot + CAP],
                    start=(tt == 0), stop=(tt == NTT - 1)), wait=w, sig="T")
            xn_war[slot] = fwd_last
            pt_war[slot] = fwd_last
        xg_toks = []
        for dc in range(8):
            w = [fwd_last]
            if xg_last_T is not None:
                w.append(xg_last_T)
            xg_toks.append(s.op("scalar", lambda e, dc=dc: e.copy(
                A.xgT[:, CAP * dc:CAP * dc + CAP], A.psumf[dc][:]),
                wait=w, sig="SC"))
            bank_free[dc] = xg_toks[-1]
        xg_done = xg_toks[-1]
        q_end_tok = {}
        for fc in range(32):
            bank_i = fc % 8
            bank = A.psumf[bank_i]
            q = fc // 8
            par = q % 2
            if fc % 8 == 0 and fc >= 8 and q + 1 <= 3:
                w1tok.append(load_w1q(q + 1, [q_end_tok[q - 1]]))
            w = [w1tok[q], xg_done]
            if bank_free.get(bank_i) is not None:
                w.append(bank_free[bank_i])
            tok = None
            for c in range(NDC):
                tok = s.op("tensor", lambda e, fc=fc, c=c, par=par, bank=bank: e.matmul(
                    bank[:],
                    A.w1q[par].ap().rearrange("p (cc f) -> p cc f", cc=NDC)[:, c, 128 * (fc % 8):128 * (fc % 8) + 128],
                    A.xgT[:, CAP * c:CAP * c + CAP],
                    start=(c == 0), stop=(c == NDC - 1)), wait=w, sig="T")
                w = []
            wh = [tok, t_b1]
            if hT_last_T is not None:
                wh.append(hT_last_T)
            t_ev = s.op("scalar", lambda e, e_i=e_i, fc=fc, bank=bank: e.activation(
                out=A.hT[:, 512 * fc:512 * fc + 512], in_=bank[:],
                func=AF.Gelu, bias=A.b1[:, 32 * e_i + fc:32 * e_i + fc + 1],
                scale=1.0), wait=wh, sig="SC")
            bank_free[bank_i] = t_ev
            if fc % 8 == 7:
                q_end_tok[q] = tok
        phaseA_done = s.tok("SC")
        w1_war[0] = s.tok("T")
        w1_war[1] = s.tok("T")
        first_mm_waits = [phaseA_done]
        for i in range(8):
            if bank_free.get(i) is not None:
                first_mm_waits.append(bank_free[i])
        mm_tok = None
        for ks in range(16):
            par = ks % 2
            w = [] if w2_last_T[par] is not None or e_i > 0 else list(gsafe)
            if w2_last_T[par] is not None:
                w.append(w2_last_T[par])
            w2tok = s.op("sync", lambda e, e_i=e_i, ks=ks, par=par: e.dma_start(
                out=A.w2sb[par].ap().rearrange("p (kc d) -> p kc d", kc=2),
                in_=w2v[e_i].rearrange("(ks kc p) d -> ks p kc d", kc=2, p=128)[ks]),
                wait=w, sig=f"d_f0w2_{par}", n=16)
            for kc in range(2):
                fc = 2 * ks + kc
                for st_ in range(4):
                    for db in range(2):
                        bi = st_ * 2 + db
                        w = [w2tok] + (first_mm_waits if fc == 0 else [])
                        mm_tok = s.op("tensor", lambda e, fc=fc, kc=kc, st_=st_, db=db, par=par, bi=bi: e.matmul(
                            A.psumf[bi][:],
                            A.hT[:, 512 * fc + 128 * st_: 512 * fc + 128 * st_ + 128],
                            A.w2sb[par][:, 1024 * kc + 512 * db: 1024 * kc + 512 * db + 512],
                            start=(fc == 0), stop=False), wait=w, sig="T")
            w2_last_T[par] = mm_tok
        hT_last_T = mm_tok
        y_toks = []
        for st_ in range(4):
            for db in range(2):
                bi = st_ * 2 + db
                t_bias = s.op("tensor", lambda e, db=db, bi=bi: e.matmul(
                    A.psumf[bi][:], P.ones_row[0:1, 0:128],
                    A.b2row[0:1, 512 * db: 512 * db + 512],
                    start=False, stop=True), wait=[P.setup_tok, t_b2], sig="T")
                w = [t_bias]
                if y_last_T is not None:
                    w.append(y_last_T)
                y_toks.append(s.op("vector", lambda e, st_=st_, db=db, bi=bi: e.tensor_copy(
                    A.y_sb[:, 1024 * st_ + 512 * db: 1024 * st_ + 512 * db + 512],
                    A.psumf[bi][:]), wait=w, sig="V"))
                bank_free[bi] = y_toks[-1]
        y_done = y_toks[-1]
        b2_last_T = s.tok("T")
        t_add = None
        for tt in range(NTT):
            w = [t_si, pos_done]
            if ptw_war is not None:
                w.append(ptw_war)
            t_p1 = s.op("vector", lambda e, tt=tt: e.tensor_scalar(
                A.ptw[:], A.siota[:], A.pos[:, tt:tt + 1], None, ALU.is_equal),
                wait=w, sig="V")
            t_p2 = s.op("vector", lambda e, tt=tt: e.tensor_scalar_mul(
                A.ptw[:], A.ptw[:], A.wsel[:, tt:tt + 1]), wait=[t_p1], sig="V")
            tb_i = 4 + (tt % 2)
            trt = None
            for scj in range(4):
                w = [t_p2, P.setup_tok]
                if scj == 0 and bank_free.get(tb_i) is not None:
                    w.append(bank_free[tb_i])
                trt = s.op("tensor", lambda e, scj=scj, tb_i=tb_i: e.transpose(
                    A.psumf[tb_i][:, 128 * scj:128 * scj + 128],
                    A.ptw[:, 128 * scj:128 * scj + 128], P.identf[:]),
                    wait=w, sig="T")
            ptw_war = trt
            wpw = [trt]
            if permW_war is not None:
                wpw.append(permW_war)
            t_pw = s.op("vector", lambda e, tb_i=tb_i: e.tensor_copy(
                A.permW[:], A.psumf[tb_i][:]), wait=wpw, sig="V")
            bank_free[tb_i] = t_pw
            for db in range(2):
                bk_i = 6 + db
                bmm = None
                for sc in range(4):
                    w = [t_pw, y_done]
                    if sc == 0 and bank_free.get(bk_i) is not None:
                        w.append(bank_free[bk_i])
                    bmm = s.op("tensor", lambda e, sc=sc, db=db, bk_i=bk_i: e.matmul(
                        A.psumf[bk_i][:],
                        A.permW[:, 128 * sc:128 * sc + 128],
                        A.y_sb[:, 1024 * sc + 512 * db: 1024 * sc + 512 * db + 512],
                        start=(sc == 0), stop=(sc == 3)), wait=w, sig="T")
                if db == 1:
                    permW_war = bmm
                pv, psc = prol.stats_done[tt]
                t_add = s.op("vector", lambda e, tt=tt, db=db, bk_i=bk_i: e.tensor_add(
                    x_slice(P, tt, 512 * db, 512), A.psumf[bk_i][:],
                    x_slice(P, tt, 512 * db, 512)), wait=[bmm, pv, psc], sig="V")
                bank_free[bk_i] = t_add
        y_last_T = s.tok("T")
        xg_last_T = s.tok("T")
        siota_readers = t_add


def emit_ffn_sub(s, nc, es, P, kind):
    A = NS()
    A.gw = es.enter_context(nc.sbuf_tensor("a_gw", [128, NTT * 8], F32))
    A.oh = es.enter_context(nc.sbuf_tensor("a_oh", [128, NTT * 8], F32))
    A.ohb = es.enter_context(nc.sbuf_tensor("a_ohb", [128, NTT * 8], BF16))
    with ExitStack() as es1:
        A.scr = es1.enter_context(nc.sbuf_tensor("a_scr", [128, D], F32))
        A.xn = es1.enter_context(nc.sbuf_tensor("a_xn", [128, 2 * D], BF16))
        with ExitStack() as esp:
            A.psumb = [esp.enter_context(nc.psum_tensor(f"a_psb{i}", [128, 1024], BF16))
                       for i in range(2)]
            prol = emit_prologue(s, P, A)
        A.psumf = [es.enter_context(nc.psum_tensor(f"a_psf{i}", [128, 512], F32))
                   for i in range(8)]
        A.gws = es1.enter_context(nc.sbuf_tensor("a_gws", [128, 64], BF16))
        A.gst = es1.enter_context(nc.sbuf_tensor("a_gst", [128, NTT * 8], F32))
        A.glog = es1.enter_context(nc.sbuf_tensor("a_glog", [128, NTT * 8], F32))
        A.gtop = es1.enter_context(nc.sbuf_tensor("a_gtop", [128, NTT * 8], F32))
        A.gexp = es1.enter_context(nc.sbuf_tensor("a_gexp", [128, NTT * 8], F32))
        A.goh2 = es1.enter_context(nc.sbuf_tensor("a_goh2", [128, 8], F32))
        gw_name = "f0_gw" if kind == "ffn0" else "f1_gw"
        if _DEBUG_STAGE == "prologue":
            P.dbgA = A
            return
        gate_done, gbank = emit_moe_gating(s, P, A, prol, gw_name,
                                           topk=1 if kind == "ffn0" else 2)
    gsafe = s.toks("T", "V", "SC")
    P.dbgA = A
    if _DEBUG_STAGE in ("prologue", "gating"):
        return
    if kind == "ffn0" and SPARSE_FFN0:
        A.b1 = es.enter_context(nc.sbuf_tensor("a_b1", [128, 256], F32))
        A.b2row = es.enter_context(nc.sbuf_tensor("a_b2row", [1, D], BF16))
        A.pos = es.enter_context(nc.sbuf_tensor("a_pos", [128, 16], F32))
        A.wsel = es.enter_context(nc.sbuf_tensor("a_wsel", [128, 16], F32))
        A.pscr = es.enter_context(nc.sbuf_tensor("a_pscr", [128, 8], F32))
        A.cums = es.enter_context(nc.sbuf_tensor("a_cums", [128, 8], F32))
        A.tmp8 = es.enter_context(nc.sbuf_tensor("a_tmp8", [128, 8], F32))
        A.eiota = es.enter_context(nc.sbuf_tensor("a_eiota", [128, 8], F32))
        A.ltri = es.enter_context(nc.sbuf_tensor("a_ltri", [128, 128], BF16))
        A.allones = es.enter_context(nc.sbuf_tensor("a_allones", [128, 128], BF16))
        A.siota = es.enter_context(nc.sbuf_tensor("a_siota", [128, 512], F32))
        A.permT = es.enter_context(nc.sbuf_tensor("a_permT", [128, 2 * 512], BF16))
        A.ptw = es.enter_context(nc.sbuf_tensor("a_ptw", [128, 512], F32))
        A.permW = es.enter_context(nc.sbuf_tensor("a_permW", [128, 512], BF16))
        A.xnroll = es.enter_context(nc.sbuf_tensor("a_xnroll", [128, 2 * D], BF16))
        A.xgT = es.enter_context(nc.sbuf_tensor("a_xgT", [128, NDC * 512], BF16))
        A.y_sb = es.enter_context(nc.sbuf_tensor("a_ysb", [128, 4 * D], BF16))
        A.w1q = [es.enter_context(nc.sbuf_tensor(f"a_w1q{i}", [128, NDC * D], BF16))
                 for i in range(2)]
        A.w2sb = [es.enter_context(nc.sbuf_tensor(f"a_w2sb{i}", [128, 2 * D], BF16))
                  for i in range(2)]
        A.hT = es.enter_context(nc.sbuf_tensor("a_hT", [128, 32 * 512], BF16))
        emit_ffn0_sparse(s, P, A, prol, gate_done, gbank, gsafe)
    elif kind == "ffn0":
        A.b1 = es.enter_context(nc.sbuf_tensor("a_b1", [128, 256], F32))
        A.b2row = es.enter_context(nc.sbuf_tensor("a_b2row", [1, D], BF16))
        A.w1sb = es.enter_context(nc.sbuf_tensor("a_w1sb", [128, NDC * 4096], BF16))
        A.w2sb = [es.enter_context(nc.sbuf_tensor(f"a_w2sb{i}", [128, 2 * D], BF16))
                  for i in range(2)]
        A.hT = es.enter_context(nc.sbuf_tensor("a_hT", [128, 32 * 512], BF16))
        emit_ffn0(s, P, A, prol, gate_done, gbank, gsafe)
    else:
        A.bg = es.enter_context(nc.sbuf_tensor("a_bg", [128, 128], F32))
        A.bu = es.enter_context(nc.sbuf_tensor("a_bu", [128, 128], F32))
        A.borow = es.enter_context(nc.sbuf_tensor("a_borow", [1, D], BF16))
        A.wgsb = es.enter_context(nc.sbuf_tensor("a_wgsb", [128, NDC * 2048], BF16))
        A.wusb = es.enter_context(nc.sbuf_tensor("a_wusb", [128, NDC * 2048], BF16))
        A.wosb = [es.enter_context(nc.sbuf_tensor(f"a_wosb{i}", [128, 2 * D], BF16))
                  for i in range(2)]
        A.gscr = es.enter_context(nc.sbuf_tensor("a_gscr", [128, 1024], BF16))
        A.hT = es.enter_context(nc.sbuf_tensor("a_hT1", [128, 16 * 512], BF16))
        emit_ffn1(s, P, A, prol, gate_done, gbank, gsafe)


WEIGHT_SPECS = {
    "f0_gw": ([128, 64], ("ffn0a", "ffn0b")),
    "f0_b1": ([128, 256], ("ffn0a", "ffn0b")),
    "f0_b2": ([1, 8 * D], ("ffn0a", "ffn0b")),
    "f0_w1": ([8, 1024, 4096], ("ffn0a", "ffn0b")),
    "f0_w2": ([8, 4096, 1024], ("ffn0a", "ffn0b")),
    "f1_gw": ([128, 64], ("ffn1a", "ffn1b")),
    "f1_bg": ([128, 128], ("ffn1a", "ffn1b")),
    "f1_bu": ([128, 128], ("ffn1a", "ffn1b")),
    "f1_bo": ([1, 8 * D], ("ffn1a", "ffn1b")),
    "f1_wg": ([8, 1024, 2048], ("ffn1a", "ffn1b")),
    "f1_wu": ([8, 1024, 2048], ("ffn1a", "ffn1b")),
    "f1_wo": ([8, 2048, 1024], ("ffn1a", "ffn1b")),
    "a0_wq": ([1024, 1024], ("attn_a0",)),
    "a0_wk": ([1024, 1024], ("attn_a0",)),
    "a0_wv": ([1024, 1024], ("attn_a0",)),
    "a0_wo": ([1024, 1024], ("attn_a0",)),
    "a1_wq": ([1024, 1024], ("attn_a1",)),
    "a1_wk": ([1024, 1024], ("attn_a1",)),
    "a1_wv": ([1024, 1024], ("attn_a1",)),
    "a1_wo": ([1024, 1024], ("attn_a1",)),
    "a1_qb": ([128, 8], ("attn_a1",)),
    "a1_kb": ([128, 8], ("attn_a1",)),
    "a1_vb": ([128, 8], ("attn_a1",)),
    "a1_pb": ([1, D], ("attn_a1",)),
    "a1_cos": ([128, T], ("attn_a1",)),
    "a1_sin": ([128, T], ("attn_a1",)),
    "b1_wdq": ([1024, 512], ("attn_b1",)),
    "b1_wdkv": ([1024, 512], ("attn_b1",)),
    "b1_wuq": ([512, 1024], ("attn_b1",)),
    "b1_wqr": ([512, 1024], ("attn_b1",)),
    "b1_wuk": ([512, 1024], ("attn_b1",)),
    "b1_wkr": ([1024, 1024], ("attn_b1",)),
    "b1_wuv": ([512, 1024], ("attn_b1",)),
    "b1_wo": ([1024, 1024], ("attn_b1",)),
    "b1_dqb": ([128, 4], ("attn_b1",)),
    "b1_dkvb": ([128, 4], ("attn_b1",)),
    "b1_ub": ([128, 32], ("attn_b1",)),
    "b1_krb": ([128, 8], ("attn_b1",)),
    "b1_bo": ([1, D], ("attn_b1",)),
    "b1_cos": ([8, 128, T], ("attn_b1",)),
    "b1_sin": ([8, 128, T], ("attn_b1",)),
    "b0_wq": ([1024, 1024], ("attn_b0",)),
    "b0_wk": ([1024, 1024], ("attn_b0",)),
    "b0_wv": ([1024, 1024], ("attn_b0",)),
    "b0_wo": ([1024, 1024], ("attn_b0",)),
    "b0_bo": ([1, D], ("attn_b0",)),
    "b0_pt": ([128, 256], ("attn_b0",)),
}


def build_program(sub_ids):
    nc = bass.Bass("TRN2", target_bir_lowering=False)
    xin = nc.dram_tensor("x_in", [T, D], F32, kind="ExternalInput").ap()
    xout = nc.dram_tensor("x_out", [T, D], F32, kind="ExternalOutput").ap()
    needed = [n for n, (shp, subs) in WEIGHT_SPECS.items()
              if any(sid in subs for sid in sub_ids)]
    P = NS()
    P.W = {n: nc.dram_tensor(n, WEIGHT_SPECS[n][0], F32, kind="ExternalInput").ap()
           for n in needed}
    s = Sched()
    es_p = ExitStack()
    P.x_sb = es_p.enter_context(nc.sbuf_tensor("x_sb", [128, NTT * D], F32))
    P.xnT = es_p.enter_context(nc.sbuf_tensor("xnT", [128, NDC * T], BF16))
    P.ident = es_p.enter_context(nc.sbuf_tensor("ident", [128, 128], BF16))
    P.ones_row = es_p.enter_context(nc.sbuf_tensor("ones_row", [1, 128], BF16))
    P.eps_t = es_p.enter_context(nc.sbuf_tensor("eps_t", [128, 1], F32))
    P.identf = es_p.enter_context(nc.sbuf_tensor("identf", [128, 128], F32))
    P.stat = es_p.enter_context(nc.sbuf_tensor("stat", [128, NTT * 8], F32))
    emit_setup(s, P, xin)
    ncp = NCProxy(nc)
    conv = [n for n in CONVERT_BF16 if n in P.W]
    if conv:
        emit_bf16_convert(s, ncp, P, conv)
        s.fence()
    for sid in sub_ids:
        with ExitStack() as es:
            if sid in ("ffn0a", "ffn0b"):
                emit_ffn_sub(s, ncp, es, P, "ffn0")
            elif sid in ("ffn1a", "ffn1b"):
                emit_ffn_sub(s, ncp, es, P, "ffn1")
            elif sid == "attn_a0":
                emit_attn_a0(s, ncp, es, P)
            elif sid == "attn_a1":
                emit_attn_a1(s, ncp, es, P)
            elif sid == "attn_b1":
                emit_attn_b1(s, ncp, es, P)
            elif sid == "attn_b0":
                emit_attn_b0(s, ncp, es, P)
            else:
                raise KeyError(sid)
            s.fence()
    s.op("sync", lambda e: e.dma_start(
        out=xout.rearrange("(tt p) d -> p tt d", p=128),
        in_=P.x_sb.ap().rearrange("p (tt d) -> p tt d", tt=NTT)), sig="d_out", n=16)
    if _DEBUG_DUMP:
        dumps = {"d_xnT": (P.xnT.ap(), BF16)}
        dumps.update(getattr(P, "dbg", {}))
        A = getattr(P, "dbgA", None)
        if A is not None and hasattr(A, "gw") and _DEBUG_STAGE != "prologue":
            dumps["d_gw"] = (A.gw.ap(), F32)
        if A is not None and hasattr(A, "hT") and _DEBUG_STAGE == "full":
            dumps["d_hT"] = (A.hT.ap(), BF16)
            dumps["d_w1"] = (A.w1sb.ap(), BF16)
        for nm, (ap, dt) in dumps.items():
            od = nc.dram_tensor(nm, list(ap.shape), dt, kind="ExternalOutput").ap()
            s.op("sync", lambda e, od=od, ap=ap: e.dma_start(out=od, in_=ap),
                 sig="d_out", n=16)
    final = s.tok("d_out")
    s.op("gpsimd", lambda e: e.engine_nop(), wait=[final])
    replay(nc, s, es_p)
    P._es = es_p
    return nc, needed


def _prep_weights(inp):
    W = {}
    W["f0_gw"] = np.ascontiguousarray(
        inp["L0_ffn_gate_w"].reshape(8, 128, 8).transpose(1, 0, 2).reshape(128, 64))
    W["f0_b1"] = np.ascontiguousarray(
        inp["L0_ffn_b1"].reshape(8, 32, 128).transpose(2, 0, 1).reshape(128, 256))
    W["f0_b2"] = np.ascontiguousarray(inp["L0_ffn_b2"].reshape(1, 8 * D))
    W["f0_w1"] = np.ascontiguousarray(inp["L0_ffn_w1"])
    W["f0_w2"] = np.ascontiguousarray(inp["L0_ffn_w2"])
    W["f1_gw"] = np.ascontiguousarray(
        inp["L1_ffn_W_router"].reshape(8, 128, 8).transpose(1, 0, 2).reshape(128, 64))
    W["f1_bg"] = np.ascontiguousarray(
        inp["L1_ffn_b_gate"].reshape(8, 16, 128).transpose(2, 0, 1).reshape(128, 128))
    W["f1_bu"] = np.ascontiguousarray(
        inp["L1_ffn_b_up"].reshape(8, 16, 128).transpose(2, 0, 1).reshape(128, 128))
    W["f1_bo"] = np.ascontiguousarray(inp["L1_ffn_b_out"].reshape(1, 8 * D))
    W["f1_wg"] = np.ascontiguousarray(inp["L1_ffn_W_gate"])
    W["f1_wu"] = np.ascontiguousarray(inp["L1_ffn_W_up"])
    W["f1_wo"] = np.ascontiguousarray(inp["L1_ffn_W_out"])
    W["a0_wq"] = np.ascontiguousarray(np.asarray(inp["L0_attn_a_Wq"]).T)
    W["a0_wk"] = np.ascontiguousarray(np.asarray(inp["L0_attn_a_Wk"]).T)
    W["a0_wv"] = np.ascontiguousarray(np.asarray(inp["L0_attn_a_Wv"]).T)
    W["a0_wo"] = np.ascontiguousarray(np.asarray(inp["L0_attn_a_Wo"]).T)
    qkv_w = np.asarray(inp["L1_attn_a_qkv_w"])
    qkv_b = np.asarray(inp["L1_attn_a_qkv_b"])
    W["a1_wq"] = np.ascontiguousarray(qkv_w[0:1024].T)
    W["a1_wk"] = np.ascontiguousarray(qkv_w[1024:2048].T)
    W["a1_wv"] = np.ascontiguousarray(qkv_w[2048:3072].T)
    W["a1_wo"] = np.ascontiguousarray(np.asarray(inp["L1_attn_a_proj_w"]).T)
    W["a1_qb"] = np.ascontiguousarray(qkv_b[0:1024].reshape(8, 128).T)
    W["a1_kb"] = np.ascontiguousarray(qkv_b[1024:2048].reshape(8, 128).T)
    W["a1_vb"] = np.ascontiguousarray(qkv_b[2048:3072].reshape(8, 128).T)
    W["a1_pb"] = np.ascontiguousarray(np.asarray(inp["L1_attn_a_proj_b"]).reshape(1, D))
    invf = (np.float32(1.0) /
            np.power(np.float32(10000.0),
                     np.arange(0, 64, 2, dtype=np.float32) / np.float32(64)))
    tpos = np.arange(T, dtype=np.float32)
    fr = tpos[:, None] * invf[None, :]
    cos32 = np.cos(fr).astype(np.float32)
    sin32 = np.sin(fr).astype(np.float32)
    ctab = np.zeros((128, T), np.float32)
    stab = np.zeros((128, T), np.float32)
    for r in range(128):
        j = r % 64
        ctab[r] = cos32[:, j % 32]
        stab[r] = (-sin32[:, j]) if j < 32 else sin32[:, j - 32]
    W["a1_cos"] = ctab
    W["a1_sin"] = stab
    g = lambda n: np.asarray(inp["L1_attn_b_" + n])
    perm = np.zeros(1024, np.int64)
    for h in range(16):
        for i in range(32):
            perm[64 * h + i] = 64 * h + 2 * i
            perm[64 * h + 32 + i] = 64 * h + 2 * i + 1
    W["b1_wdq"] = np.ascontiguousarray(g("Wd_q_w"))
    W["b1_wdkv"] = np.ascontiguousarray(g("Wd_kv_w"))
    W["b1_wuq"] = np.ascontiguousarray(g("Wu_q_w")[:, perm])
    W["b1_wqr"] = np.ascontiguousarray(g("Wq_r_w")[:, perm])
    W["b1_wuk"] = np.ascontiguousarray(g("Wu_k_w")[:, perm])
    W["b1_wkr"] = np.ascontiguousarray(g("Wk_r_w")[:, perm])
    W["b1_wuv"] = np.ascontiguousarray(g("Wu_v_w"))
    W["b1_wo"] = np.ascontiguousarray(g("Wo_w"))
    W["b1_dqb"] = np.ascontiguousarray(g("Wd_q_b").reshape(4, 128).T)
    W["b1_dkvb"] = np.ascontiguousarray(g("Wd_kv_b").reshape(4, 128).T)
    ub = np.zeros((128, 32), np.float32)
    ub[:, 0:8] = g("Wu_q_b")[perm].reshape(8, 128).T
    ub[:, 8:16] = g("Wq_r_b")[perm].reshape(8, 128).T
    ub[:, 16:24] = g("Wu_k_b")[perm].reshape(8, 128).T
    ub[:, 24:32] = g("Wu_v_b").reshape(8, 128).T
    W["b1_ub"] = ub
    W["b1_krb"] = np.ascontiguousarray(g("Wk_r_b")[perm].reshape(8, 128).T)
    W["b1_bo"] = np.ascontiguousarray(g("Wo_b").reshape(1, D))
    ids = np.arange(512, dtype=np.float32)
    theta = np.power(np.float32(10000.0), np.float32(-2.0) * ids) / np.float32(512.0)
    tpos = np.arange(T, dtype=np.float32)
    rarg = tpos[:, None] * theta[None, :]
    cosf = np.cos(rarg).astype(np.float32)
    sinf = np.sin(rarg).astype(np.float32)
    bc = np.zeros((8, 128, T), np.float32)
    bs = np.zeros((8, 128, T), np.float32)
    for oc in range(8):
        for r in range(128):
            h = 2 * oc + (1 if r >= 64 else 0)
            j = r % 64
            if j < 32:
                i = 32 * h + j
                bc[oc, r] = cosf[:, i]
                bs[oc, r] = -sinf[:, i]
            else:
                i = 32 * h + (j - 32)
                bc[oc, r] = cosf[:, i]
                bs[oc, r] = sinf[:, i]
    W["b1_cos"] = bc
    W["b1_sin"] = bs
    W["b0_wq"] = np.ascontiguousarray(inp["L0_attn_b_Wq"])
    W["b0_wk"] = np.ascontiguousarray(inp["L0_attn_b_Wk"])
    W["b0_wv"] = np.ascontiguousarray(inp["L0_attn_b_Wv"])
    W["b0_wo"] = np.ascontiguousarray(inp["L0_attn_b_Wo"])
    W["b0_bo"] = np.ascontiguousarray(np.asarray(inp["L0_attn_b_bo"]).reshape(1, D))
    pt = np.asarray(inp["L0_attn_b_P"]).T
    W["b0_pt"] = np.ascontiguousarray(np.vstack([pt, pt]))
    return {k: np.asarray(v, dtype=np.float32) for k, v in W.items()}


LAUNCH_PLAN = [["attn_a0", "ffn0a", "attn_b0", "ffn0b", "attn_a1", "ffn1a", "attn_b1", "ffn1b"]]
_DEBUG_STAGE = "full"
_DEBUG_DUMP = False
_DEBUG_NEXPERTS = 8
_DEBUG_NTB = 4
_DEBUG_NOGELU = False
SPARSE_FFN0 = True


def run_subs(plan, x_state, W):
    for group in plan:
        nc, needed = build_program(group)
        in_maps = []
        for c in range(8):
            m = {"x_in": np.ascontiguousarray(x_state[c])}
            for n in needed:
                m[n] = W[n]
            in_maps.append(m)
        res = run_bass_kernel_spmd(nc, in_maps, core_ids=list(range(8)))
        x_state = [np.asarray(res.results[c]["x_out"]) for c in range(8)]
    return x_state


def kernel(**inputs):
    W = _prep_weights(inputs)
    x = np.asarray(inputs["L0_attn_a_x"], dtype=np.float32)
    state = [np.ascontiguousarray(x[c]) for c in range(8)]
    state = run_subs(LAUNCH_PLAN, state, W)
    return np.stack(state, axis=0).astype(np.float32)
